# version 14
# baseline (speedup 1.0000x reference)
import math
import contextlib
import numpy as np
import concourse.bass as bass
import concourse.mybir as mybir
from concourse.bass_utils import run_bass_kernel_spmd

F32 = mybir.dt.float32
BF16 = mybir.dt.bfloat16
I32 = mybir.dt.int32
AF = mybir.ActivationFunctionType
ALU = mybir.AluOpType

D = 1024
NCHUNK = 8
TT = 512
NTILE = 9
TW = TT * NTILE
EPS = 1e-5
C0 = (1.0 + EPS) ** -0.5
SEG = 16
CH = 64
PI2_HI = float(np.float32(math.pi / 2))
PI2_LO = math.pi / 2 - PI2_HI



def stage_w(W):
    W = np.ascontiguousarray(W)
    oc = W.shape[0] // 128
    ic = W.shape[1] // 128
    t = W.reshape(oc, 128, ic, 128).transpose(3, 2, 0, 1)
    return np.ascontiguousarray(t)


def stage_w64_out(W):
    W = np.ascontiguousarray(W)
    ic = W.shape[1] // 128
    return np.ascontiguousarray(W.reshape(64, ic, 128).transpose(2, 1, 0))


def stage_w64_in(W):
    W = np.ascontiguousarray(W)
    oc = W.shape[0] // 128
    return np.ascontiguousarray(W.reshape(oc, 128, 64).transpose(2, 0, 1))


def stage_cols(v):
    return np.ascontiguousarray(np.asarray(v).reshape(-1, 128).T)


def stage_stream(x_slab):
    T = x_slab.shape[0]
    t = x_slab.T.reshape(NCHUNK, 128, T)
    return np.ascontiguousarray(t.transpose(1, 0, 2))


def unstage_stream(arr):
    return np.ascontiguousarray(arr.transpose(1, 0, 2).reshape(D, -1).T)



ENGS = ("tensor", "vector", "scalar", "gpsimd", "sync", "gdma", "sdma")
COMPUTE_LANES = ("tensor", "vector", "scalar", "gpsimd", "gdma")


class B:
    def __init__(self):
        self.nc = bass.Bass("TRN2", target_bir_lowering=False, debug=False)
        self.stack = contextlib.ExitStack()
        self.sects = []
        self._blk = None
        self.sems = {}
        self.val = {e: 0 for e in ENGS}
        self.scopes = []
        self.uid = 0

    def dram_in(self, name, shape, dtype=F32):
        return self.nc.dram_tensor(name, shape, dtype, kind="ExternalInput").ap()

    def dram_out(self, name, shape, dtype=F32):
        return self.nc.dram_tensor(name, shape, dtype, kind="ExternalOutput").ap()

    def dram_tmp(self, name, shape, dtype=F32):
        return self.nc.dram_tensor(name, shape, dtype, kind="Internal").ap()

    def _st(self):
        return self.sects[-1] if self.sects else self.stack

    def sbuf(self, name, shape, dtype=F32):
        self.uid += 1
        return self._st().enter_context(
            self.nc.sbuf_tensor(f"{name}_{self.uid}", shape, dtype))

    def psum_pair(self, name, shape, dtype=F32):
        return [self.psum(name + "a", shape, dtype), self.psum(name + "b", shape, dtype)]

    def psum(self, name, shape, dtype=F32):
        self.uid += 1
        shape = list(shape)
        if dtype == F32 and shape[-1] < 512:
            shape[-1] = 512
        return self._st().enter_context(
            self.nc.psum_tensor(f"{name}_{self.uid}", shape, dtype))

    def scope_push(self, base=None):
        self.scopes.append({"base": dict(self.val) if base is None else dict(base), "seen": {}})

    def scope_pop(self):
        self.scopes.pop()

    def push(self):
        self.sects.append(contextlib.ExitStack())

    def pop(self):
        self.sects.pop().close()

    def begin(self):
        for e in ENGS:
            self.sems[e] = self.stack.enter_context(self.nc.semaphore(f"sem_{e}"))
        self._blk = self.stack.enter_context(self.nc.Block())
        self.bodies = {e: [] for e in ("tensor", "vector", "scalar", "gpsimd", "sync")}

    def now(self):
        return dict(self.val)

    def op(self, eng, fn, dma=False, after=None, extra=None):
        key = eng
        if dma:
            key = {"gpsimd": "gdma", "scalar": "sdma"}.get(eng, eng)
        if after is None:
            if self.scopes:
                sc = self.scopes[-1]
                after = dict(sc["base"])
                for l2, v2 in sc["seen"].items():
                    after[l2] = max(after.get(l2, 0), v2)
            else:
                after = {e: self.val[e] for e in COMPUTE_LANES}
        if extra:
            after = dict(after)
            for k2, v2 in extra.items():
                after[k2] = max(after.get(k2, 0), v2)
        if dma:
            after = dict(after)
            after[key] = max(after.get(key, 0), self.val[key])
        waits = [(self.sems[e], v) for e, v in after.items() if v > 0]
        inc = 16 if dma else 1
        sem = self.sems[key]

        def body(e):
            for smh, v in waits:
                e.wait_ge(smh, v)
            ins = fn(e)
            ins.then_inc(sem, inc)
        self.bodies[eng].append(body)
        self.val[key] += inc
        if self.scopes:
            self.scopes[-1]["seen"][key] = self.val[key]

    def mm(self, ps_ap, pairs, after=None):
        def f(e):
            last = None
            n = len(pairs)
            for i, (l, r) in enumerate(pairs):
                last = e.matmul(ps_ap, l, r, start=(i == 0), stop=(i == n - 1))
            return last
        self.op("tensor", f, after=after)

    def pipe(self, n, stages, guard=2, extra=None, start=None):
        if start is None:
            start = {e: self.val[e] for e in COMPUTE_LANES}
        else:
            start = dict(start)
        if extra:
            for k2, v2 in extra.items():
                start[k2] = max(start.get(k2, 0), v2)
        last_eng = stages[-1][0]
        last_marks = []
        s0_mark = None
        touched = {}
        for i in range(n):
            prev = None
            for k, (eng, fn) in enumerate(stages):
                after = {}
                if i == 0:
                    after.update(start)
                if k == 0:
                    if i >= guard:
                        after[last_eng] = max(after.get(last_eng, 0), last_marks[i - guard])
                    elif i == 1:
                        after[stages[0][0]] = max(after.get(stages[0][0], 0), s0_mark)
                else:
                    pe, pv = prev
                    if pe == eng:
                        after[eng] = max(after.get(eng, 0), pv)
                    else:
                        after[pe] = max(after.get(pe, 0), pv)
                self.op(eng, lambda e, fn=fn, i=i: fn(e, i), after=after)
                prev = (eng, self.val[eng])
                touched[eng] = self.val[eng]
                if k == 0:
                    s0_mark = self.val[eng]
            last_marks.append(self.val[last_eng])
        return touched

    def mmg(self, e, ps_ap, pairs):
        last = None
        n = len(pairs)
        for i, (l, r) in enumerate(pairs):
            last = e.matmul(ps_ap, l, r, start=(i == 0), stop=(i == n - 1))
        return last

    def end(self):
        vals = self.now()
        for eng, lst in self.bodies.items():
            if not lst:
                continue

            def mega(e, lst=lst):
                for f in lst:
                    f(e)
            getattr(self._blk, eng)(mega)

        def fin(e):
            for en in ENGS:
                if vals[en]:
                    e.wait_ge(self.sems[en], vals[en])
        self._blk.sync(fin)
        while self.sects:
            self.pop()
        self.stack.close()
        return self.nc


def mmax(*ds):
    out = {}
    for d_ in ds:
        if not d_:
            continue
        for k, v in d_.items():
            out[k] = max(out.get(k, 0), v)
    return out



def emit_consts(b):
    cst = {}
    t = b.sbuf("c_eps", [128, 1], F32)
    b.op("gpsimd", lambda e: e.memset(t[:, :], EPS))
    cst["eps"] = t
    t2 = b.sbuf("c_eps2", [128, 1], F32)
    b.op("gpsimd", lambda e: e.memset(t2[:, :], EPS * EPS))
    cst["eps2"] = t2
    ones = b.sbuf("c_ones", [128, 128], BF16)
    b.op("gpsimd", lambda e: e.memset(ones[:, :], 1.0))
    cst["ones"] = ones
    return cst


def load_w(b, name, dram_ap, ic, oc, dtype=BF16):
    w = b.sbuf(name, [128, ic, oc, 128], dtype)
    b.op("gpsimd", lambda e: e.dma_start(out=w[:, :, :, :], in_=dram_ap[:, :, :, :]),
         dma=True, extra={"sdma": b.val["sdma"], "sync": b.val["sync"]})
    return w


def load_cols(b, name, dram_ap, n, dtype=F32):
    t = b.sbuf(name, [128, n], dtype)
    b.op("gpsimd", lambda e: e.dma_start(out=t[:, :], in_=dram_ap[:, :]), dma=True,
         extra={"sdma": b.val["sdma"], "sync": b.val["sync"]})
    return t


class LNScratch:
    def __init__(self, b, n, cst):
        self.b = b
        self.cst = cst
        self.nn = n
        self.xb = b.sbuf("ln_xb", [128, NCHUNK, n], BF16)
        self.m = b.sbuf("ln_m", [128, n], F32)
        self.A = b.sbuf("ln_A", [128, n], F32)
        self.t1 = b.sbuf("ln_t1", [128, n], F32)
        self.t2 = None
        self.A2 = None
        self.psS = b.psum("ln_psS", [128, n], F32)
        self.psQ = b.psum("ln_psQ", [128, n], F32)

    def mkcomp(self, n):
        if self.A2 is None:
            self.t2 = self.b.sbuf("ln_t2", [128, self.nn], F32)
            self.A2 = self.b.sbuf("ln_A2", [128, self.nn], F32)

    def stats(self, x_chunks, n, nch=NCHUNK, comp=False, first=None):
        b, cst = self.b, self.cst
        if comp:
            self.mkcomp(n)

        def copies(e):
            last = None
            for c in range(nch):
                last = e.activation(self.xb[:, c, :n], x_chunks(c), AF.Copy, bias=0.0, scale=1.0)
            return last
        b.op("scalar", copies, extra=first)
        b.mm(self.psS[:, :n], [(cst["ones"][:, :], self.xb[:, c, :n]) for c in range(nch)])
        psS_mark = b.val["tensor"]

        def squares(e):
            last = None
            for c in range(nch):
                last = e.activation(self.xb[:, c, :n], self.xb[:, c, :n], AF.Square, bias=0.0, scale=1.0)
            return last
        b.op("scalar", squares, after={"scalar": b.val["scalar"], "tensor": psS_mark})
        b.mm(self.psQ[:, :n], [(cst["ones"][:, :], self.xb[:, c, :n]) for c in range(nch)])
        inv = 1.0 / (128 * nch)
        b.op("vector", lambda e: e.tensor_scalar(self.m[:, :n], self.psS[:, :n], inv, None, ALU.mult))
        b.op("vector", lambda e: e.tensor_mul(self.t1[:, :n], self.m[:, :n], self.m[:, :n]))
        b.op("vector", lambda e: e.scalar_tensor_tensor(
            self.t1[:, :n], self.psQ[:, :n], inv, self.t1[:, :n], ALU.mult, ALU.subtract))
        if comp:
            b.op("scalar", lambda e: e.activation(
                self.t2[:, :n], self.t1[:, :n], AF.Sqrt, bias=cst["eps2"][:, 0:1],
                scale=1.0 + EPS))
            b.op("vector", lambda e: e.reciprocal(self.A2[:, :n], self.t2[:, :n]))
        b.op("scalar", lambda e: e.activation(
            self.t1[:, :n], self.t1[:, :n], AF.Sqrt, bias=cst["eps"][:, 0:1], scale=1.0))
        b.op("vector", lambda e: e.reciprocal(self.A[:, :n], self.t1[:, :n]))



def emit_eig_tables(b, cst, ld_col, fr_col):
    eg = {}
    mag = b.sbuf("eg_mag", [64, 1], F32)
    b.op("scalar", lambda e: e.activation(mag[:, :], ld_col[:, :], AF.Sigmoid, bias=0.0, scale=1.0))
    s2 = b.sbuf("eg_s2", [64, 1], F32)
    b.op("scalar", lambda e: e.activation(s2[:, :], fr_col[:, :], AF.Sin, bias=0.0, scale=0.5))
    u = b.sbuf("eg_u", [64, 1], F32)
    b.op("vector", lambda e: e.tensor_scalar(u[:, :], fr_col[:, :], -0.5, PI2_HI, ALU.mult, ALU.add))
    b.op("vector", lambda e: e.tensor_scalar(u[:, :], u[:, :], PI2_LO, None, ALU.add))
    c2 = b.sbuf("eg_c2", [64, 1], F32)
    b.op("scalar", lambda e: e.activation(c2[:, :], u[:, :], AF.Sin, bias=0.0, scale=1.0))
    sth = b.sbuf("eg_sth", [64, 1], F32)
    b.op("vector", lambda e: e.tensor_mul(sth[:, :], s2[:, :], c2[:, :]))
    b.op("vector", lambda e: e.tensor_scalar(sth[:, :], sth[:, :], 2.0, None, ALU.mult))
    cth = b.sbuf("eg_cth", [64, 1], F32)
    b.op("vector", lambda e: e.tensor_mul(cth[:, :], s2[:, :], s2[:, :]))
    b.op("vector", lambda e: e.tensor_scalar(cth[:, :], cth[:, :], -2.0, 1.0, ALU.mult, ALU.add))
    a = b.sbuf("eg_a", [64, 1], F32)
    b.op("vector", lambda e: e.tensor_mul(a[:, :], cth[:, :], mag[:, :]))
    bb = b.sbuf("eg_b", [64, 1], F32)
    b.op("vector", lambda e: e.tensor_mul(bb[:, :], sth[:, :], mag[:, :]))
    rmag = b.sbuf("eg_rmag", [64, 1], F32)
    b.op("vector", lambda e: e.reciprocal(rmag[:, :], mag[:, :]))
    ia = b.sbuf("eg_ia", [64, 1], F32)
    b.op("vector", lambda e: e.tensor_mul(ia[:, :], cth[:, :], rmag[:, :]))
    ib = b.sbuf("eg_ib", [64, 1], F32)
    b.op("vector", lambda e: e.tensor_scalar(ib[:, :], sth[:, :], -1.0, None, ALU.mult))
    b.op("vector", lambda e: e.tensor_mul(ib[:, :], ib[:, :], rmag[:, :]))

    W17r = b.sbuf("eg_W17r", [64, 17], F32)
    W17i = b.sbuf("eg_W17i", [64, 17], F32)
    U16r = b.sbuf("eg_U16r", [64, 16], F32)
    U16i = b.sbuf("eg_U16i", [64, 16], F32)
    tmp = b.sbuf("eg_tmp", [64, 1], F32)
    for (Wr, Wi, n, ar, br) in ((W17r, W17i, 17, a, bb), (U16r, U16i, 16, ia, ib)):
        b.op("gpsimd", lambda e, Wr=Wr: e.memset(Wr[:, 0:1], 1.0))
        b.op("gpsimd", lambda e, Wi=Wi: e.memset(Wi[:, 0:1], 0.0))
        for j in range(1, n):
            b.op("vector", lambda e, j=j, Wr=Wr, ar=ar: e.tensor_scalar(
                Wr[:, j:j + 1], Wr[:, j - 1:j], ar[:, 0:1], None, ALU.mult))
            b.op("vector", lambda e, j=j, Wi=Wi, br=br: e.tensor_scalar(
                tmp[:, :], Wi[:, j - 1:j], br[:, 0:1], None, ALU.mult))
            b.op("vector", lambda e, j=j, Wr=Wr: e.tensor_sub(
                Wr[:, j:j + 1], Wr[:, j:j + 1], tmp[:, :]))
            b.op("vector", lambda e, j=j, Wi=Wi, ar=ar: e.tensor_scalar(
                Wi[:, j:j + 1], Wi[:, j - 1:j], ar[:, 0:1], None, ALU.mult))
            b.op("vector", lambda e, j=j, Wr=Wr, Wi=Wi, br=br: e.tensor_scalar(
                tmp[:, :], Wr[:, j - 1:j], br[:, 0:1], None, ALU.mult))
            b.op("vector", lambda e, j=j, Wi=Wi: e.tensor_add(
                Wi[:, j:j + 1], Wi[:, j:j + 1], tmp[:, :]))
    mask = b.sbuf("eg_mask", [64, 2 * TT], F32)
    b.op("gpsimd", lambda e: e.memset(mask[:, :], 1.0))
    b.op("gpsimd", lambda e: e.memset(mask[:, 0:2 * TT:SEG], 0.0))
    eg.update(W17r=W17r, W17i=W17i, U16r=U16r, U16i=U16i, mask=mask)
    return eg


def emit_eigenstate(b, eg, bt, cout, n, carry_ap, scratch):
    W17r, W17i = eg["W17r"], eg["W17i"]
    U16r, U16i = eg["U16r"], eg["U16i"]
    y, p, z, G = scratch["y"], scratch["p"], scratch["z"], scratch["G"]
    ta, tb = scratch["ta"], scratch["tb"]
    nseg = n // SEG
    nch = n // CH

    def b16(t, rev=False):
        ap = t[:, 16:0:-1] if rev else t[:, 0:16]
        return ap.unsqueeze(1).broadcast_to([64, 2 * nseg, 16])

    def b16h(t, rev=False):
        ap = t[:, 16:0:-1] if rev else t[:, 0:16]
        return ap.unsqueeze(1).broadcast_to([64, nseg, 16])

    def r3(ap):
        return ap.rearrange("p (g f) -> p g f", f=16)

    b.op("vector", lambda e: e.tensor_mul(r3(y[:, 0:2 * n]), r3(bt[:, 0:2 * n]), b16(W17r)))
    b.op("vector", lambda e: e.tensor_mul(r3(ta[:, 0:n]), r3(bt[:, n:2 * n]), b16h(W17i)))
    b.op("vector", lambda e: e.tensor_sub(y[:, 0:n], y[:, 0:n], ta[:, 0:n]))
    b.op("vector", lambda e: e.tensor_mul(r3(ta[:, 0:n]), r3(bt[:, 0:n]), b16h(W17i)))
    b.op("vector", lambda e: e.tensor_add(y[:, n:2 * n], y[:, n:2 * n], ta[:, 0:n]))
    b.op("vector", lambda e: e.tensor_tensor_scan(
        p[:, 0:2 * n], eg["mask"][:, 0:2 * n], y[:, 2 * n - 1::-1], 0.0, ALU.mult, ALU.add))
    sufr = p[:, 2 * n - 1:n - 1:-1]
    sufi = p[:, n - 1::-1]
    b.op("vector", lambda e: e.tensor_mul(r3(z[:, 0:2 * n]), r3(p[:, 2 * n - 1::-1]), b16(U16r)))
    b.op("vector", lambda e: e.tensor_mul(r3(ta[:, 0:n]), r3(sufi), b16h(U16i)))
    b.op("vector", lambda e: e.tensor_sub(z[:, 0:n], z[:, 0:n], ta[:, 0:n]))
    b.op("vector", lambda e: e.tensor_mul(r3(ta[:, 0:n]), r3(sufr), b16h(U16i)))
    b.op("vector", lambda e: e.tensor_add(z[:, n:2 * n], z[:, n:2 * n], ta[:, 0:n]))
    a16r = W17r[:, 16:17]
    a16i = W17i[:, 16:17]

    def Gs(m, imag):
        base = nseg if imag else 0
        return G[:, base + m::4][:, 0:nch]

    def Ts(m, imag):
        base = n if imag else 0
        return z[:, base + m * 16::CH][:, 0:nch]
    b.op("gpsimd", lambda e: e.memset(Gs(3, False), 0.0))
    b.op("gpsimd", lambda e: e.memset(Gs(3, True), 0.0))
    b.op("vector", lambda e: e.tensor_copy(Gs(2, False), Ts(3, False)))
    b.op("vector", lambda e: e.tensor_copy(Gs(2, True), Ts(3, True)))
    for m in (1, 0):
        b.op("vector", lambda e, m=m: e.scalar_tensor_tensor(
            Gs(m, False), Gs(m + 1, False), a16r[:, 0:1], Ts(m + 1, False), ALU.mult, ALU.add))
        b.op("vector", lambda e, m=m: e.tensor_scalar(
            ta[:, 0:nch], Gs(m + 1, True), a16i[:, 0:1], None, ALU.mult))
        b.op("vector", lambda e, m=m: e.tensor_sub(Gs(m, False), Gs(m, False), ta[:, 0:nch]))
        b.op("vector", lambda e, m=m: e.scalar_tensor_tensor(
            Gs(m, True), Gs(m + 1, True), a16r[:, 0:1], Ts(m + 1, True), ALU.mult, ALU.add))
        b.op("vector", lambda e, m=m: e.tensor_scalar(
            ta[:, 0:nch], Gs(m + 1, False), a16i[:, 0:1], None, ALU.mult))
        b.op("vector", lambda e, m=m: e.tensor_add(Gs(m, True), Gs(m, True), ta[:, 0:nch]))
    GrB = G[:, 0:nseg].unsqueeze(2).broadcast_to([64, nseg, 16])
    GiB = G[:, nseg:2 * nseg].unsqueeze(2).broadcast_to([64, nseg, 16])
    b.op("vector", lambda e: e.tensor_mul(r3(ta[:, 0:n]), b16h(W17r, True), GrB))
    b.op("vector", lambda e: e.tensor_mul(r3(tb[:, 0:n]), b16h(W17i, True), GiB))
    b.op("vector", lambda e: e.tensor_sub(ta[:, 0:n], ta[:, 0:n], tb[:, 0:n]))
    b.op("vector", lambda e: e.tensor_add(cout[:, 0:n], z[:, 0:n], ta[:, 0:n]))
    b.op("vector", lambda e: e.tensor_mul(r3(ta[:, 0:n]), b16h(W17r, True), GiB))
    b.op("vector", lambda e: e.tensor_mul(r3(tb[:, 0:n]), b16h(W17i, True), GrB))
    b.op("vector", lambda e: e.tensor_add(ta[:, 0:n], ta[:, 0:n], tb[:, 0:n]))
    b.op("vector", lambda e: e.tensor_add(cout[:, n:2 * n], z[:, n:2 * n], ta[:, 0:n]))
    if carry_ap is not None:
        b.op("vector", lambda e: e.tensor_add(cout[:, 0:1], cout[:, 0:1], carry_ap[:, 0:1]))
        b.op("vector", lambda e: e.tensor_add(cout[:, n:n + 1], cout[:, n:n + 1], carry_ap[:, 1:2]))
    if nch > 1:
        b.op("vector", lambda e: e.tensor_add(
            cout[:, CH::CH][:, 0:nch - 1], cout[:, CH::CH][:, 0:nch - 1],
            bt[:, CH - 1::CH][:, 0:nch - 1]))
        b.op("vector", lambda e: e.tensor_add(
            cout[:, n + CH::CH][:, 0:nch - 1], cout[:, n + CH::CH][:, 0:nch - 1],
            bt[:, n + CH - 1::CH][:, 0:nch - 1]))


def mk_eig_scratch(b, n):
    return {
        "y": b.sbuf("eg_y", [64, 2 * n], F32),
        "p": b.sbuf("eg_p", [64, 2 * n], F32),
        "z": b.sbuf("eg_z", [64, 2 * n], F32),
        "G": b.sbuf("eg_G", [64, 2 * (n // SEG)], F32),
        "ta": b.sbuf("eg_ta", [64, n], F32),
        "tb": b.sbuf("eg_tb", [64, n], F32),
    }



def emit_l1a(b, cst, d):
    b.push()
    w_gate = load_w(b, "w_gate_s", d["w_gate"], 8, 8)
    w_in_r = b.sbuf("w_in_r_s", [128, 8, 64], BF16)
    b.op("gpsimd", lambda e: e.dma_start(out=w_in_r[:, :, :], in_=d["w_in_r"][:, :, :]), dma=True,
         extra={"sdma": b.val["sdma"], "sync": b.val["sync"]})
    w_in_i = b.sbuf("w_in_i_s", [128, 8, 64], BF16)
    b.op("gpsimd", lambda e: e.dma_start(out=w_in_i[:, :, :], in_=d["w_in_i"][:, :, :]), dma=True)
    wor = b.sbuf("wor_s", [64, 8, 128], BF16)
    b.op("gpsimd", lambda e: e.dma_start(out=wor[:, :, :], in_=d["w_out_r"][:, :, :]), dma=True)
    woi = b.sbuf("woi_s", [64, 8, 128], BF16)
    b.op("gpsimd", lambda e: e.dma_start(out=woi[:, :, :], in_=d["w_out_i"][:, :, :]), dma=True)
    b_gate = load_cols(b, "b_gate_s", d["b_gate"], 8)
    ln1g = load_cols(b, "ln1g_s", d["ln1g"], 8)
    ln1b = load_cols(b, "ln1b_s", d["ln1b"], 8)
    ld = b.sbuf("ld_s", [64, 1], F32)
    b.op("gpsimd", lambda e: e.dma_start(out=ld[:, :], in_=d["ldecay"][:, :]), dma=True)
    fr = b.sbuf("fr_s", [64, 1], F32)
    b.op("gpsimd", lambda e: e.dma_start(out=fr[:, :], in_=d["freq"][:, :]), dma=True)
    flag = load_cols(b, "flag_s", d["flag"], 1)
    wcp = b.sbuf("wcp", [64, 64], BF16)
    b.op("gpsimd", lambda e: e.memset(wcp[:, :], 0.0),
         extra={"sdma": b.val["sdma"], "sync": b.val["sync"]})
    for h in range(4):
        o = 16 * h
        b.op("gpsimd", lambda e, o=o, h=h: e.dma_start(
            out=wcp[o:o + 16, o:o + 16], in_=d["coupT"][h, :, :]), dma=True)
    eg = emit_eig_tables(b, cst, ld, fr)
    ln = LNScratch(b, TT, cst)
    egs = mk_eig_scratch(b, TT)

    xt = [b.sbuf("xt0", [128, NCHUNK, TT], F32), b.sbuf("xt1", [128, NCHUNK, TT], F32),
          b.sbuf("xt2", [128, NCHUNK, TT], F32)]
    y1 = [b.sbuf("y10", [128, NCHUNK, TT], F32), b.sbuf("y11", [128, NCHUNK, TT], F32),
          b.sbuf("y12", [128, NCHUNK, TT], F32)]
    xnb = [b.sbuf("xnb0", [128, NCHUNK, TT], BF16), b.sbuf("xnb1", [128, NCHUNK, TT], BF16)]
    bt = b.sbuf("bt", [64, 2 * TT], F32)
    ct = b.sbuf("ct", [64, 2 * TT], F32)
    cb = b.sbuf("cb", [64, 2 * TT], BF16)
    cpb = [b.sbuf("cpb0", [64, 2 * TT], BF16), b.sbuf("cpb1", [64, 2 * TT], BF16)]
    g8 = b.sbuf("g8", [128, NCHUNK, TT], F32)
    ho = b.sbuf("ho", [128, 2, TT], F32)
    carryA = b.sbuf("carryA", [64, 2], F32)
    carryB = b.sbuf("carryB", [64, 2], F32)
    ps1 = b.psum_pair("ps1", [128, TT], F32)
    ps2 = b.psum_pair("ps2", [128, TT], F32)
    psBr = b.psum("psBr", [64, TT], F32)
    psBi = b.psum("psBi", [64, TT], F32)
    b.op("gpsimd", lambda e: e.memset(carryA[:, :], 0.0))
    ldm = {}
    stm = {}
    sd0 = b.val["sdma"]

    def load(t):
        if 0 <= t < NTILE:
            b.op("sync", lambda e, t=t: e.dma_start(
                out=xt[t % 3][:, :, :], in_=d["xs"][:, :, t * TT:(t + 1) * TT]), dma=True,
                extra={"sdma": sd0})
            ldm[t] = b.val["sync"]

    def pre(t):
        x = xt[t % 3]
        yy = y1[t % 3]
        xb = xnb[t % 2]
        cp = cpb[t % 2]
        ln.stats(lambda c, x=x: x[:, c, :], TT, comp=True, first={"sync": ldm[t]})
        for c in range(NCHUNK):
            b.op("vector", lambda e, c=c, x=x, yy=yy: e.tensor_sub(yy[:, c, :], x[:, c, :], ln.m[:, :]),
                 extra=({"sdma": stm[t - 3]} if (c == 0 and t - 3 in stm) else None))
            b.op("vector", lambda e, c=c, yy=yy: e.tensor_mul(ln.t1[:, :], yy[:, c, :], ln.A2[:, :]))
            b.op("vector", lambda e, c=c, xb=xb: e.tensor_scalar(
                xb[:, c, :], ln.t1[:, :], ln1g[:, c:c + 1], ln1b[:, c:c + 1], ALU.mult, ALU.add))
            b.op("gpsimd", lambda e, c=c, yy=yy: e.tensor_mul(yy[:, c, :], yy[:, c, :], ln.A[:, :]))
        b.mm(psBr[:, :], [(w_in_r[:, k, :], xb[:, k, :]) for k in range(8)])
        b.op("vector", lambda e: e.tensor_copy(bt[:, 0:TT], psBr[:, :]))
        b.mm(psBi[:, :], [(w_in_i[:, k, :], xb[:, k, :]) for k in range(8)])
        b.op("vector", lambda e: e.tensor_copy(bt[:, TT:2 * TT], psBi[:, :]))
        ccol = carryA if t != 1 else carryB
        emit_eigenstate(b, eg, bt, ct, TT, ccol[:, 0:2], egs)
        b.op("vector", lambda e: e.tensor_copy(carryA[:, 0:1], bt[:, TT - 1:TT]))
        b.op("vector", lambda e: e.tensor_copy(carryA[:, 1:2], bt[:, 2 * TT - 1:2 * TT]))
        if t == 0:
            b.op("vector", lambda e: e.tensor_mul(
                carryB[:, :], carryA[:, :], flag[0:64, 0:1].broadcast_to([64, 2])))
        b.op("scalar", lambda e: e.activation(cb[:, :], ct[:, :], AF.Copy, bias=0.0, scale=1.0))
        b.mm(psBr[:, :], [(wcp[:, :], cb[:, 0:TT])])
        b.op("scalar", lambda e, cp=cp: e.activation(cp[:, 0:TT], psBr[:, :], AF.Copy, bias=0.0, scale=1.0))
        b.mm(psBi[:, :], [(wcp[:, :], cb[:, TT:2 * TT])])
        b.op("scalar", lambda e, cp=cp: e.activation(cp[:, TT:2 * TT], psBi[:, :], AF.Copy, bias=0.0, scale=1.0))

    def pipe(t, start):
        xb = xnb[t % 2]
        cp = cpb[t % 2]
        yy = y1[t % 3]
        m1 = b.pipe(8, [
            ("tensor", lambda e, i, xb=xb: b.mmg(e, ps1[i % 2][:, :], [(w_gate[:, k, i, :], xb[:, k, :]) for k in range(8)])),
            ("scalar", lambda e, i: e.activation(
                g8[:, i, :], ps1[i % 2][:, :], AF.Sigmoid, bias=b_gate[:, i:i + 1], scale=1.0)),
        ], start=start)
        b.pipe(8, [
            ("tensor", lambda e, i, cp=cp: b.mmg(e, ps2[i % 2][:, :], [(wor[:, i, :], cp[:, 0:TT]), (woi[:, i, :], cp[:, TT:2 * TT])])),
            ("scalar", lambda e, i: e.activation(
                ho[:, i % 2, :], ps2[i % 2][:, :], AF.Copy, bias=0.0, scale=1.0)),
            ("gpsimd", lambda e, i: e.tensor_mul(ho[:, i % 2, :], ho[:, i % 2, :], g8[:, i, :])),
            ("gpsimd", lambda e, i, yy=yy: e.tensor_add(yy[:, i, :], yy[:, i, :], ho[:, i % 2, :])),
        ], start=mmax(start, m1))

    def post(t):
        yy = y1[t % 3]
        b.op("scalar", lambda e, t=t, yy=yy: e.dma_start(
            out=d["x1"][:, :, t * TT:(t + 1) * TT], in_=yy[:, :, :]), dma=True)
        stm[t] = b.val["sdma"]
        load(t + 3)

    load(0)
    load(1)
    load(2)
    preSnap = {}
    pipeEnd = {}
    b.scope_push()
    pre(0)
    b.scope_pop()
    preSnap[0] = b.now()
    for t in range(NTILE):
        pipe(t, mmax(preSnap[t], pipeEnd.get(t - 1),
                     {"sdma": stm[t - 2]} if t - 2 in stm else None))
        pipeEnd[t] = b.now()
        if t + 1 < NTILE:
            b.scope_push(base=preSnap[t])
            pre(t + 1)
            b.scope_pop()
            preSnap[t + 1] = b.now()
        if t - 1 >= 0:
            b.scope_push()
            post(t - 1)
            b.scope_pop()
    b.scope_push()
    post(NTILE - 1)
    b.scope_pop()
    b.pop()


def emit_l1b(b, cst, d):
    N = 256
    NT = TW // N
    b.push()
    w1 = load_w(b, "w1_s", d["w_mlp1"], 8, 32)
    w2 = load_w(b, "w2_s", d["w_mlp2"], 32, 8)
    b1 = load_cols(b, "b1_s", d["b_mlp1"], 32)
    b2 = load_cols(b, "b2_s", d["b_mlp2"], 8)
    ln2g = load_cols(b, "ln2g_s", d["ln2g"], 8)
    ln2b = load_cols(b, "ln2b_s", d["ln2b"], 8)
    ln = LNScratch(b, N, cst)
    x1t = [b.sbuf("x1t0", [128, NCHUNK, N], F32), b.sbuf("x1t1", [128, NCHUNK, N], F32)]
    xst = [b.sbuf("xst0", [128, NCHUNK, N], F32), b.sbuf("xst1", [128, NCHUNK, N], F32)]
    xn2b = [b.sbuf("xn2b0", [128, NCHUNK, N], BF16), b.sbuf("xn2b1", [128, NCHUNK, N], BF16)]
    h1b = b.sbuf("h1b", [128, 32, N], BF16)
    ao = [b.sbuf("ao0", [128, NCHUNK, N], F32), b.sbuf("ao1", [128, NCHUNK, N], F32)]
    ps1 = b.psum_pair("ps1", [128, N], F32)
    ldm = {}
    stm = {}
    sd0 = b.val["sdma"]

    def loadx1(t):
        if 0 <= t < NT:
            b.op("sync", lambda e, t=t: e.dma_start(
                out=x1t[t % 2][:, :, :], in_=d["x1"][:, :, t * N:(t + 1) * N]), dma=True,
                extra={"sdma": sd0})
            ldm[t] = b.val["sync"]

    def loadxs(t):
        if 0 <= t < NT:
            b.op("sync", lambda e, t=t: e.dma_start(
                out=xst[t % 2][:, :, :], in_=d["xs"][:, :, t * N:(t + 1) * N]), dma=True,
                extra={"sdma": sd0})

    def load(t):
        loadx1(t)
        loadxs(t)

    def pre(t):
        x1 = x1t[t % 2]
        xb = xn2b[t % 2]
        ln.stats(lambda c, x1=x1: x1[:, c, :], N)
        for c in range(NCHUNK):
            b.op("vector", lambda e, c=c, x1=x1: e.tensor_sub(ln.t1[:, :N], x1[:, c, :], ln.m[:, :N]))
            b.op("vector", lambda e, c=c: e.tensor_mul(ln.t1[:, :N], ln.t1[:, :N], ln.A[:, :N]))
            b.op("vector", lambda e, c=c, xb=xb: e.tensor_scalar(
                xb[:, c, :], ln.t1[:, :N], ln2g[:, c:c + 1], ln2b[:, c:c + 1], ALU.mult, ALU.add))

    def pipe(t, start):
        x1 = x1t[t % 2]
        xb = xn2b[t % 2]
        a = ao[t % 2]
        m1 = b.pipe(32, [
            ("tensor", lambda e, i, xb=xb: b.mmg(e, ps1[i % 2][:, 0:N], [(w1[:, k, i, :], xb[:, k, :]) for k in range(8)])),
            ("scalar", lambda e, i: e.activation(
                h1b[:, i, :], ps1[i % 2][:, 0:N], AF.Silu, bias=b1[:, i:i + 1], scale=1.0)),
        ], start=start)
        b.pipe(8, [
            ("tensor", lambda e, i: b.mmg(e, ps1[i % 2][:, 0:N], [(w2[:, k, i, :], h1b[:, k, :]) for k in range(32)])),
            ("scalar", lambda e, i, a=a: e.activation(
                a[:, i, :], ps1[i % 2][:, 0:N], AF.Identity, bias=b2[:, i:i + 1], scale=1.0)),
            ("vector", lambda e, i, a=a, x1=x1: e.tensor_add(a[:, i, :], a[:, i, :], x1[:, i, :])),
        ], start=mmax(start, m1))

    def post(t):
        xs = xst[t % 2]
        a = ao[t % 2]
        ln.stats(lambda c, a=a: a[:, c, :], N)
        for c in range(NCHUNK):
            b.op("vector", lambda e, c=c, a=a: e.tensor_sub(a[:, c, :], a[:, c, :], ln.m[:, :N]))
            b.op("vector", lambda e, c=c, a=a: e.tensor_mul(a[:, c, :], a[:, c, :], ln.A[:, :N]))
            b.op("vector", lambda e, c=c, a=a, xs=xs: e.tensor_add(a[:, c, :], a[:, c, :], xs[:, c, :]))
        b.op("scalar", lambda e, t=t, a=a: e.dma_start(
            out=d["s1"][:, :, t * N:(t + 1) * N], in_=a[:, :, :]), dma=True)
        stm[t] = b.val["sdma"]
        loadxs(t + 2)

    load(0)
    load(1)
    preSnap = {}
    pipeEnd = {}
    b.scope_push()
    pre(0)
    b.scope_pop()
    preSnap[0] = b.now()
    for t in range(NT):
        if t + 1 < NT:
            b.scope_push()
            pre(t + 1)
            b.scope_pop()
            preSnap[t + 1] = b.now()
        pipe(t, mmax(preSnap[t], pipeEnd.get(t - 1),
                     {"sdma": stm[t - 2]} if t - 2 in stm else None))
        pipeEnd[t] = b.now()
        loadx1(t + 2)
        if t - 1 >= 0:
            b.scope_push()
            post(t - 1)
            b.scope_pop()
    b.scope_push()
    post(NT - 1)
    b.scope_pop()
    b.pop()


def emit_l2(b, cst, d):
    emit_l2a(b, cst, d)
    emit_l2b(b, cst, d)


def emit_l2a(b, cst, d):
    SC = 1.0 / 127.0
    b.push()
    wg = load_w(b, "wg_s", d["wg"], 8, 24)
    wu = load_w(b, "wu_s", d["wu"], 8, 24)
    ln = LNScratch(b, TT, cst)
    xt = [b.sbuf("xt0", [128, NCHUNK, TT], F32), b.sbuf("xt1", [128, NCHUNK, TT], F32)]
    y2b = [b.sbuf("y2b0", [128, NCHUNK, TT], BF16), b.sbuf("y2b1", [128, NCHUNK, TT], BF16)]
    sg_t = b.sbuf("sg_t", [128, 2, TT], F32)
    ht = b.sbuf("ht", [128, 24, TT], BF16)
    ps1 = b.psum_pair("ps1", [128, TT], F32)
    ps2 = b.psum_pair("ps2", [128, TT], F32)
    ldm = {}
    stm = {}
    sd0 = b.val["sdma"]

    def load(t):
        if 0 <= t < NTILE:
            b.op("sync", lambda e, t=t: e.dma_start(
                out=xt[t % 2][:, :, :], in_=d["s1"][:, :, t * TT:(t + 1) * TT]), dma=True,
                extra={"sdma": sd0})
            ldm[t] = b.val["sync"]

    def pre(t):
        x = xt[t % 2]
        yb = y2b[t % 2]
        ln.stats(lambda c, x=x: x[:, c, :], TT)
        for c in range(NCHUNK):
            b.op("vector", lambda e, c=c, x=x: e.tensor_sub(ln.t1[:, :], x[:, c, :], ln.m[:, :]))
            b.op("vector", lambda e, c=c, yb=yb: e.tensor_mul(yb[:, c, :], ln.t1[:, :], ln.A[:, :]))

    def pipe(t, start):
        yb = y2b[t % 2]

        def l2mm(e, i, yb=yb):
            b.mmg(e, ps1[i % 2][:, :], [(wg[:, k, i, :], yb[:, k, :]) for k in range(8)])
            return b.mmg(e, ps2[i % 2][:, :], [(wu[:, k, i, :], yb[:, k, :]) for k in range(8)])
        b.pipe(24, [
            ("tensor", l2mm),
            ("scalar", lambda e, i: e.activation(
                sg_t[:, i % 2, :], ps1[i % 2][:, :], AF.Silu, bias=0.0, scale=SC)),
            ("vector", lambda e, i: e.scalar_tensor_tensor(
                ht[:, i, :], ps2[i % 2][:, :], SC, sg_t[:, i % 2, :], ALU.mult, ALU.mult)),
        ], start=start)

    def post(t):
        b.op("scalar", lambda e, t=t: e.dma_start(
            out=d["hq"][:, :, t * TT:(t + 1) * TT], in_=ht[:, :, :]), dma=True)
        stm[t] = b.val["sdma"]
        load(t + 2)


    load(0)
    load(1)
    preSnap = {}
    pipeEnd = {}
    b.scope_push()
    pre(0)
    b.scope_pop()
    preSnap[0] = b.now()
    pipe(0, preSnap[0])
    pipeEnd[0] = b.now()
    for t in range(NTILE):
        if t + 1 < NTILE:
            b.scope_push()
            pre(t + 1)
            b.scope_pop()
            preSnap[t + 1] = b.now()
        b.scope_push()
        post(t)
        b.scope_pop()
        if t + 1 < NTILE:
            pipe(t + 1, mmax(preSnap[t + 1], pipeEnd[t], {"sdma": stm[t]}))
            pipeEnd[t + 1] = b.now()
    b.pop()


def emit_l2b(b, cst, d):
    SC = 1.0 / 127.0
    b.push()
    wd = load_w(b, "wd_s", d["wd"], 24, 8)
    ln = LNScratch(b, TT, cst)
    xt = [b.sbuf("xt0", [128, NCHUNK, TT], F32), b.sbuf("xt1", [128, NCHUNK, TT], F32)]
    ht = [b.sbuf("ht0", [128, 24, TT], BF16), b.sbuf("ht1", [128, 24, TT], BF16)]
    Ft = [b.sbuf("Ft0", [128, NCHUNK, TT], F32), b.sbuf("Ft1", [128, NCHUNK, TT], F32)]
    ps1 = b.psum_pair("ps1", [128, TT], F32)
    ldm = {}
    stm = {}
    sd0 = b.val["sdma"]

    def loadxt(t):
        if 0 <= t < NTILE:
            b.op("sync", lambda e, t=t: e.dma_start(
                out=xt[t % 2][:, :, :], in_=d["s1"][:, :, t * TT:(t + 1) * TT]), dma=True,
                extra={"sdma": sd0})

    def loadht(t):
        if 0 <= t < NTILE:
            b.op("sync", lambda e, t=t: e.dma_start(
                out=ht[t % 2][:, :, :], in_=d["hq"][:, :, t * TT:(t + 1) * TT]), dma=True,
                extra={"sdma": sd0})
            ldm[t] = b.val["sync"]

    def load(t):
        loadxt(t)
        loadht(t)

    def pipe(t, start):
        h = ht[t % 2]
        f = Ft[t % 2]
        ex = {"sync": ldm[t], "gdma": b.val["gdma"]}
        if t - 2 in stm:
            ex["sdma"] = stm[t - 2]
        b.pipe(8, [
            ("tensor", lambda e, i, h=h: b.mmg(
                e, ps1[i % 2][:, :], [(wd[:, k, i, :], h[:, k, :]) for k in range(24)])),
            ("scalar", lambda e, i, f=f: e.activation(
                f[:, i, :], ps1[i % 2][:, :], AF.Identity, bias=0.0, scale=SC)),
        ], start=dict(start), extra=ex)

    def post(t):
        x = xt[t % 2]
        f = Ft[t % 2]
        ln.stats(lambda c, f=f: f[:, c, :], TT)
        for c in range(NCHUNK):
            b.op("vector", lambda e, c=c, f=f: e.tensor_sub(f[:, c, :], f[:, c, :], ln.m[:, :]))
            b.op("vector", lambda e, c=c, f=f: e.tensor_mul(f[:, c, :], f[:, c, :], ln.A[:, :]))
            b.op("vector", lambda e, c=c, f=f, x=x: e.tensor_add(f[:, c, :], f[:, c, :], x[:, c, :]))
        b.op("scalar", lambda e, t=t, f=f: e.dma_start(
            out=d["s2"][:, :, t * TT:(t + 1) * TT], in_=f[:, :, :]), dma=True)
        stm[t] = b.val["sdma"]
        loadxt(t + 2)

    load(0)
    load(1)
    pipeEnd = {}
    postEnd = {}
    pipe(0, b.now())
    pipeEnd[0] = b.now()
    for t in range(NTILE):
        if t + 1 < NTILE:
            pipe(t + 1, mmax(pipeEnd[t], postEnd.get(t - 1)))
            pipeEnd[t + 1] = b.now()
            loadht(t + 2)
        b.scope_push()
        post(t)
        b.scope_pop()
        postEnd[t] = b.now()
    b.pop()


def emit_l3(b, cst, d):
    b.push()
    wk = load_w(b, "wk_s", d["wk"], 8, 8)
    wv = load_w(b, "wv_s", d["wv"], 8, 8)
    wr = load_w(b, "wr_s", d["wr"], 8, 8)
    wo = load_w(b, "wo_s", d["wo"], 8, 8)
    lnw = load_cols(b, "lnw_s", d["lnw"], 8)
    lnb = load_cols(b, "lnb_s", d["lnb"], 8)
    tmk = load_cols(b, "tmk_s", d["tmk"], 8)
    tmv = load_cols(b, "tmv_s", d["tmv"], 8)
    tmr = load_cols(b, "tmr_s", d["tmr"], 8)
    dw = load_cols(b, "dw_s", d["dw"], 8)
    flag = load_cols(b, "flag_s", d["flag"], 1)
    dec = b.sbuf("dec", [128, 8], F32)
    b.op("scalar", lambda e: e.activation(dec[:, :], dw[:, :], AF.Sigmoid, bias=0.0, scale=1.0),
         extra={"sync": b.val["sync"]})
    omk = b.sbuf("omk", [128, 8], F32)
    omv = b.sbuf("omv", [128, 8], F32)
    omr = b.sbuf("omr", [128, 8], F32)
    for om, tm in ((omk, tmk), (omv, tmv), (omr, tmr)):
        b.op("vector", lambda e, om=om, tm=tm: e.tensor_scalar(
            om[:, :], tm[:, :], -1.0, 1.0, ALU.mult, ALU.add))
    ln = LNScratch(b, TT, cst)
    xt = [b.sbuf("xt0", [128, NCHUNK, TT], F32), b.sbuf("xt1", [128, NCHUNK, TT], F32)]
    y3 = b.sbuf("y3", [128, NCHUNK, TT], F32)
    xn = b.sbuf("xn", [128, NCHUNK, TT + 1], F32)
    mixk = b.sbuf("mixk", [128, NCHUNK, TT], BF16)
    mixv = b.sbuf("mixv", [128, NCHUNK, TT], BF16)
    mixr = b.sbuf("mixr", [128, NCHUNK, TT], BF16)
    kf = b.sbuf("kf", [128, 2, TT], F32)
    kv = b.sbuf("kv", [128, 2, TT], F32)
    ssq = b.sbuf("ssq", [128, 2, TT], F32)
    ssb = b.sbuf("ssb", [128, NCHUNK, TT], BF16)
    rf = b.sbuf("rf", [128, NCHUNK, TT], F32)
    scol = b.sbuf("scol", [128, 8], F32)
    xcar = b.sbuf("xcar", [128, 8], F32)
    psK = b.psum_pair("psK", [128, TT], F32)
    psV = b.psum_pair("psV", [128, TT], F32)
    psR = b.psum_pair("psR", [128, TT], F32)
    b.op("gpsimd", lambda e: e.memset(scol[:, :], 0.0),
         extra={"sdma": b.val["sdma"], "sync": b.val["sync"]})
    b.op("gpsimd", lambda e: e.memset(xcar[:, :], 0.0))
    ldm = {}
    stm = {}

    sd0 = b.val["sdma"]

    def load(t):
        if 0 <= t < NTILE:
            b.op("sync", lambda e, t=t: e.dma_start(
                out=xt[t % 2][:, :, :], in_=d["s2"][:, :, t * TT:(t + 1) * TT]), dma=True,
                extra={"sdma": sd0})
            ldm[t] = b.val["sync"]
    load(0)
    for t in range(NTILE):
        load(t + 1)
        x = xt[t % 2]
        ln.stats(lambda c, x=x: x[:, c, :], TT, comp=True, first={"sync": ldm[t]})
        for c in range(NCHUNK):
            b.op("vector", lambda e, c=c, x=x: e.tensor_sub(y3[:, c, :], x[:, c, :], ln.m[:, :]),
                 extra=({"sdma": stm[t - 1]} if (c == 0 and t - 1 in stm) else None))
            b.op("vector", lambda e, c=c: e.tensor_mul(ln.t1[:, :], y3[:, c, :], ln.A2[:, :]))
            b.op("vector", lambda e, c=c: e.tensor_scalar(
                xn[:, c, 1:TT + 1], ln.t1[:, :], lnw[:, c:c + 1], lnb[:, c:c + 1], ALU.mult, ALU.add))
            b.op("vector", lambda e, c=c: e.tensor_mul(y3[:, c, :], y3[:, c, :], ln.A[:, :]))
            b.op("vector", lambda e, c=c: e.tensor_copy(xn[:, c, 0:1], xcar[:, c:c + 1]))
        b.op("vector", lambda e: e.tensor_copy(xcar[:, :], xn[:, :, TT:TT + 1].squeeze(2)))
        if t == 0:
            b.op("vector", lambda e: e.tensor_mul(
                xcar[:, :], xcar[:, :], flag[:, 0:1].broadcast_to([128, 8])))
        for c in range(NCHUNK):
            for mix, tm, om in ((mixk, tmk, omk), (mixv, tmv, omv), (mixr, tmr, omr)):
                b.op("vector", lambda e, c=c, mix=mix, tm=tm: e.tensor_scalar(
                    ln.t1[:, :], xn[:, c, 1:TT + 1], tm[:, c:c + 1], None, ALU.mult))
                b.op("vector", lambda e, c=c, mix=mix, om=om: e.scalar_tensor_tensor(
                    mix[:, c, :], xn[:, c, 0:TT], om[:, c:c + 1], ln.t1[:, :], ALU.mult, ALU.add))
        b.pipe(8, [
            ("tensor", lambda e, i: b.mmg(e, psK[i % 2][:, :], [(wk[:, k, i, :], mixk[:, k, :]) for k in range(8)])),
            ("scalar", lambda e, i: e.activation(kf[:, i % 2, :], psK[i % 2][:, :], AF.Copy, bias=0.0, scale=1.0)),
            ("tensor", lambda e, i: b.mmg(e, psV[i % 2][:, :], [(wv[:, k, i, :], mixv[:, k, :]) for k in range(8)])),
            ("vector", lambda e, i: e.tensor_mul(kv[:, i % 2, :], kf[:, i % 2, :], psV[i % 2][:, :])),
            ("vector", lambda e, i: e.tensor_tensor_scan(
                ssq[:, i % 2, :], dec[:, i:i + 1].broadcast_to([128, TT]), kv[:, i % 2, :],
                scol[:, i:i + 1], ALU.mult, ALU.add)),
            ("vector", lambda e, i: e.tensor_copy(scol[:, i:i + 1], ssq[:, i % 2, TT - 1:TT])),
            ("scalar", lambda e, i: e.activation(ssb[:, i, :], ssq[:, i % 2, :], AF.Copy, bias=0.0, scale=1.0)),
            ("tensor", lambda e, i: b.mmg(e, psR[i % 2][:, :], [(wr[:, k, i, :], mixr[:, k, :]) for k in range(8)])),
            ("scalar", lambda e, i: e.activation(rf[:, i, :], psR[i % 2][:, :], AF.Sigmoid, bias=0.0, scale=1.0)),
        ])
        if t == 0:
            b.op("vector", lambda e: e.tensor_mul(
                scol[:, :], scol[:, :], flag[:, 0:1].broadcast_to([128, 8])))
        b.pipe(8, [
            ("tensor", lambda e, i: b.mmg(e, psK[i % 2][:, :], [(wo[:, k, i, :], ssb[:, k, :]) for k in range(8)])),
            ("vector", lambda e, i: e.tensor_mul(rf[:, i, :], rf[:, i, :], psK[i % 2][:, :])),
            ("vector", lambda e, i: e.tensor_add(y3[:, i, :], y3[:, i, :], rf[:, i, :])),
        ])
        ln.stats(lambda c: y3[:, c, :], TT)
        for c in range(NCHUNK):
            b.op("vector", lambda e, c=c: e.tensor_sub(y3[:, c, :], y3[:, c, :], ln.m[:, :]))
            b.op("vector", lambda e, c=c: e.tensor_mul(y3[:, c, :], y3[:, c, :], ln.A[:, :]))
            b.op("vector", lambda e, c=c, x=x: e.tensor_add(y3[:, c, :], y3[:, c, :], x[:, c, :]))
        b.op("scalar", lambda e, t=t: e.dma_start(out=d["s3"][:, :, t * TT:(t + 1) * TT], in_=y3[:, :, :]), dma=True)
        stm[t] = b.val["sdma"]
    b.pop()


def emit_l4(b, cst, d):
    b.push()
    u1 = load_w(b, "u1_s", d["u1"], 8, 2)
    v1 = load_w(b, "v1_s", d["v1"], 2, 32)
    u2 = load_w(b, "u2_s", d["u2"], 32, 2)
    v2 = load_w(b, "v2_s", d["v2"], 2, 8)
    b1 = load_cols(b, "b1_s", d["fb1"], 32)
    b2 = load_cols(b, "b2_s", d["fb2"], 8)
    ln = LNScratch(b, TT, cst)
    lnP = LNScratch(b, TT, cst)
    xt = [b.sbuf("xt0", [128, NCHUNK, TT], F32), b.sbuf("xt1", [128, NCHUNK, TT], F32),
          b.sbuf("xt2", [128, NCHUNK, TT], F32)]
    y4b = [b.sbuf("y4b0", [128, NCHUNK, TT], BF16), b.sbuf("y4b1", [128, NCHUNK, TT], BF16)]
    pb = b.sbuf("pb", [128, 2, TT], BF16)
    hb = b.sbuf("hb", [128, 32, TT], BF16)
    qb = b.sbuf("qb", [128, 2, TT], BF16)
    G = [b.sbuf("G40", [128, NCHUNK, TT], F32), b.sbuf("G41", [128, NCHUNK, TT], F32)]
    ps1 = b.psum_pair("ps41", [128, TT], F32)
    ldm = {}
    stm = {}
    sd0 = b.val["sdma"]

    def load(t):
        if 1 <= t < NTILE:
            b.op("sync", lambda e, t=t: e.dma_start(
                out=xt[t % 3][:, :, :], in_=d["s3"][:, :, t * TT:(t + 1) * TT]), dma=True,
                extra={"sdma": sd0})
            ldm[t] = b.val["sync"]

    def pre(t):
        x = xt[t % 3]
        yb = y4b[t % 2]
        ln.stats(lambda c, x=x: x[:, c, :], TT)
        for c in range(NCHUNK):
            b.op("vector", lambda e, c=c, x=x: e.tensor_sub(ln.t1[:, :], x[:, c, :], ln.m[:, :]))
            b.op("vector", lambda e, c=c, yb=yb: e.tensor_mul(yb[:, c, :], ln.t1[:, :], ln.A[:, :]))

    def pipe(t, start):
        yb = y4b[t % 2]
        g = G[t % 2]
        m1 = b.pipe(2, [
            ("tensor", lambda e, i, yb=yb: b.mmg(e, ps1[i % 2][:, :], [(u1[:, k, i, :], yb[:, k, :]) for k in range(8)])),
            ("scalar", lambda e, i: e.activation(pb[:, i, :], ps1[i % 2][:, :], AF.Copy, bias=0.0, scale=1.0)),
        ], start=start)
        m2 = b.pipe(32, [
            ("tensor", lambda e, i: b.mmg(e, ps1[i % 2][:, :], [(v1[:, k, i, :], pb[:, k, :]) for k in range(2)])),
            ("scalar", lambda e, i: e.activation(
                hb[:, i, :], ps1[i % 2][:, :], AF.Gelu_apprx_tanh, bias=b1[:, i:i + 1], scale=1.0)),
        ], start=mmax(start, m1))
        m3 = b.pipe(2, [
            ("tensor", lambda e, i: b.mmg(e, ps1[i % 2][:, :], [(u2[:, k, i, :], hb[:, k, :]) for k in range(32)])),
            ("scalar", lambda e, i: e.activation(qb[:, i, :], ps1[i % 2][:, :], AF.Copy, bias=0.0, scale=1.0)),
        ], start=mmax(start, m2))
        b.pipe(8, [
            ("tensor", lambda e, i: b.mmg(e, ps1[i % 2][:, :], [(v2[:, k, i, :], qb[:, k, :]) for k in range(2)])),
            ("scalar", lambda e, i, g=g: e.activation(
                g[:, i, :], ps1[i % 2][:, :], AF.Identity, bias=b2[:, i:i + 1], scale=1.0)),
        ], start=mmax(start, m3))

    def post(t):
        x = xt[t % 3]
        g = G[t % 2]
        lnP.stats(lambda c, g=g: g[:, c, :], TT)
        for c in range(NCHUNK):
            b.op("vector", lambda e, c=c, g=g: e.tensor_sub(g[:, c, :], g[:, c, :], lnP.m[:, :]))
            b.op("vector", lambda e, c=c, g=g: e.tensor_mul(g[:, c, :], g[:, c, :], lnP.A[:, :]))
            b.op("vector", lambda e, c=c, g=g, x=x: e.tensor_add(g[:, c, :], g[:, c, :], x[:, c, :]))
        b.op("scalar", lambda e, t=t, g=g: e.dma_start(
            out=d["out"][:, :, (t - 1) * TT:t * TT], in_=g[:, :, :]), dma=True)
        stm[t] = b.val["sdma"]
        load(t + 3)

    load(1)
    load(2)
    load(3)
    preSnap = {}
    pipeEnd = {}
    postEnd = {}
    b.scope_push()
    pre(1)
    b.scope_pop()
    preSnap[1] = b.now()
    for t in range(1, NTILE):
        if t + 1 < NTILE:
            b.scope_push()
            pre(t + 1)
            b.scope_pop()
            preSnap[t + 1] = b.now()
        pipe(t, mmax(preSnap[t], pipeEnd.get(t - 1), postEnd.get(t - 2),
                     {"sdma": stm[t - 2]} if t - 2 in stm else None))
        pipeEnd[t] = b.now()
        if t - 1 >= 1:
            b.scope_push()
            post(t - 1)
            b.scope_pop()
            postEnd[t - 1] = b.now()
    b.scope_push()
    post(NTILE - 1)
    b.scope_pop()
    b.pop()



def build_fused():
    b = B()
    d = {}
    d["xs"] = b.dram_in("xs", [128, NCHUNK, TW])
    d["flag"] = b.dram_in("flag", [128, 1])
    d["w_in_r"] = b.dram_in("w_in_r", [128, 8, 64])
    d["w_in_i"] = b.dram_in("w_in_i", [128, 8, 64])
    d["w_gate"] = b.dram_in("w_gate", [128, 8, 8, 128])
    d["w_out_r"] = b.dram_in("w_out_r", [64, 8, 128])
    d["w_out_i"] = b.dram_in("w_out_i", [64, 8, 128])
    d["coupT"] = b.dram_in("coupT", [4, 16, 16])
    d["b_gate"] = b.dram_in("b_gate", [128, 8])
    d["ln1g"] = b.dram_in("ln1g", [128, 8])
    d["ln1b"] = b.dram_in("ln1b", [128, 8])
    d["ldecay"] = b.dram_in("ldecay", [64, 1])
    d["freq"] = b.dram_in("freq", [64, 1])
    d["w_mlp1"] = b.dram_in("w_mlp1", [128, 8, 32, 128])
    d["w_mlp2"] = b.dram_in("w_mlp2", [128, 32, 8, 128])
    d["b_mlp1"] = b.dram_in("b_mlp1", [128, 32])
    d["b_mlp2"] = b.dram_in("b_mlp2", [128, 8])
    d["ln2g"] = b.dram_in("ln2g", [128, 8])
    d["ln2b"] = b.dram_in("ln2b", [128, 8])
    d["wg"] = b.dram_in("wg", [128, 8, 24, 128], I32)
    d["wu"] = b.dram_in("wu", [128, 8, 24, 128], I32)
    d["wd"] = b.dram_in("wd", [128, 24, 8, 128], I32)
    d["wk"] = b.dram_in("wk", [128, 8, 8, 128])
    d["wv"] = b.dram_in("wv", [128, 8, 8, 128])
    d["wr"] = b.dram_in("wr", [128, 8, 8, 128])
    d["wo"] = b.dram_in("wo", [128, 8, 8, 128])
    d["lnw"] = b.dram_in("lnw", [128, 8])
    d["lnb"] = b.dram_in("lnb", [128, 8])
    d["tmk"] = b.dram_in("tmk", [128, 8])
    d["tmv"] = b.dram_in("tmv", [128, 8])
    d["tmr"] = b.dram_in("tmr", [128, 8])
    d["dw"] = b.dram_in("dw", [128, 8])
    d["u1"] = b.dram_in("u1", [128, 8, 2, 128])
    d["v1"] = b.dram_in("v1", [128, 2, 32, 128])
    d["u2"] = b.dram_in("u2", [128, 32, 2, 128])
    d["v2"] = b.dram_in("v2", [128, 2, 8, 128])
    d["fb1"] = b.dram_in("fb1", [128, 32])
    d["fb2"] = b.dram_in("fb2", [128, 8])
    d["out"] = b.dram_out("out", [128, NCHUNK, TW - TT])
    d["x1"] = b.dram_tmp("x1_t", [128, NCHUNK, TW])
    d["s1"] = b.dram_tmp("s1_t", [128, NCHUNK, TW])
    d["hq"] = b.dram_tmp("hq_t", [128, 24, TW], BF16)
    d["s2"] = b.dram_tmp("s2_t", [128, NCHUNK, TW])
    d["s3"] = b.dram_tmp("s3_t", [128, NCHUNK, TW])
    b.begin()
    cst = emit_consts(b)
    emit_l1a(b, cst, d)
    emit_l1b(b, cst, d)
    emit_l2(b, cst, d)
    emit_l3(b, cst, d)
    emit_l4(b, cst, d)
    return b.end()



_NCS = {}


def _get_nc(name, fn):
    if name not in _NCS:
        _NCS[name] = fn()
    return _NCS[name]


def kernel(**ins):
    x = np.asarray(ins["L0_attn_x"])
    B_, T_, D_ = x.shape
    H = T_ // 2

    xs_cores = []
    flags = []
    for b_i in range(B_):
        for h in range(2):
            main = x[b_i, h * H:(h + 1) * H, :]
            if h == 1:
                wu = x[b_i, H - TT:H, :]
                fl = np.ones((128, 1), np.float32)
            else:
                wu = np.zeros((TT, D_), np.float32)
                fl = np.zeros((128, 1), np.float32)
            slab = np.concatenate([wu, main], axis=0)
            xs_cores.append(stage_stream(slab))
            flags.append(fl)

    stat = dict(
        w_in_r=stage_w64_out(np.asarray(ins["L0_attn_w_in"])[0:64, :]),
        w_in_i=stage_w64_out(np.asarray(ins["L0_attn_w_in"])[64:128, :]),
        w_gate=stage_w(ins["L0_attn_w_gate"]),
        w_out_r=stage_w64_in(np.asarray(ins["L0_attn_w_out"])[:, 0:64]),
        w_out_i=stage_w64_in(np.asarray(ins["L0_attn_w_out"])[:, 64:128]),
        coupT=np.ascontiguousarray(np.asarray(ins["L0_attn_coupling"]).transpose(0, 2, 1)),
        b_gate=stage_cols(ins["L0_attn_b_gate"]),
        ln1g=stage_cols(ins["L0_attn_ln1_g"]),
        ln1b=stage_cols(ins["L0_attn_ln1_b"]),
        ldecay=np.ascontiguousarray(np.asarray(ins["L0_attn_log_decay"]).reshape(64, 1)),
        freq=np.ascontiguousarray(np.asarray(ins["L0_attn_frequency"]).reshape(64, 1)),
        w_mlp1=stage_w(ins["L0_attn_w_mlp1"]),
        w_mlp2=stage_w(ins["L0_attn_w_mlp2"]),
        b_mlp1=stage_cols(ins["L0_attn_b_mlp1"]),
        b_mlp2=stage_cols(ins["L0_attn_b_mlp2"]),
        ln2g=stage_cols(ins["L0_attn_ln2_g"]),
        ln2b=stage_cols(ins["L0_attn_ln2_b"]),
        wg=stage_w(ins["L0_ffn_w_gate"]),
        wu=stage_w(ins["L0_ffn_w_up"]),
        wd=stage_w(ins["L0_ffn_w_down"]),
        wk=stage_w(ins["L1_attn_Wk"]),
        wv=stage_w(ins["L1_attn_Wv"]),
        wr=stage_w(ins["L1_attn_Wr"]),
        wo=stage_w(ins["L1_attn_Wo"]),
        lnw=stage_cols(ins["L1_attn_ln_w"]),
        lnb=stage_cols(ins["L1_attn_ln_b"]),
        tmk=stage_cols(ins["L1_attn_tmk"]),
        tmv=stage_cols(ins["L1_attn_tmv"]),
        tmr=stage_cols(ins["L1_attn_tmr"]),
        dw=stage_cols(ins["L1_attn_decay_w"]),
        u1=stage_w(np.asarray(ins["L1_ffn_U1"]).T),
        v1=stage_w(np.asarray(ins["L1_ffn_V1"]).T),
        u2=stage_w(np.asarray(ins["L1_ffn_U2"]).T),
        v2=stage_w(np.asarray(ins["L1_ffn_V2"]).T),
        fb1=stage_cols(ins["L1_ffn_b1"]),
        fb2=stage_cols(ins["L1_ffn_b2"]),
    )

    nc = _get_nc("fused", build_fused)
    maps = [dict(stat, xs=xs_cores[i], flag=flags[i]) for i in range(8)]
    r = run_bass_kernel_spmd(nc, maps, core_ids=list(range(8)))

    out = np.empty((B_, T_, D_), np.float32)
    for b_i in range(B_):
        for h in range(2):
            arr = np.asarray(r.results[b_i * 2 + h]["out"])
            out[b_i, h * H:(h + 1) * H, :] = unstage_stream(arr)
    return out


# version 15
# speedup vs baseline: 1.0809x; 1.0809x over previous
import math
import contextlib
import numpy as np
import concourse.bass as bass
import concourse.mybir as mybir
from concourse.bass_utils import run_bass_kernel_spmd

F32 = mybir.dt.float32
BF16 = mybir.dt.bfloat16
I32 = mybir.dt.int32
AF = mybir.ActivationFunctionType
ALU = mybir.AluOpType

D = 1024
NCHUNK = 8
TT = 512
NTILE = 9
TW = TT * NTILE
EPS = 1e-5
C0 = (1.0 + EPS) ** -0.5
SEG = 16
CH = 64
PI2_HI = float(np.float32(math.pi / 2))
PI2_LO = math.pi / 2 - PI2_HI



def stage_w(W):
    W = np.ascontiguousarray(W)
    oc = W.shape[0] // 128
    ic = W.shape[1] // 128
    t = W.reshape(oc, 128, ic, 128).transpose(3, 2, 0, 1)
    return np.ascontiguousarray(t)


def stage_w64_out(W):
    W = np.ascontiguousarray(W)
    ic = W.shape[1] // 128
    return np.ascontiguousarray(W.reshape(64, ic, 128).transpose(2, 1, 0))


def stage_w64_in(W):
    W = np.ascontiguousarray(W)
    oc = W.shape[0] // 128
    return np.ascontiguousarray(W.reshape(oc, 128, 64).transpose(2, 0, 1))


def stage_cols(v):
    return np.ascontiguousarray(np.asarray(v).reshape(-1, 128).T)


def stage_stream(x_slab):
    T = x_slab.shape[0]
    t = x_slab.T.reshape(NCHUNK, 128, T)
    return np.ascontiguousarray(t.transpose(1, 0, 2))


def unstage_stream(arr):
    return np.ascontiguousarray(arr.transpose(1, 0, 2).reshape(D, -1).T)



ENGS = ("tensor", "vector", "scalar", "gpsimd", "sync", "gdma", "sdma")
COMPUTE_LANES = ("tensor", "vector", "scalar", "gpsimd", "gdma")


class B:
    def __init__(self):
        self.nc = bass.Bass("TRN2", target_bir_lowering=False, debug=False)
        self.stack = contextlib.ExitStack()
        self.sects = []
        self._blk = None
        self.sems = {}
        self.val = {e: 0 for e in ENGS}
        self.scopes = []
        self.uid = 0

    def dram_in(self, name, shape, dtype=F32):
        return self.nc.dram_tensor(name, shape, dtype, kind="ExternalInput").ap()

    def dram_out(self, name, shape, dtype=F32):
        return self.nc.dram_tensor(name, shape, dtype, kind="ExternalOutput").ap()

    def dram_tmp(self, name, shape, dtype=F32):
        return self.nc.dram_tensor(name, shape, dtype, kind="Internal").ap()

    def _st(self):
        return self.sects[-1] if self.sects else self.stack

    def sbuf(self, name, shape, dtype=F32):
        self.uid += 1
        return self._st().enter_context(
            self.nc.sbuf_tensor(f"{name}_{self.uid}", shape, dtype))

    def psum_pair(self, name, shape, dtype=F32):
        return [self.psum(name + "a", shape, dtype), self.psum(name + "b", shape, dtype)]

    def psum(self, name, shape, dtype=F32):
        self.uid += 1
        shape = list(shape)
        if dtype == F32 and shape[-1] < 512:
            shape[-1] = 512
        return self._st().enter_context(
            self.nc.psum_tensor(f"{name}_{self.uid}", shape, dtype))

    def scope_push(self, base=None):
        self.scopes.append({"base": dict(self.val) if base is None else dict(base), "seen": {}})

    def scope_pop(self):
        self.scopes.pop()

    def push(self):
        self.sects.append(contextlib.ExitStack())

    def pop(self):
        self.sects.pop().close()

    def begin(self):
        for e in ENGS:
            self.sems[e] = self.stack.enter_context(self.nc.semaphore(f"sem_{e}"))
        self._blk = self.stack.enter_context(self.nc.Block())
        self.bodies = {e: [] for e in ("tensor", "vector", "scalar", "gpsimd", "sync")}

    def now(self):
        return dict(self.val)

    def op(self, eng, fn, dma=False, after=None, extra=None):
        key = eng
        if dma:
            key = {"gpsimd": "gdma", "scalar": "sdma"}.get(eng, eng)
        if after is None:
            if self.scopes:
                sc = self.scopes[-1]
                after = dict(sc["base"])
                for l2, v2 in sc["seen"].items():
                    after[l2] = max(after.get(l2, 0), v2)
            else:
                after = {e: self.val[e] for e in COMPUTE_LANES}
        if extra:
            after = dict(after)
            for k2, v2 in extra.items():
                after[k2] = max(after.get(k2, 0), v2)
        if dma:
            after = dict(after)
            after[key] = max(after.get(key, 0), self.val[key])
        waits = [(self.sems[e], v) for e, v in after.items() if v > 0]
        inc = 16 if dma else 1
        sem = self.sems[key]

        def body(e):
            for smh, v in waits:
                e.wait_ge(smh, v)
            ins = fn(e)
            ins.then_inc(sem, inc)
        self.bodies[eng].append(body)
        self.val[key] += inc
        if self.scopes:
            self.scopes[-1]["seen"][key] = self.val[key]

    def mm(self, ps_ap, pairs, after=None):
        def f(e):
            last = None
            n = len(pairs)
            for i, (l, r) in enumerate(pairs):
                last = e.matmul(ps_ap, l, r, start=(i == 0), stop=(i == n - 1))
            return last
        self.op("tensor", f, after=after)

    def pipe(self, n, stages, guard=2, extra=None, start=None):
        if start is None:
            start = {e: self.val[e] for e in COMPUTE_LANES}
        else:
            start = dict(start)
        if extra:
            for k2, v2 in extra.items():
                start[k2] = max(start.get(k2, 0), v2)
        last_eng = stages[-1][0]
        last_marks = []
        s0_mark = None
        touched = {}
        for i in range(n):
            prev = None
            for k, (eng, fn) in enumerate(stages):
                after = {}
                if i == 0:
                    after.update(start)
                if k == 0:
                    if i >= guard:
                        after[last_eng] = max(after.get(last_eng, 0), last_marks[i - guard])
                    elif i == 1:
                        after[stages[0][0]] = max(after.get(stages[0][0], 0), s0_mark)
                else:
                    pe, pv = prev
                    if pe == eng:
                        after[eng] = max(after.get(eng, 0), pv)
                    else:
                        after[pe] = max(after.get(pe, 0), pv)
                self.op(eng, lambda e, fn=fn, i=i: fn(e, i), after=after)
                prev = (eng, self.val[eng])
                touched[eng] = self.val[eng]
                if k == 0:
                    s0_mark = self.val[eng]
            last_marks.append(self.val[last_eng])
        return touched

    def mmg(self, e, ps_ap, pairs):
        last = None
        n = len(pairs)
        for i, (l, r) in enumerate(pairs):
            last = e.matmul(ps_ap, l, r, start=(i == 0), stop=(i == n - 1))
        return last

    def end(self):
        vals = self.now()
        for eng, lst in self.bodies.items():
            if not lst:
                continue

            def mega(e, lst=lst):
                for f in lst:
                    f(e)
            getattr(self._blk, eng)(mega)

        def fin(e):
            for en in ENGS:
                if vals[en]:
                    e.wait_ge(self.sems[en], vals[en])
        self._blk.sync(fin)
        while self.sects:
            self.pop()
        self.stack.close()
        return self.nc


def mmax(*ds):
    out = {}
    for d_ in ds:
        if not d_:
            continue
        for k, v in d_.items():
            out[k] = max(out.get(k, 0), v)
    return out



def emit_consts(b):
    cst = {}
    t = b.sbuf("c_eps", [128, 1], F32)
    b.op("gpsimd", lambda e: e.memset(t[:, :], EPS))
    cst["eps"] = t
    t2 = b.sbuf("c_eps2", [128, 1], F32)
    b.op("gpsimd", lambda e: e.memset(t2[:, :], EPS * EPS))
    cst["eps2"] = t2
    ones = b.sbuf("c_ones", [128, 128], BF16)
    b.op("gpsimd", lambda e: e.memset(ones[:, :], 1.0))
    cst["ones"] = ones
    return cst


def load_w(b, name, dram_ap, ic, oc, dtype=BF16):
    w = b.sbuf(name, [128, ic, oc, 128], dtype)
    b.op("gpsimd", lambda e: e.dma_start(out=w[:, :, :, :], in_=dram_ap[:, :, :, :]),
         dma=True, extra={"sdma": b.val["sdma"], "sync": b.val["sync"]})
    return w


def load_cols(b, name, dram_ap, n, dtype=F32):
    t = b.sbuf(name, [128, n], dtype)
    b.op("gpsimd", lambda e: e.dma_start(out=t[:, :], in_=dram_ap[:, :]), dma=True,
         extra={"sdma": b.val["sdma"], "sync": b.val["sync"]})
    return t


class LNScratch:
    def __init__(self, b, n, cst):
        self.b = b
        self.cst = cst
        self.nn = n
        self.xb = b.sbuf("ln_xb", [128, NCHUNK, n], BF16)
        self.m = b.sbuf("ln_m", [128, n], F32)
        self.A = b.sbuf("ln_A", [128, n], F32)
        self.t1 = b.sbuf("ln_t1", [128, n], F32)
        self.t2 = None
        self.A2 = None
        self.psS = b.psum("ln_psS", [128, n], F32)
        self.psQ = b.psum("ln_psQ", [128, n], F32)

    def mkcomp(self, n):
        if self.A2 is None:
            self.t2 = self.b.sbuf("ln_t2", [128, self.nn], F32)
            self.A2 = self.b.sbuf("ln_A2", [128, self.nn], F32)

    def stats(self, x_chunks, n, nch=NCHUNK, comp=False, first=None, x_all=None):
        b, cst = self.b, self.cst
        if comp:
            self.mkcomp(n)

        if x_all is not None:
            b.op("scalar", lambda e: e.activation(
                self.xb[:, 0:nch, :n], x_all, AF.Copy, bias=0.0, scale=1.0), extra=first)
        else:
            def copies(e):
                last = None
                for c in range(nch):
                    last = e.activation(self.xb[:, c, :n], x_chunks(c), AF.Copy, bias=0.0, scale=1.0)
                return last
            b.op("scalar", copies, extra=first)
        b.mm(self.psS[:, :n], [(cst["ones"][:, :], self.xb[:, c, :n]) for c in range(nch)])
        psS_mark = b.val["tensor"]
        b.op("scalar", lambda e: e.activation(
            self.xb[:, 0:nch, :n], self.xb[:, 0:nch, :n], AF.Square, bias=0.0, scale=1.0),
            after={"scalar": b.val["scalar"], "tensor": psS_mark})
        b.mm(self.psQ[:, :n], [(cst["ones"][:, :], self.xb[:, c, :n]) for c in range(nch)])
        inv = 1.0 / (128 * nch)
        b.op("vector", lambda e: e.tensor_scalar(self.m[:, :n], self.psS[:, :n], inv, None, ALU.mult))
        b.op("vector", lambda e: e.tensor_mul(self.t1[:, :n], self.m[:, :n], self.m[:, :n]))
        b.op("vector", lambda e: e.scalar_tensor_tensor(
            self.t1[:, :n], self.psQ[:, :n], inv, self.t1[:, :n], ALU.mult, ALU.subtract))
        if comp:
            b.op("scalar", lambda e: e.activation(
                self.t2[:, :n], self.t1[:, :n], AF.Sqrt, bias=cst["eps2"][:, 0:1],
                scale=1.0 + EPS))
            b.op("vector", lambda e: e.reciprocal(self.A2[:, :n], self.t2[:, :n]))
        b.op("scalar", lambda e: e.activation(
            self.t1[:, :n], self.t1[:, :n], AF.Sqrt, bias=cst["eps"][:, 0:1], scale=1.0))
        b.op("vector", lambda e: e.reciprocal(self.A[:, :n], self.t1[:, :n]))



def emit_eig_tables(b, cst, ld_col, fr_col):
    eg = {}
    mag = b.sbuf("eg_mag", [64, 1], F32)
    b.op("scalar", lambda e: e.activation(mag[:, :], ld_col[:, :], AF.Sigmoid, bias=0.0, scale=1.0))
    s2 = b.sbuf("eg_s2", [64, 1], F32)
    b.op("scalar", lambda e: e.activation(s2[:, :], fr_col[:, :], AF.Sin, bias=0.0, scale=0.5))
    u = b.sbuf("eg_u", [64, 1], F32)
    b.op("vector", lambda e: e.tensor_scalar(u[:, :], fr_col[:, :], -0.5, PI2_HI, ALU.mult, ALU.add))
    b.op("vector", lambda e: e.tensor_scalar(u[:, :], u[:, :], PI2_LO, None, ALU.add))
    c2 = b.sbuf("eg_c2", [64, 1], F32)
    b.op("scalar", lambda e: e.activation(c2[:, :], u[:, :], AF.Sin, bias=0.0, scale=1.0))
    sth = b.sbuf("eg_sth", [64, 1], F32)
    b.op("vector", lambda e: e.tensor_mul(sth[:, :], s2[:, :], c2[:, :]))
    b.op("vector", lambda e: e.tensor_scalar(sth[:, :], sth[:, :], 2.0, None, ALU.mult))
    cth = b.sbuf("eg_cth", [64, 1], F32)
    b.op("vector", lambda e: e.tensor_mul(cth[:, :], s2[:, :], s2[:, :]))
    b.op("vector", lambda e: e.tensor_scalar(cth[:, :], cth[:, :], -2.0, 1.0, ALU.mult, ALU.add))
    a = b.sbuf("eg_a", [64, 1], F32)
    b.op("vector", lambda e: e.tensor_mul(a[:, :], cth[:, :], mag[:, :]))
    bb = b.sbuf("eg_b", [64, 1], F32)
    b.op("vector", lambda e: e.tensor_mul(bb[:, :], sth[:, :], mag[:, :]))
    rmag = b.sbuf("eg_rmag", [64, 1], F32)
    b.op("vector", lambda e: e.reciprocal(rmag[:, :], mag[:, :]))
    ia = b.sbuf("eg_ia", [64, 1], F32)
    b.op("vector", lambda e: e.tensor_mul(ia[:, :], cth[:, :], rmag[:, :]))
    ib = b.sbuf("eg_ib", [64, 1], F32)
    b.op("vector", lambda e: e.tensor_scalar(ib[:, :], sth[:, :], -1.0, None, ALU.mult))
    b.op("vector", lambda e: e.tensor_mul(ib[:, :], ib[:, :], rmag[:, :]))

    W17r = b.sbuf("eg_W17r", [64, 17], F32)
    W17i = b.sbuf("eg_W17i", [64, 17], F32)
    U16r = b.sbuf("eg_U16r", [64, 16], F32)
    U16i = b.sbuf("eg_U16i", [64, 16], F32)
    tmp = b.sbuf("eg_tmp", [64, 1], F32)
    for (Wr, Wi, n, ar, br) in ((W17r, W17i, 17, a, bb), (U16r, U16i, 16, ia, ib)):
        b.op("gpsimd", lambda e, Wr=Wr: e.memset(Wr[:, 0:1], 1.0))
        b.op("gpsimd", lambda e, Wi=Wi: e.memset(Wi[:, 0:1], 0.0))
        for j in range(1, n):
            b.op("vector", lambda e, j=j, Wr=Wr, ar=ar: e.tensor_scalar(
                Wr[:, j:j + 1], Wr[:, j - 1:j], ar[:, 0:1], None, ALU.mult))
            b.op("vector", lambda e, j=j, Wi=Wi, br=br: e.tensor_scalar(
                tmp[:, :], Wi[:, j - 1:j], br[:, 0:1], None, ALU.mult))
            b.op("vector", lambda e, j=j, Wr=Wr: e.tensor_sub(
                Wr[:, j:j + 1], Wr[:, j:j + 1], tmp[:, :]))
            b.op("vector", lambda e, j=j, Wi=Wi, ar=ar: e.tensor_scalar(
                Wi[:, j:j + 1], Wi[:, j - 1:j], ar[:, 0:1], None, ALU.mult))
            b.op("vector", lambda e, j=j, Wr=Wr, Wi=Wi, br=br: e.tensor_scalar(
                tmp[:, :], Wr[:, j - 1:j], br[:, 0:1], None, ALU.mult))
            b.op("vector", lambda e, j=j, Wi=Wi: e.tensor_add(
                Wi[:, j:j + 1], Wi[:, j:j + 1], tmp[:, :]))
    mask = b.sbuf("eg_mask", [64, 2 * TT], F32)
    b.op("gpsimd", lambda e: e.memset(mask[:, :], 1.0))
    b.op("gpsimd", lambda e: e.memset(mask[:, 0:2 * TT:SEG], 0.0))
    eg.update(W17r=W17r, W17i=W17i, U16r=U16r, U16i=U16i, mask=mask)
    return eg


def emit_eigenstate(b, eg, bt, cout, n, carry_ap, scratch):
    W17r, W17i = eg["W17r"], eg["W17i"]
    U16r, U16i = eg["U16r"], eg["U16i"]
    y, p, z, G = scratch["y"], scratch["p"], scratch["z"], scratch["G"]
    ta, tb = scratch["ta"], scratch["tb"]
    nseg = n // SEG
    nch = n // CH

    def b16(t, rev=False):
        ap = t[:, 16:0:-1] if rev else t[:, 0:16]
        return ap.unsqueeze(1).broadcast_to([64, 2 * nseg, 16])

    def b16h(t, rev=False):
        ap = t[:, 16:0:-1] if rev else t[:, 0:16]
        return ap.unsqueeze(1).broadcast_to([64, nseg, 16])

    def r3(ap):
        return ap.rearrange("p (g f) -> p g f", f=16)

    tc, td = scratch["tc"], scratch["td"]
    b.op("vector", lambda e: e.tensor_mul(r3(y[:, 0:n]), r3(bt[:, 0:n]), b16h(W17r)))
    b.op("vector", lambda e: e.tensor_mul(r3(ta[:, 0:n]), r3(bt[:, n:2 * n]), b16h(W17i)))
    b.op("vector", lambda e: e.tensor_sub(y[:, 0:n], y[:, 0:n], ta[:, 0:n]))
    b.op("gpsimd", lambda e: e.tensor_mul(r3(y[:, n:2 * n]), r3(bt[:, n:2 * n]), b16h(W17r)))
    b.op("gpsimd", lambda e: e.tensor_mul(r3(tc[:, 0:n]), r3(bt[:, 0:n]), b16h(W17i)))
    b.op("gpsimd", lambda e: e.tensor_add(y[:, n:2 * n], y[:, n:2 * n], tc[:, 0:n]))
    b.op("vector", lambda e: e.tensor_tensor_scan(
        p[:, 0:n], eg["mask"][:, 0:n], y[:, n - 1::-1], 0.0, ALU.mult, ALU.add))
    b.op("vector", lambda e: e.tensor_tensor_scan(
        p[:, n:2 * n], eg["mask"][:, 0:n], y[:, 2 * n - 1:n - 1:-1], 0.0, ALU.mult, ALU.add))
    sufr = p[:, n - 1::-1]
    sufi = p[:, 2 * n - 1:n - 1:-1]
    b.op("vector", lambda e: e.tensor_mul(r3(z[:, 0:n]), r3(sufr), b16h(U16r)))
    b.op("vector", lambda e: e.tensor_mul(r3(ta[:, 0:n]), r3(sufi), b16h(U16i)))
    b.op("vector", lambda e: e.tensor_sub(z[:, 0:n], z[:, 0:n], ta[:, 0:n]))
    b.op("gpsimd", lambda e: e.tensor_mul(r3(z[:, n:2 * n]), r3(sufi), b16h(U16r)))
    b.op("gpsimd", lambda e: e.tensor_mul(r3(tc[:, 0:n]), r3(sufr), b16h(U16i)))
    b.op("gpsimd", lambda e: e.tensor_add(z[:, n:2 * n], z[:, n:2 * n], tc[:, 0:n]))
    a16r = W17r[:, 16:17]
    a16i = W17i[:, 16:17]

    def Gs(m, imag):
        base = nseg if imag else 0
        return G[:, base + m::4][:, 0:nch]

    def Ts(m, imag):
        base = n if imag else 0
        return z[:, base + m * 16::CH][:, 0:nch]
    b.op("gpsimd", lambda e: e.memset(Gs(3, False), 0.0))
    b.op("gpsimd", lambda e: e.memset(Gs(3, True), 0.0))
    b.op("vector", lambda e: e.tensor_copy(Gs(2, False), Ts(3, False)))
    b.op("vector", lambda e: e.tensor_copy(Gs(2, True), Ts(3, True)))
    for m in (1, 0):
        b.op("vector", lambda e, m=m: e.scalar_tensor_tensor(
            Gs(m, False), Gs(m + 1, False), a16r[:, 0:1], Ts(m + 1, False), ALU.mult, ALU.add))
        b.op("vector", lambda e, m=m: e.tensor_scalar(
            ta[:, 0:nch], Gs(m + 1, True), a16i[:, 0:1], None, ALU.mult))
        b.op("vector", lambda e, m=m: e.tensor_sub(Gs(m, False), Gs(m, False), ta[:, 0:nch]))
        b.op("vector", lambda e, m=m: e.scalar_tensor_tensor(
            Gs(m, True), Gs(m + 1, True), a16r[:, 0:1], Ts(m + 1, True), ALU.mult, ALU.add))
        b.op("vector", lambda e, m=m: e.tensor_scalar(
            ta[:, 0:nch], Gs(m + 1, False), a16i[:, 0:1], None, ALU.mult))
        b.op("vector", lambda e, m=m: e.tensor_add(Gs(m, True), Gs(m, True), ta[:, 0:nch]))
    GrB = G[:, 0:nseg].unsqueeze(2).broadcast_to([64, nseg, 16])
    GiB = G[:, nseg:2 * nseg].unsqueeze(2).broadcast_to([64, nseg, 16])
    b.op("vector", lambda e: e.tensor_mul(r3(ta[:, 0:n]), b16h(W17r, True), GrB))
    b.op("vector", lambda e: e.tensor_mul(r3(tb[:, 0:n]), b16h(W17i, True), GiB))
    b.op("vector", lambda e: e.tensor_sub(ta[:, 0:n], ta[:, 0:n], tb[:, 0:n]))
    b.op("vector", lambda e: e.tensor_add(cout[:, 0:n], z[:, 0:n], ta[:, 0:n]))
    b.op("gpsimd", lambda e: e.tensor_mul(r3(tc[:, 0:n]), b16h(W17r, True), GiB))
    b.op("gpsimd", lambda e: e.tensor_mul(r3(td[:, 0:n]), b16h(W17i, True), GrB))
    b.op("gpsimd", lambda e: e.tensor_add(tc[:, 0:n], tc[:, 0:n], td[:, 0:n]))
    b.op("gpsimd", lambda e: e.tensor_add(cout[:, n:2 * n], z[:, n:2 * n], tc[:, 0:n]))
    if carry_ap is not None:
        b.op("vector", lambda e: e.tensor_add(cout[:, 0:1], cout[:, 0:1], carry_ap[:, 0:1]))
        b.op("gpsimd", lambda e: e.tensor_add(cout[:, n:n + 1], cout[:, n:n + 1], carry_ap[:, 1:2]))
    if nch > 1:
        b.op("vector", lambda e: e.tensor_add(
            cout[:, CH::CH][:, 0:nch - 1], cout[:, CH::CH][:, 0:nch - 1],
            bt[:, CH - 1::CH][:, 0:nch - 1]))
        b.op("gpsimd", lambda e: e.tensor_add(
            cout[:, n + CH::CH][:, 0:nch - 1], cout[:, n + CH::CH][:, 0:nch - 1],
            bt[:, n + CH - 1::CH][:, 0:nch - 1]))


def mk_eig_scratch(b, n):
    y = b.sbuf("eg_y", [64, 2 * n], F32)
    return {
        "y": y,
        "p": b.sbuf("eg_p", [64, 2 * n], F32),
        "z": y,
        "G": b.sbuf("eg_G", [64, 2 * (n // SEG)], F32),
        "ta": b.sbuf("eg_ta", [64, n], F32),
        "tb": b.sbuf("eg_tb", [64, n], F32),
        "tc": b.sbuf("eg_tc", [64, n], F32),
        "td": b.sbuf("eg_td", [64, n], F32),
    }



def emit_l1a(b, cst, d):
    b.push()
    w_gate = load_w(b, "w_gate_s", d["w_gate"], 8, 8)
    w_in_r = b.sbuf("w_in_r_s", [128, 8, 64], BF16)
    b.op("gpsimd", lambda e: e.dma_start(out=w_in_r[:, :, :], in_=d["w_in_r"][:, :, :]), dma=True,
         extra={"sdma": b.val["sdma"], "sync": b.val["sync"]})
    w_in_i = b.sbuf("w_in_i_s", [128, 8, 64], BF16)
    b.op("gpsimd", lambda e: e.dma_start(out=w_in_i[:, :, :], in_=d["w_in_i"][:, :, :]), dma=True)
    wor = b.sbuf("wor_s", [64, 8, 128], BF16)
    b.op("gpsimd", lambda e: e.dma_start(out=wor[:, :, :], in_=d["w_out_r"][:, :, :]), dma=True)
    woi = b.sbuf("woi_s", [64, 8, 128], BF16)
    b.op("gpsimd", lambda e: e.dma_start(out=woi[:, :, :], in_=d["w_out_i"][:, :, :]), dma=True)
    b_gate = load_cols(b, "b_gate_s", d["b_gate"], 8)
    ln1g = load_cols(b, "ln1g_s", d["ln1g"], 8)
    ln1b = load_cols(b, "ln1b_s", d["ln1b"], 8)
    ld = b.sbuf("ld_s", [64, 1], F32)
    b.op("gpsimd", lambda e: e.dma_start(out=ld[:, :], in_=d["ldecay"][:, :]), dma=True)
    fr = b.sbuf("fr_s", [64, 1], F32)
    b.op("gpsimd", lambda e: e.dma_start(out=fr[:, :], in_=d["freq"][:, :]), dma=True)
    flag = load_cols(b, "flag_s", d["flag"], 1)
    wcp = b.sbuf("wcp", [64, 64], F32)
    b.op("gpsimd", lambda e: e.memset(wcp[:, :], 0.0),
         extra={"sdma": b.val["sdma"], "sync": b.val["sync"]})
    for h in range(4):
        o = 16 * h
        b.op("gpsimd", lambda e, o=o, h=h: e.dma_start(
            out=wcp[o:o + 16, o:o + 16], in_=d["coupT"][h, :, :]), dma=True)
    eg = emit_eig_tables(b, cst, ld, fr)
    ln = LNScratch(b, TT, cst)
    egs = mk_eig_scratch(b, TT)

    xt = [b.sbuf("xt0", [128, NCHUNK, TT], F32), b.sbuf("xt1", [128, NCHUNK, TT], F32),
          b.sbuf("xt2", [128, NCHUNK, TT], F32)]
    y1 = [b.sbuf("y10", [128, NCHUNK, TT], F32), b.sbuf("y11", [128, NCHUNK, TT], F32),
          b.sbuf("y12", [128, NCHUNK, TT], F32)]
    xnb = [b.sbuf("xnb0", [128, NCHUNK, TT], BF16), b.sbuf("xnb1", [128, NCHUNK, TT], BF16)]
    bt = b.sbuf("bt", [64, 2 * TT], F32)
    ct = b.sbuf("ct", [64, 2 * TT], F32)
    cpb = [b.sbuf("cpb0", [64, 2 * TT], BF16), b.sbuf("cpb1", [64, 2 * TT], BF16)]
    g8 = b.sbuf("g8", [128, NCHUNK, TT], F32)
    ho = b.sbuf("ho", [128, 2, TT], F32)
    carryA = b.sbuf("carryA", [64, 2], F32)
    carryB = b.sbuf("carryB", [64, 2], F32)
    ps1 = b.psum_pair("ps1", [128, TT], F32)
    ps2 = b.psum_pair("ps2", [128, TT], F32)
    psBr = b.psum("psBr", [64, TT], F32)
    psBi = b.psum("psBi", [64, TT], F32)
    b.op("gpsimd", lambda e: e.memset(carryA[:, :], 0.0))
    ldm = {}
    stm = {}
    sd0 = b.val["sdma"]

    def load(t):
        if 0 <= t < NTILE:
            b.op("sync", lambda e, t=t: e.dma_start(
                out=xt[t % 3][:, :, :], in_=d["xs"][:, :, t * TT:(t + 1) * TT]), dma=True,
                extra={"sdma": sd0})
            ldm[t] = b.val["sync"]

    mB = ln.m[:, :].unsqueeze(1).broadcast_to([128, NCHUNK, TT])
    A2B = None
    AB = ln.A[:, :].unsqueeze(1).broadcast_to([128, NCHUNK, TT])
    gB = ln1g[:, :].unsqueeze(2).broadcast_to([128, NCHUNK, TT])
    bB = ln1b[:, :].unsqueeze(2).broadcast_to([128, NCHUNK, TT])

    def pre(t):
        nonlocal A2B
        x = xt[t % 3]
        yy = y1[t % 3]
        xb = xnb[t % 2]
        cp = cpb[t % 2]
        ln.stats(None, TT, comp=True, first={"sync": ldm[t]}, x_all=x[:, :, :])
        if A2B is None:
            A2B = ln.A2[:, :].unsqueeze(1).broadcast_to([128, NCHUNK, TT])
        b.op("vector", lambda e, x=x, yy=yy: e.tensor_sub(yy[:, :, :], x[:, :, :], mB),
             extra=({"sdma": stm[t - 3]} if t - 3 in stm else None))
        b.op("vector", lambda e, yy=yy, xb=xb: e.tensor_mul(xb[:, :, :], yy[:, :, :], A2B))
        b.op("vector", lambda e, xb=xb: e.tensor_mul(xb[:, :, :], xb[:, :, :], gB))
        b.op("vector", lambda e, xb=xb: e.tensor_add(xb[:, :, :], xb[:, :, :], bB))
        b.op("gpsimd", lambda e, yy=yy: e.tensor_mul(yy[:, :, :], yy[:, :, :], AB))
        b.mm(psBr[:, :], [(w_in_r[:, k, :], xb[:, k, :]) for k in range(8)])
        b.op("vector", lambda e: e.tensor_copy(bt[:, 0:TT], psBr[:, :]))
        b.mm(psBi[:, :], [(w_in_i[:, k, :], xb[:, k, :]) for k in range(8)])
        b.op("vector", lambda e: e.tensor_copy(bt[:, TT:2 * TT], psBi[:, :]))
        ccol = carryA if t != 1 else carryB
        emit_eigenstate(b, eg, bt, ct, TT, ccol[:, 0:2], egs)
        b.op("vector", lambda e: e.tensor_copy(carryA[:, 0:1], bt[:, TT - 1:TT]))
        b.op("vector", lambda e: e.tensor_copy(carryA[:, 1:2], bt[:, 2 * TT - 1:2 * TT]))
        if t == 0:
            b.op("vector", lambda e: e.tensor_mul(
                carryB[:, :], carryA[:, :], flag[0:64, 0:1].broadcast_to([64, 2])))
        b.mm(psBr[:, :], [(wcp[:, :], ct[:, 0:TT])])
        b.op("scalar", lambda e, cp=cp: e.activation(cp[:, 0:TT], psBr[:, :], AF.Copy, bias=0.0, scale=1.0))
        b.mm(psBi[:, :], [(wcp[:, :], ct[:, TT:2 * TT])])
        b.op("scalar", lambda e, cp=cp: e.activation(cp[:, TT:2 * TT], psBi[:, :], AF.Copy, bias=0.0, scale=1.0))

    def pipe(t, start):
        xb = xnb[t % 2]
        cp = cpb[t % 2]
        yy = y1[t % 3]
        m1 = b.pipe(8, [
            ("tensor", lambda e, i, xb=xb: b.mmg(e, ps1[i % 2][:, :], [(w_gate[:, k, i, :], xb[:, k, :]) for k in range(8)])),
            ("scalar", lambda e, i: e.activation(
                g8[:, i, :], ps1[i % 2][:, :], AF.Sigmoid, bias=b_gate[:, i:i + 1], scale=1.0)),
        ], start=start)
        b.pipe(8, [
            ("tensor", lambda e, i, cp=cp: b.mmg(e, ps2[i % 2][:, :], [(wor[:, i, :], cp[:, 0:TT]), (woi[:, i, :], cp[:, TT:2 * TT])])),
            ("scalar", lambda e, i: e.activation(
                ho[:, i % 2, :], ps2[i % 2][:, :], AF.Copy, bias=0.0, scale=1.0)),
            ("gpsimd", lambda e, i: e.tensor_mul(ho[:, i % 2, :], ho[:, i % 2, :], g8[:, i, :])),
            ("gpsimd", lambda e, i, yy=yy: e.tensor_add(yy[:, i, :], yy[:, i, :], ho[:, i % 2, :])),
        ], start=mmax(start, m1))

    def post(t):
        yy = y1[t % 3]
        b.op("scalar", lambda e, t=t, yy=yy: e.dma_start(
            out=d["x1"][:, :, t * TT:(t + 1) * TT], in_=yy[:, :, :]), dma=True)
        stm[t] = b.val["sdma"]
        load(t + 3)

    load(0)
    load(1)
    load(2)
    preSnap = {}
    pipeEnd = {}
    b.scope_push()
    pre(0)
    b.scope_pop()
    preSnap[0] = b.now()
    for t in range(NTILE):
        pipe(t, mmax(preSnap[t], pipeEnd.get(t - 1),
                     {"sdma": stm[t - 2]} if t - 2 in stm else None))
        pipeEnd[t] = b.now()
        if t + 1 < NTILE:
            b.scope_push(base=preSnap[t])
            pre(t + 1)
            b.scope_pop()
            preSnap[t + 1] = b.now()
        if t - 1 >= 0:
            b.scope_push()
            post(t - 1)
            b.scope_pop()
    b.scope_push()
    post(NTILE - 1)
    b.scope_pop()
    b.pop()


def emit_l1b(b, cst, d):
    N = 256
    NT = TW // N
    b.push()
    w1 = load_w(b, "w1_s", d["w_mlp1"], 8, 32)
    w2 = load_w(b, "w2_s", d["w_mlp2"], 32, 8)
    b1 = load_cols(b, "b1_s", d["b_mlp1"], 32)
    b2 = load_cols(b, "b2_s", d["b_mlp2"], 8)
    ln2g = load_cols(b, "ln2g_s", d["ln2g"], 8)
    ln2b = load_cols(b, "ln2b_s", d["ln2b"], 8)
    ln = LNScratch(b, N, cst)
    x1t = [b.sbuf("x1t0", [128, NCHUNK, N], F32), b.sbuf("x1t1", [128, NCHUNK, N], F32)]
    xst = [b.sbuf("xst0", [128, NCHUNK, N], F32), b.sbuf("xst1", [128, NCHUNK, N], F32)]
    xn2b = [b.sbuf("xn2b0", [128, NCHUNK, N], BF16), b.sbuf("xn2b1", [128, NCHUNK, N], BF16)]
    h1b = b.sbuf("h1b", [128, 32, N], BF16)
    ao = [b.sbuf("ao0", [128, NCHUNK, N], F32), b.sbuf("ao1", [128, NCHUNK, N], F32)]
    ps1 = b.psum_pair("ps1", [128, N], F32)
    ldm = {}
    stm = {}
    sd0 = b.val["sdma"]

    def loadx1(t):
        if 0 <= t < NT:
            b.op("sync", lambda e, t=t: e.dma_start(
                out=x1t[t % 2][:, :, :], in_=d["x1"][:, :, t * N:(t + 1) * N]), dma=True,
                extra={"sdma": sd0})
            ldm[t] = b.val["sync"]

    def loadxs(t):
        if 0 <= t < NT:
            b.op("sync", lambda e, t=t: e.dma_start(
                out=xst[t % 2][:, :, :], in_=d["xs"][:, :, t * N:(t + 1) * N]), dma=True,
                extra={"sdma": sd0})

    def load(t):
        loadx1(t)
        loadxs(t)

    mB = ln.m[:, :N].unsqueeze(1).broadcast_to([128, NCHUNK, N])
    AB = ln.A[:, :N].unsqueeze(1).broadcast_to([128, NCHUNK, N])
    gB = ln2g[:, :].unsqueeze(2).broadcast_to([128, NCHUNK, N])
    bB = ln2b[:, :].unsqueeze(2).broadcast_to([128, NCHUNK, N])

    def pre(t):
        x1 = x1t[t % 2]
        xb = xn2b[t % 2]
        ln.stats(None, N, x_all=x1[:, :, :], first={"sync": ldm[t]})
        b.op("vector", lambda e, x1=x1, xb=xb: e.tensor_sub(xb[:, :, :], x1[:, :, :], mB))
        b.op("vector", lambda e, xb=xb: e.tensor_mul(xb[:, :, :], xb[:, :, :], AB))
        b.op("vector", lambda e, xb=xb: e.tensor_mul(xb[:, :, :], xb[:, :, :], gB))
        b.op("vector", lambda e, xb=xb: e.tensor_add(xb[:, :, :], xb[:, :, :], bB))

    def pipe(t, start):
        x1 = x1t[t % 2]
        xb = xn2b[t % 2]
        a = ao[t % 2]
        m1 = b.pipe(32, [
            ("tensor", lambda e, i, xb=xb: b.mmg(e, ps1[i % 2][:, 0:N], [(w1[:, k, i, :], xb[:, k, :]) for k in range(8)])),
            ("scalar", lambda e, i: e.activation(
                h1b[:, i, :], ps1[i % 2][:, 0:N], AF.Silu, bias=b1[:, i:i + 1], scale=1.0)),
        ], start=start)
        b.pipe(8, [
            ("tensor", lambda e, i: b.mmg(e, ps1[i % 2][:, 0:N], [(w2[:, k, i, :], h1b[:, k, :]) for k in range(32)])),
            ("scalar", lambda e, i, a=a: e.activation(
                a[:, i, :], ps1[i % 2][:, 0:N], AF.Identity, bias=b2[:, i:i + 1], scale=1.0)),
            ("gpsimd", lambda e, i, a=a, x1=x1: e.tensor_add(a[:, i, :], a[:, i, :], x1[:, i, :])),
        ], start=mmax(start, m1))

    def post(t):
        xs = xst[t % 2]
        a = ao[t % 2]
        ln.stats(lambda c, a=a: a[:, c, :], N, x_all=a[:, :, :])
        b.op("vector", lambda e, a=a: e.tensor_sub(a[:, :, :], a[:, :, :], mB))
        b.op("vector", lambda e, a=a: e.tensor_mul(a[:, :, :], a[:, :, :], AB))
        b.op("vector", lambda e, a=a, xs=xs: e.tensor_add(a[:, :, :], a[:, :, :], xs[:, :, :]))
        b.op("scalar", lambda e, t=t, a=a: e.dma_start(
            out=d["s1"][:, :, t * N:(t + 1) * N], in_=a[:, :, :]), dma=True)
        stm[t] = b.val["sdma"]
        loadxs(t + 2)

    load(0)
    load(1)
    preSnap = {}
    pipeEnd = {}
    b.scope_push()
    pre(0)
    b.scope_pop()
    preSnap[0] = b.now()
    for t in range(NT):
        pipe(t, mmax(preSnap[t], pipeEnd.get(t - 1),
                     {"sdma": stm[t - 2]} if t - 2 in stm else None))
        pipeEnd[t] = b.now()
        if t + 1 < NT:
            b.scope_push(base=preSnap[t])
            pre(t + 1)
            b.scope_pop()
            preSnap[t + 1] = b.now()
        loadx1(t + 2)
        if t - 1 >= 0:
            b.scope_push()
            post(t - 1)
            b.scope_pop()
    b.scope_push()
    post(NT - 1)
    b.scope_pop()
    b.pop()


def emit_l2(b, cst, d):
    emit_l2a(b, cst, d)
    emit_l2b(b, cst, d)


def emit_l2a(b, cst, d):
    SC = 1.0 / 127.0
    b.push()
    wg = load_w(b, "wg_s", d["wg"], 8, 24)
    wu = load_w(b, "wu_s", d["wu"], 8, 24)
    ln = LNScratch(b, TT, cst)
    xt = [b.sbuf("xt0", [128, NCHUNK, TT], F32), b.sbuf("xt1", [128, NCHUNK, TT], F32)]
    y2b = [b.sbuf("y2b0", [128, NCHUNK, TT], BF16), b.sbuf("y2b1", [128, NCHUNK, TT], BF16)]
    sg_t = b.sbuf("sg_t", [128, 2, TT], F32)
    ht = b.sbuf("ht", [128, 24, TT], BF16)
    ps1 = b.psum_pair("ps1", [128, TT], F32)
    ps2 = b.psum_pair("ps2", [128, TT], F32)
    ldm = {}
    stm = {}
    sd0 = b.val["sdma"]

    def load(t):
        if 0 <= t < NTILE:
            b.op("sync", lambda e, t=t: e.dma_start(
                out=xt[t % 2][:, :, :], in_=d["s1"][:, :, t * TT:(t + 1) * TT]), dma=True,
                extra={"sdma": sd0})
            ldm[t] = b.val["sync"]

    mB = ln.m[:, :].unsqueeze(1).broadcast_to([128, NCHUNK, TT])
    AB = ln.A[:, :].unsqueeze(1).broadcast_to([128, NCHUNK, TT])

    def pre(t):
        x = xt[t % 2]
        yb = y2b[t % 2]
        ln.stats(None, TT, x_all=x[:, :, :])
        b.op("vector", lambda e, x=x, yb=yb: e.tensor_sub(yb[:, :, :], x[:, :, :], mB))
        b.op("vector", lambda e, yb=yb: e.tensor_mul(yb[:, :, :], yb[:, :, :], AB))

    def pipe(t, start):
        yb = y2b[t % 2]

        def l2mm(e, i, yb=yb):
            b.mmg(e, ps1[i % 2][:, :], [(wg[:, k, i, :], yb[:, k, :]) for k in range(8)])
            return b.mmg(e, ps2[i % 2][:, :], [(wu[:, k, i, :], yb[:, k, :]) for k in range(8)])
        b.pipe(24, [
            ("tensor", l2mm),
            ("scalar", lambda e, i: e.activation(
                sg_t[:, i % 2, :], ps1[i % 2][:, :], AF.Silu, bias=0.0, scale=SC)),
            ("vector", lambda e, i: e.scalar_tensor_tensor(
                ht[:, i, :], ps2[i % 2][:, :], SC, sg_t[:, i % 2, :], ALU.mult, ALU.mult)),
        ], start=start)

    def post(t):
        b.op("scalar", lambda e, t=t: e.dma_start(
            out=d["hq"][:, :, t * TT:(t + 1) * TT], in_=ht[:, :, :]), dma=True)
        stm[t] = b.val["sdma"]
        load(t + 2)


    load(0)
    load(1)
    preSnap = {}
    pipeEnd = {}
    b.scope_push()
    pre(0)
    b.scope_pop()
    preSnap[0] = b.now()
    pipe(0, preSnap[0])
    pipeEnd[0] = b.now()
    for t in range(NTILE):
        if t + 1 < NTILE:
            b.scope_push()
            pre(t + 1)
            b.scope_pop()
            preSnap[t + 1] = b.now()
        b.scope_push()
        post(t)
        b.scope_pop()
        if t + 1 < NTILE:
            pipe(t + 1, mmax(preSnap[t + 1], pipeEnd[t], {"sdma": stm[t]}))
            pipeEnd[t + 1] = b.now()
    b.pop()


def emit_l2b(b, cst, d):
    SC = 1.0 / 127.0
    b.push()
    wd = load_w(b, "wd_s", d["wd"], 24, 8)
    ln = LNScratch(b, TT, cst)
    xt = [b.sbuf("xt0", [128, NCHUNK, TT], F32), b.sbuf("xt1", [128, NCHUNK, TT], F32)]
    ht = [b.sbuf("ht0", [128, 24, TT], BF16), b.sbuf("ht1", [128, 24, TT], BF16)]
    Ft = [b.sbuf("Ft0", [128, NCHUNK, TT], F32), b.sbuf("Ft1", [128, NCHUNK, TT], F32)]
    ps1 = b.psum_pair("ps1", [128, TT], F32)
    ldm = {}
    stm = {}
    sd0 = b.val["sdma"]

    def loadxt(t):
        if 0 <= t < NTILE:
            b.op("sync", lambda e, t=t: e.dma_start(
                out=xt[t % 2][:, :, :], in_=d["s1"][:, :, t * TT:(t + 1) * TT]), dma=True,
                extra={"sdma": sd0})

    def loadht(t):
        if 0 <= t < NTILE:
            b.op("sync", lambda e, t=t: e.dma_start(
                out=ht[t % 2][:, :, :], in_=d["hq"][:, :, t * TT:(t + 1) * TT]), dma=True,
                extra={"sdma": sd0})
            ldm[t] = b.val["sync"]

    def load(t):
        loadxt(t)
        loadht(t)

    def pipe(t, start):
        h = ht[t % 2]
        f = Ft[t % 2]
        ex = {"sync": ldm[t], "gdma": b.val["gdma"]}
        if t - 2 in stm:
            ex["sdma"] = stm[t - 2]
        b.pipe(8, [
            ("tensor", lambda e, i, h=h: b.mmg(
                e, ps1[i % 2][:, :], [(wd[:, k, i, :], h[:, k, :]) for k in range(24)])),
            ("scalar", lambda e, i, f=f: e.activation(
                f[:, i, :], ps1[i % 2][:, :], AF.Identity, bias=0.0, scale=SC)),
        ], start=dict(start), extra=ex)

    mB = ln.m[:, :].unsqueeze(1).broadcast_to([128, NCHUNK, TT])
    AB = ln.A[:, :].unsqueeze(1).broadcast_to([128, NCHUNK, TT])

    def post(t):
        x = xt[t % 2]
        f = Ft[t % 2]
        ln.stats(None, TT, x_all=f[:, :, :])
        b.op("vector", lambda e, f=f: e.tensor_sub(f[:, :, :], f[:, :, :], mB))
        b.op("vector", lambda e, f=f: e.tensor_mul(f[:, :, :], f[:, :, :], AB))
        b.op("vector", lambda e, f=f, x=x: e.tensor_add(f[:, :, :], f[:, :, :], x[:, :, :]))
        b.op("scalar", lambda e, t=t, f=f: e.dma_start(
            out=d["s2"][:, :, t * TT:(t + 1) * TT], in_=f[:, :, :]), dma=True)
        stm[t] = b.val["sdma"]
        loadxt(t + 2)

    load(0)
    load(1)
    pipeEnd = {}
    postEnd = {}
    pipe(0, b.now())
    pipeEnd[0] = b.now()
    for t in range(NTILE):
        if t + 1 < NTILE:
            pipe(t + 1, mmax(pipeEnd[t], postEnd.get(t - 1)))
            pipeEnd[t + 1] = b.now()
            loadht(t + 2)
        b.scope_push()
        post(t)
        b.scope_pop()
        postEnd[t] = b.now()
    b.pop()


def emit_l3(b, cst, d):
    b.push()
    wk = load_w(b, "wk_s", d["wk"], 8, 8)
    wv = load_w(b, "wv_s", d["wv"], 8, 8)
    wr = load_w(b, "wr_s", d["wr"], 8, 8)
    wo = load_w(b, "wo_s", d["wo"], 8, 8)
    lnw = load_cols(b, "lnw_s", d["lnw"], 8)
    lnb = load_cols(b, "lnb_s", d["lnb"], 8)
    tmk = load_cols(b, "tmk_s", d["tmk"], 8)
    tmv = load_cols(b, "tmv_s", d["tmv"], 8)
    tmr = load_cols(b, "tmr_s", d["tmr"], 8)
    dw = load_cols(b, "dw_s", d["dw"], 8)
    flag = load_cols(b, "flag_s", d["flag"], 1)
    dec = b.sbuf("dec", [128, 8], F32)
    b.op("scalar", lambda e: e.activation(dec[:, :], dw[:, :], AF.Sigmoid, bias=0.0, scale=1.0),
         extra={"sync": b.val["sync"]})
    omk = b.sbuf("omk", [128, 8], F32)
    omv = b.sbuf("omv", [128, 8], F32)
    omr = b.sbuf("omr", [128, 8], F32)
    for om, tm in ((omk, tmk), (omv, tmv), (omr, tmr)):
        b.op("vector", lambda e, om=om, tm=tm: e.tensor_scalar(
            om[:, :], tm[:, :], -1.0, 1.0, ALU.mult, ALU.add))
    ln = LNScratch(b, TT, cst)
    xt = [b.sbuf("xt0", [128, NCHUNK, TT], F32), b.sbuf("xt1", [128, NCHUNK, TT], F32)]
    y3 = b.sbuf("y3", [128, NCHUNK, TT], F32)
    xn = b.sbuf("xn", [128, NCHUNK, TT + 1], F32)
    mixk = b.sbuf("mixk", [128, NCHUNK, TT], BF16)
    mixv = b.sbuf("mixv", [128, NCHUNK, TT], BF16)
    mixr = b.sbuf("mixr", [128, NCHUNK, TT], BF16)
    kf = b.sbuf("kf", [128, 2, TT], F32)
    kv = b.sbuf("kv", [128, 2, TT], F32)
    ssq = b.sbuf("ssq", [128, 2, TT], F32)
    ssb = b.sbuf("ssb", [128, NCHUNK, TT], BF16)
    rf = b.sbuf("rf", [128, NCHUNK, TT], F32)
    scol = b.sbuf("scol", [128, 8], F32)
    xcar = b.sbuf("xcar", [128, 8], F32)
    psK = b.psum_pair("psK", [128, TT], F32)
    psV = b.psum_pair("psV", [128, TT], F32)
    psR = b.psum_pair("psR", [128, TT], F32)
    b.op("gpsimd", lambda e: e.memset(scol[:, :], 0.0),
         extra={"sdma": b.val["sdma"], "sync": b.val["sync"]})
    b.op("gpsimd", lambda e: e.memset(xcar[:, :], 0.0))
    ldm = {}
    stm = {}

    sd0 = b.val["sdma"]

    def load(t):
        if 0 <= t < NTILE:
            b.op("sync", lambda e, t=t: e.dma_start(
                out=xt[t % 2][:, :, :], in_=d["s2"][:, :, t * TT:(t + 1) * TT]), dma=True,
                extra={"sdma": sd0})
            ldm[t] = b.val["sync"]
    mB = ln.m[:, :].unsqueeze(1).broadcast_to([128, NCHUNK, TT])
    AB = ln.A[:, :].unsqueeze(1).broadcast_to([128, NCHUNK, TT])
    gB = lnw[:, :].unsqueeze(2).broadcast_to([128, NCHUNK, TT])
    bB = lnb[:, :].unsqueeze(2).broadcast_to([128, NCHUNK, TT])
    A2B = None
    load(0)
    for t in range(NTILE):
        load(t + 1)
        x = xt[t % 2]
        ln.stats(None, TT, comp=True, first={"sync": ldm[t]}, x_all=x[:, :, :])
        if A2B is None:
            A2B = ln.A2[:, :].unsqueeze(1).broadcast_to([128, NCHUNK, TT])
        b.op("vector", lambda e, x=x: e.tensor_sub(y3[:, :, :], x[:, :, :], mB),
             extra=({"sdma": stm[t - 1]} if t - 1 in stm else None))
        b.op("vector", lambda e: e.tensor_mul(xn[:, :, 1:TT + 1], y3[:, :, :], A2B))
        b.op("vector", lambda e: e.tensor_mul(xn[:, :, 1:TT + 1], xn[:, :, 1:TT + 1], gB))
        b.op("vector", lambda e: e.tensor_add(xn[:, :, 1:TT + 1], xn[:, :, 1:TT + 1], bB))
        b.op("gpsimd", lambda e: e.tensor_mul(y3[:, :, :], y3[:, :, :], AB))
        b.op("vector", lambda e: e.tensor_copy(xn[:, :, 0:1], xcar[:, :].unsqueeze(2)))
        b.op("vector", lambda e: e.tensor_copy(xcar[:, :], xn[:, :, TT:TT + 1].squeeze(2)))
        if t == 0:
            b.op("vector", lambda e: e.tensor_mul(
                xcar[:, :], xcar[:, :], flag[:, 0:1].broadcast_to([128, 8])))
        for mix, tm, om, meng in ((mixk, tmk, omk, "vector"), (mixv, tmv, omv, "gpsimd"),
                                  (mixr, tmr, omr, "vector")):
            tmB = tm[:, :].unsqueeze(2).broadcast_to([128, NCHUNK, TT])
            omB = om[:, :].unsqueeze(2).broadcast_to([128, NCHUNK, TT])
            sc = rf if meng == "vector" else ssb
            b.op(meng, lambda e, sc=sc, tmB=tmB: e.tensor_mul(sc[:, :, :], xn[:, :, 1:TT + 1], tmB))
            b.op(meng, lambda e, mix=mix, omB=omB: e.tensor_mul(mix[:, :, :], xn[:, :, 0:TT], omB))
            b.op(meng, lambda e, mix=mix, sc=sc: e.tensor_add(mix[:, :, :], mix[:, :, :], sc[:, :, :]))
        b.pipe(8, [
            ("tensor", lambda e, i: b.mmg(e, psK[i % 2][:, :], [(wk[:, k, i, :], mixk[:, k, :]) for k in range(8)])),
            ("scalar", lambda e, i: e.activation(kf[:, i % 2, :], psK[i % 2][:, :], AF.Copy, bias=0.0, scale=1.0)),
            ("tensor", lambda e, i: b.mmg(e, psV[i % 2][:, :], [(wv[:, k, i, :], mixv[:, k, :]) for k in range(8)])),
            ("vector", lambda e, i: e.tensor_mul(kv[:, i % 2, :], kf[:, i % 2, :], psV[i % 2][:, :])),
            ("vector", lambda e, i: e.tensor_tensor_scan(
                ssq[:, i % 2, :], dec[:, i:i + 1].broadcast_to([128, TT]), kv[:, i % 2, :],
                scol[:, i:i + 1], ALU.mult, ALU.add)),
            ("vector", lambda e, i: e.tensor_copy(scol[:, i:i + 1], ssq[:, i % 2, TT - 1:TT])),
            ("scalar", lambda e, i: e.activation(ssb[:, i, :], ssq[:, i % 2, :], AF.Copy, bias=0.0, scale=1.0)),
            ("tensor", lambda e, i: b.mmg(e, psR[i % 2][:, :], [(wr[:, k, i, :], mixr[:, k, :]) for k in range(8)])),
            ("scalar", lambda e, i: e.activation(rf[:, i, :], psR[i % 2][:, :], AF.Sigmoid, bias=0.0, scale=1.0)),
        ])
        if t == 0:
            b.op("vector", lambda e: e.tensor_mul(
                scol[:, :], scol[:, :], flag[:, 0:1].broadcast_to([128, 8])))
        b.pipe(8, [
            ("tensor", lambda e, i: b.mmg(e, psK[i % 2][:, :], [(wo[:, k, i, :], ssb[:, k, :]) for k in range(8)])),
            ("vector", lambda e, i: e.tensor_mul(rf[:, i, :], rf[:, i, :], psK[i % 2][:, :])),
            ("vector", lambda e, i: e.tensor_add(y3[:, i, :], y3[:, i, :], rf[:, i, :])),
        ])
        ln.stats(None, TT, x_all=y3[:, :, :])
        b.op("vector", lambda e: e.tensor_sub(y3[:, :, :], y3[:, :, :], mB))
        b.op("vector", lambda e: e.tensor_mul(y3[:, :, :], y3[:, :, :], AB))
        b.op("vector", lambda e, x=x: e.tensor_add(y3[:, :, :], y3[:, :, :], x[:, :, :]))
        b.op("scalar", lambda e, t=t: e.dma_start(out=d["s3"][:, :, t * TT:(t + 1) * TT], in_=y3[:, :, :]), dma=True)
        stm[t] = b.val["sdma"]
    b.pop()


def emit_l4(b, cst, d):
    b.push()
    u1 = load_w(b, "u1_s", d["u1"], 8, 2)
    v1 = load_w(b, "v1_s", d["v1"], 2, 32)
    u2 = load_w(b, "u2_s", d["u2"], 32, 2)
    v2 = load_w(b, "v2_s", d["v2"], 2, 8)
    b1 = load_cols(b, "b1_s", d["fb1"], 32)
    b2 = load_cols(b, "b2_s", d["fb2"], 8)
    ln = LNScratch(b, TT, cst)
    lnP = LNScratch(b, TT, cst)
    xt = [b.sbuf("xt0", [128, NCHUNK, TT], F32), b.sbuf("xt1", [128, NCHUNK, TT], F32),
          b.sbuf("xt2", [128, NCHUNK, TT], F32)]
    y4b = [b.sbuf("y4b0", [128, NCHUNK, TT], BF16), b.sbuf("y4b1", [128, NCHUNK, TT], BF16)]
    pb = b.sbuf("pb", [128, 2, TT], BF16)
    hb = b.sbuf("hb", [128, 32, TT], BF16)
    qb = b.sbuf("qb", [128, 2, TT], BF16)
    G = [b.sbuf("G40", [128, NCHUNK, TT], F32), b.sbuf("G41", [128, NCHUNK, TT], F32)]
    ps1 = b.psum_pair("ps41", [128, TT], F32)
    ldm = {}
    stm = {}
    sd0 = b.val["sdma"]

    def load(t):
        if 1 <= t < NTILE:
            b.op("sync", lambda e, t=t: e.dma_start(
                out=xt[t % 3][:, :, :], in_=d["s3"][:, :, t * TT:(t + 1) * TT]), dma=True,
                extra={"sdma": sd0})
            ldm[t] = b.val["sync"]

    mB = ln.m[:, :].unsqueeze(1).broadcast_to([128, NCHUNK, TT])
    AB = ln.A[:, :].unsqueeze(1).broadcast_to([128, NCHUNK, TT])
    mPB = lnP.m[:, :].unsqueeze(1).broadcast_to([128, NCHUNK, TT])
    APB = lnP.A[:, :].unsqueeze(1).broadcast_to([128, NCHUNK, TT])

    def pre(t):
        x = xt[t % 3]
        yb = y4b[t % 2]
        ln.stats(None, TT, x_all=x[:, :, :])
        b.op("vector", lambda e, x=x, yb=yb: e.tensor_sub(yb[:, :, :], x[:, :, :], mB))
        b.op("vector", lambda e, yb=yb: e.tensor_mul(yb[:, :, :], yb[:, :, :], AB))

    def pipe(t, start):
        yb = y4b[t % 2]
        g = G[t % 2]
        m1 = b.pipe(2, [
            ("tensor", lambda e, i, yb=yb: b.mmg(e, ps1[i % 2][:, :], [(u1[:, k, i, :], yb[:, k, :]) for k in range(8)])),
            ("scalar", lambda e, i: e.activation(pb[:, i, :], ps1[i % 2][:, :], AF.Copy, bias=0.0, scale=1.0)),
        ], start=start)
        m2 = b.pipe(32, [
            ("tensor", lambda e, i: b.mmg(e, ps1[i % 2][:, :], [(v1[:, k, i, :], pb[:, k, :]) for k in range(2)])),
            ("scalar", lambda e, i: e.activation(
                hb[:, i, :], ps1[i % 2][:, :], AF.Gelu_apprx_tanh, bias=b1[:, i:i + 1], scale=1.0)),
        ], start=mmax(start, m1))
        m3 = b.pipe(2, [
            ("tensor", lambda e, i: b.mmg(e, ps1[i % 2][:, :], [(u2[:, k, i, :], hb[:, k, :]) for k in range(32)])),
            ("scalar", lambda e, i: e.activation(qb[:, i, :], ps1[i % 2][:, :], AF.Copy, bias=0.0, scale=1.0)),
        ], start=mmax(start, m2))
        b.pipe(8, [
            ("tensor", lambda e, i: b.mmg(e, ps1[i % 2][:, :], [(v2[:, k, i, :], qb[:, k, :]) for k in range(2)])),
            ("scalar", lambda e, i, g=g: e.activation(
                g[:, i, :], ps1[i % 2][:, :], AF.Identity, bias=b2[:, i:i + 1], scale=1.0)),
        ], start=mmax(start, m3))

    def post(t):
        x = xt[t % 3]
        g = G[t % 2]
        lnP.stats(None, TT, x_all=g[:, :, :])
        b.op("vector", lambda e, g=g: e.tensor_sub(g[:, :, :], g[:, :, :], mPB))
        b.op("vector", lambda e, g=g: e.tensor_mul(g[:, :, :], g[:, :, :], APB))
        b.op("vector", lambda e, g=g, x=x: e.tensor_add(g[:, :, :], g[:, :, :], x[:, :, :]))
        b.op("scalar", lambda e, t=t, g=g: e.dma_start(
            out=d["out"][:, :, (t - 1) * TT:t * TT], in_=g[:, :, :]), dma=True)
        stm[t] = b.val["sdma"]
        load(t + 3)

    load(1)
    load(2)
    load(3)
    preSnap = {}
    pipeEnd = {}
    postEnd = {}
    b.scope_push()
    pre(1)
    b.scope_pop()
    preSnap[1] = b.now()
    for t in range(1, NTILE):
        pipe(t, mmax(preSnap[t], pipeEnd.get(t - 1), postEnd.get(t - 2),
                     {"sdma": stm[t - 2]} if t - 2 in stm else None))
        pipeEnd[t] = b.now()
        if t + 1 < NTILE:
            b.scope_push(base=mmax(preSnap[t], postEnd.get(t - 2)))
            pre(t + 1)
            b.scope_pop()
            preSnap[t + 1] = b.now()
        if t - 1 >= 1:
            b.scope_push()
            post(t - 1)
            b.scope_pop()
            postEnd[t - 1] = b.now()
    b.scope_push()
    post(NTILE - 1)
    b.scope_pop()
    b.pop()



def build_fused():
    b = B()
    d = {}
    d["xs"] = b.dram_in("xs", [128, NCHUNK, TW])
    d["flag"] = b.dram_in("flag", [128, 1])
    d["w_in_r"] = b.dram_in("w_in_r", [128, 8, 64])
    d["w_in_i"] = b.dram_in("w_in_i", [128, 8, 64])
    d["w_gate"] = b.dram_in("w_gate", [128, 8, 8, 128])
    d["w_out_r"] = b.dram_in("w_out_r", [64, 8, 128])
    d["w_out_i"] = b.dram_in("w_out_i", [64, 8, 128])
    d["coupT"] = b.dram_in("coupT", [4, 16, 16])
    d["b_gate"] = b.dram_in("b_gate", [128, 8])
    d["ln1g"] = b.dram_in("ln1g", [128, 8])
    d["ln1b"] = b.dram_in("ln1b", [128, 8])
    d["ldecay"] = b.dram_in("ldecay", [64, 1])
    d["freq"] = b.dram_in("freq", [64, 1])
    d["w_mlp1"] = b.dram_in("w_mlp1", [128, 8, 32, 128])
    d["w_mlp2"] = b.dram_in("w_mlp2", [128, 32, 8, 128])
    d["b_mlp1"] = b.dram_in("b_mlp1", [128, 32])
    d["b_mlp2"] = b.dram_in("b_mlp2", [128, 8])
    d["ln2g"] = b.dram_in("ln2g", [128, 8])
    d["ln2b"] = b.dram_in("ln2b", [128, 8])
    d["wg"] = b.dram_in("wg", [128, 8, 24, 128], I32)
    d["wu"] = b.dram_in("wu", [128, 8, 24, 128], I32)
    d["wd"] = b.dram_in("wd", [128, 24, 8, 128], I32)
    d["wk"] = b.dram_in("wk", [128, 8, 8, 128])
    d["wv"] = b.dram_in("wv", [128, 8, 8, 128])
    d["wr"] = b.dram_in("wr", [128, 8, 8, 128])
    d["wo"] = b.dram_in("wo", [128, 8, 8, 128])
    d["lnw"] = b.dram_in("lnw", [128, 8])
    d["lnb"] = b.dram_in("lnb", [128, 8])
    d["tmk"] = b.dram_in("tmk", [128, 8])
    d["tmv"] = b.dram_in("tmv", [128, 8])
    d["tmr"] = b.dram_in("tmr", [128, 8])
    d["dw"] = b.dram_in("dw", [128, 8])
    d["u1"] = b.dram_in("u1", [128, 8, 2, 128])
    d["v1"] = b.dram_in("v1", [128, 2, 32, 128])
    d["u2"] = b.dram_in("u2", [128, 32, 2, 128])
    d["v2"] = b.dram_in("v2", [128, 2, 8, 128])
    d["fb1"] = b.dram_in("fb1", [128, 32])
    d["fb2"] = b.dram_in("fb2", [128, 8])
    d["out"] = b.dram_out("out", [128, NCHUNK, TW - TT])
    d["x1"] = b.dram_tmp("x1_t", [128, NCHUNK, TW])
    d["s1"] = b.dram_tmp("s1_t", [128, NCHUNK, TW])
    d["hq"] = b.dram_tmp("hq_t", [128, 24, TW], BF16)
    d["s2"] = b.dram_tmp("s2_t", [128, NCHUNK, TW])
    d["s3"] = b.dram_tmp("s3_t", [128, NCHUNK, TW])
    b.begin()
    cst = emit_consts(b)
    emit_l1a(b, cst, d)
    emit_l1b(b, cst, d)
    emit_l2(b, cst, d)
    emit_l3(b, cst, d)
    emit_l4(b, cst, d)
    return b.end()



_NCS = {}


def _get_nc(name, fn):
    if name not in _NCS:
        _NCS[name] = fn()
    return _NCS[name]


def kernel(**ins):
    x = np.asarray(ins["L0_attn_x"])
    B_, T_, D_ = x.shape
    H = T_ // 2

    xs_cores = []
    flags = []
    for b_i in range(B_):
        for h in range(2):
            main = x[b_i, h * H:(h + 1) * H, :]
            if h == 1:
                wu = x[b_i, H - TT:H, :]
                fl = np.ones((128, 1), np.float32)
            else:
                wu = np.zeros((TT, D_), np.float32)
                fl = np.zeros((128, 1), np.float32)
            slab = np.concatenate([wu, main], axis=0)
            xs_cores.append(stage_stream(slab))
            flags.append(fl)

    stat = dict(
        w_in_r=stage_w64_out(np.asarray(ins["L0_attn_w_in"])[0:64, :]),
        w_in_i=stage_w64_out(np.asarray(ins["L0_attn_w_in"])[64:128, :]),
        w_gate=stage_w(ins["L0_attn_w_gate"]),
        w_out_r=stage_w64_in(np.asarray(ins["L0_attn_w_out"])[:, 0:64]),
        w_out_i=stage_w64_in(np.asarray(ins["L0_attn_w_out"])[:, 64:128]),
        coupT=np.ascontiguousarray(np.asarray(ins["L0_attn_coupling"]).transpose(0, 2, 1)),
        b_gate=stage_cols(ins["L0_attn_b_gate"]),
        ln1g=stage_cols(ins["L0_attn_ln1_g"]),
        ln1b=stage_cols(ins["L0_attn_ln1_b"]),
        ldecay=np.ascontiguousarray(np.asarray(ins["L0_attn_log_decay"]).reshape(64, 1)),
        freq=np.ascontiguousarray(np.asarray(ins["L0_attn_frequency"]).reshape(64, 1)),
        w_mlp1=stage_w(ins["L0_attn_w_mlp1"]),
        w_mlp2=stage_w(ins["L0_attn_w_mlp2"]),
        b_mlp1=stage_cols(ins["L0_attn_b_mlp1"]),
        b_mlp2=stage_cols(ins["L0_attn_b_mlp2"]),
        ln2g=stage_cols(ins["L0_attn_ln2_g"]),
        ln2b=stage_cols(ins["L0_attn_ln2_b"]),
        wg=stage_w(ins["L0_ffn_w_gate"]),
        wu=stage_w(ins["L0_ffn_w_up"]),
        wd=stage_w(ins["L0_ffn_w_down"]),
        wk=stage_w(ins["L1_attn_Wk"]),
        wv=stage_w(ins["L1_attn_Wv"]),
        wr=stage_w(ins["L1_attn_Wr"]),
        wo=stage_w(ins["L1_attn_Wo"]),
        lnw=stage_cols(ins["L1_attn_ln_w"]),
        lnb=stage_cols(ins["L1_attn_ln_b"]),
        tmk=stage_cols(ins["L1_attn_tmk"]),
        tmv=stage_cols(ins["L1_attn_tmv"]),
        tmr=stage_cols(ins["L1_attn_tmr"]),
        dw=stage_cols(ins["L1_attn_decay_w"]),
        u1=stage_w(np.asarray(ins["L1_ffn_U1"]).T),
        v1=stage_w(np.asarray(ins["L1_ffn_V1"]).T),
        u2=stage_w(np.asarray(ins["L1_ffn_U2"]).T),
        v2=stage_w(np.asarray(ins["L1_ffn_V2"]).T),
        fb1=stage_cols(ins["L1_ffn_b1"]),
        fb2=stage_cols(ins["L1_ffn_b2"]),
    )

    nc = _get_nc("fused", build_fused)
    maps = [dict(stat, xs=xs_cores[i], flag=flags[i]) for i in range(8)]
    r = run_bass_kernel_spmd(nc, maps, core_ids=list(range(8)))

    out = np.empty((B_, T_, D_), np.float32)
    for b_i in range(B_):
        for h in range(2):
            arr = np.asarray(r.results[b_i * 2 + h]["out"])
            out[b_i, h * H:(h + 1) * H, :] = unstage_stream(arr)
    return out


# version 16
# speedup vs baseline: 1.0961x; 1.0141x over previous
import math
import contextlib
import numpy as np
import concourse.bass as bass
import concourse.mybir as mybir
from concourse.bass_utils import run_bass_kernel_spmd

F32 = mybir.dt.float32
BF16 = mybir.dt.bfloat16
I32 = mybir.dt.int32
AF = mybir.ActivationFunctionType
ALU = mybir.AluOpType

D = 1024
NCHUNK = 8
TT = 512
NTILE = 9
TW = TT * NTILE
EPS = 1e-5
C0 = (1.0 + EPS) ** -0.5
SEG = 16
CH = 64
PI2_HI = float(np.float32(math.pi / 2))
PI2_LO = math.pi / 2 - PI2_HI



def stage_w(W):
    W = np.ascontiguousarray(W)
    oc = W.shape[0] // 128
    ic = W.shape[1] // 128
    t = W.reshape(oc, 128, ic, 128).transpose(3, 2, 0, 1)
    return np.ascontiguousarray(t)


def stage_w64_out(W):
    W = np.ascontiguousarray(W)
    ic = W.shape[1] // 128
    return np.ascontiguousarray(W.reshape(64, ic, 128).transpose(2, 1, 0))


def stage_w64_in(W):
    W = np.ascontiguousarray(W)
    oc = W.shape[0] // 128
    return np.ascontiguousarray(W.reshape(oc, 128, 64).transpose(2, 0, 1))


def stage_cols(v):
    return np.ascontiguousarray(np.asarray(v).reshape(-1, 128).T)


def stage_stream(x_slab):
    T = x_slab.shape[0]
    t = x_slab.T.reshape(NCHUNK, 128, T)
    return np.ascontiguousarray(t.transpose(1, 0, 2))


def unstage_stream(arr):
    return np.ascontiguousarray(arr.transpose(1, 0, 2).reshape(D, -1).T)



ENGS = ("tensor", "vector", "scalar", "gpsimd", "sync", "gdma", "sdma")
COMPUTE_LANES = ("tensor", "vector", "scalar", "gpsimd", "gdma")


class B:
    def __init__(self):
        self.nc = bass.Bass("TRN2", target_bir_lowering=False, debug=False)
        self.stack = contextlib.ExitStack()
        self.sects = []
        self._blk = None
        self.sems = {}
        self.val = {e: 0 for e in ENGS}
        self.scopes = []
        self.uid = 0

    def dram_in(self, name, shape, dtype=F32):
        return self.nc.dram_tensor(name, shape, dtype, kind="ExternalInput").ap()

    def dram_out(self, name, shape, dtype=F32):
        return self.nc.dram_tensor(name, shape, dtype, kind="ExternalOutput").ap()

    def dram_tmp(self, name, shape, dtype=F32):
        return self.nc.dram_tensor(name, shape, dtype, kind="Internal").ap()

    def _st(self):
        return self.sects[-1] if self.sects else self.stack

    def sbuf(self, name, shape, dtype=F32):
        self.uid += 1
        return self._st().enter_context(
            self.nc.sbuf_tensor(f"{name}_{self.uid}", shape, dtype))

    def psum_pair(self, name, shape, dtype=F32):
        return [self.psum(name + "a", shape, dtype), self.psum(name + "b", shape, dtype)]

    def psum(self, name, shape, dtype=F32):
        self.uid += 1
        shape = list(shape)
        if dtype == F32 and shape[-1] < 512:
            shape[-1] = 512
        return self._st().enter_context(
            self.nc.psum_tensor(f"{name}_{self.uid}", shape, dtype))

    def scope_push(self, base=None):
        self.scopes.append({"base": dict(self.val) if base is None else dict(base), "seen": {}})

    def scope_pop(self):
        self.scopes.pop()

    def push(self):
        self.sects.append(contextlib.ExitStack())

    def pop(self):
        self.sects.pop().close()

    def begin(self):
        for e in ENGS:
            self.sems[e] = self.stack.enter_context(self.nc.semaphore(f"sem_{e}"))
        self._blk = self.stack.enter_context(self.nc.Block())
        self.bodies = {e: [] for e in ("tensor", "vector", "scalar", "gpsimd", "sync")}

    def now(self):
        return dict(self.val)

    def op(self, eng, fn, dma=False, after=None, extra=None):
        key = eng
        if dma:
            key = {"gpsimd": "gdma", "scalar": "sdma"}.get(eng, eng)
        if after is None:
            if self.scopes:
                sc = self.scopes[-1]
                after = dict(sc["base"])
                for l2, v2 in sc["seen"].items():
                    after[l2] = max(after.get(l2, 0), v2)
            else:
                after = {e: self.val[e] for e in COMPUTE_LANES}
        if extra:
            after = dict(after)
            for k2, v2 in extra.items():
                after[k2] = max(after.get(k2, 0), v2)
        if dma:
            after = dict(after)
            after[key] = max(after.get(key, 0), self.val[key])
        waits = [(self.sems[e], v) for e, v in after.items() if v > 0]
        inc = 16 if dma else 1
        sem = self.sems[key]

        def body(e):
            for smh, v in waits:
                e.wait_ge(smh, v)
            ins = fn(e)
            ins.then_inc(sem, inc)
        self.bodies[eng].append(body)
        self.val[key] += inc
        if self.scopes:
            self.scopes[-1]["seen"][key] = self.val[key]

    def mm(self, ps_ap, pairs, after=None):
        def f(e):
            last = None
            n = len(pairs)
            for i, (l, r) in enumerate(pairs):
                last = e.matmul(ps_ap, l, r, start=(i == 0), stop=(i == n - 1))
            return last
        self.op("tensor", f, after=after)

    def pipe(self, n, stages, guard=2, extra=None, start=None):
        if start is None:
            start = {e: self.val[e] for e in COMPUTE_LANES}
        else:
            start = dict(start)
        if extra:
            for k2, v2 in extra.items():
                start[k2] = max(start.get(k2, 0), v2)
        last_eng = stages[-1][0]
        last_marks = []
        s0_mark = None
        touched = {}
        for i in range(n):
            prev = None
            for k, (eng, fn) in enumerate(stages):
                after = {}
                if i == 0:
                    after.update(start)
                if k == 0:
                    if i >= guard:
                        after[last_eng] = max(after.get(last_eng, 0), last_marks[i - guard])
                    elif i == 1:
                        after[stages[0][0]] = max(after.get(stages[0][0], 0), s0_mark)
                else:
                    pe, pv = prev
                    if pe == eng:
                        after[eng] = max(after.get(eng, 0), pv)
                    else:
                        after[pe] = max(after.get(pe, 0), pv)
                self.op(eng, lambda e, fn=fn, i=i: fn(e, i), after=after)
                prev = (eng, self.val[eng])
                touched[eng] = self.val[eng]
                if k == 0:
                    s0_mark = self.val[eng]
            last_marks.append(self.val[last_eng])
        return touched

    def mmg(self, e, ps_ap, pairs):
        last = None
        n = len(pairs)
        for i, (l, r) in enumerate(pairs):
            last = e.matmul(ps_ap, l, r, start=(i == 0), stop=(i == n - 1))
        return last

    def end(self):
        vals = self.now()
        for eng, lst in self.bodies.items():
            if not lst:
                continue

            def mega(e, lst=lst):
                for f in lst:
                    f(e)
            getattr(self._blk, eng)(mega)

        def fin(e):
            for en in ENGS:
                if vals[en]:
                    e.wait_ge(self.sems[en], vals[en])
        self._blk.sync(fin)
        while self.sects:
            self.pop()
        self.stack.close()
        return self.nc


def mmax(*ds):
    out = {}
    for d_ in ds:
        if not d_:
            continue
        for k, v in d_.items():
            out[k] = max(out.get(k, 0), v)
    return out



def emit_consts(b):
    cst = {}
    t = b.sbuf("c_eps", [128, 1], F32)
    b.op("gpsimd", lambda e: e.memset(t[:, :], EPS))
    cst["eps"] = t
    t2 = b.sbuf("c_eps2", [128, 1], F32)
    b.op("gpsimd", lambda e: e.memset(t2[:, :], EPS * EPS))
    cst["eps2"] = t2
    ones = b.sbuf("c_ones", [128, 128], BF16)
    b.op("gpsimd", lambda e: e.memset(ones[:, :], 1.0))
    cst["ones"] = ones
    return cst


def load_w(b, name, dram_ap, ic, oc, dtype=BF16):
    w = b.sbuf(name, [128, ic, oc, 128], dtype)
    b.op("gpsimd", lambda e: e.dma_start(out=w[:, :, :, :], in_=dram_ap[:, :, :, :]),
         dma=True, extra={"sdma": b.val["sdma"], "sync": b.val["sync"]})
    return w


def load_cols(b, name, dram_ap, n, dtype=F32):
    t = b.sbuf(name, [128, n], dtype)
    b.op("gpsimd", lambda e: e.dma_start(out=t[:, :], in_=dram_ap[:, :]), dma=True,
         extra={"sdma": b.val["sdma"], "sync": b.val["sync"]})
    return t


class LNScratch:
    def __init__(self, b, n, cst):
        self.b = b
        self.cst = cst
        self.nn = n
        self.xb = b.sbuf("ln_xb", [128, NCHUNK, n], BF16)
        self.m = b.sbuf("ln_m", [128, n], F32)
        self.A = b.sbuf("ln_A", [128, n], F32)
        self.t1 = b.sbuf("ln_t1", [128, n], F32)
        self.t2 = None
        self.A2 = None
        self.psS = b.psum("ln_psS", [128, n], F32)
        self.psQ = b.psum("ln_psQ", [128, n], F32)

    def mkcomp(self, n):
        if self.A2 is None:
            self.t2 = self.b.sbuf("ln_t2", [128, self.nn], F32)
            self.A2 = self.b.sbuf("ln_A2", [128, self.nn], F32)

    def stats(self, x_chunks, n, nch=NCHUNK, comp=False, first=None, x_all=None):
        b, cst = self.b, self.cst
        if comp:
            self.mkcomp(n)

        if x_all is not None:
            b.op("scalar", lambda e: e.activation(
                self.xb[:, 0:nch, :n], x_all, AF.Copy, bias=0.0, scale=1.0), extra=first)
        else:
            def copies(e):
                last = None
                for c in range(nch):
                    last = e.activation(self.xb[:, c, :n], x_chunks(c), AF.Copy, bias=0.0, scale=1.0)
                return last
            b.op("scalar", copies, extra=first)
        b.mm(self.psS[:, :n], [(cst["ones"][:, :], self.xb[:, c, :n]) for c in range(nch)])
        psS_mark = b.val["tensor"]
        b.op("scalar", lambda e: e.activation(
            self.xb[:, 0:nch, :n], self.xb[:, 0:nch, :n], AF.Square, bias=0.0, scale=1.0),
            after={"scalar": b.val["scalar"], "tensor": psS_mark})
        b.mm(self.psQ[:, :n], [(cst["ones"][:, :], self.xb[:, c, :n]) for c in range(nch)])
        inv = 1.0 / (128 * nch)
        b.op("vector", lambda e: e.tensor_scalar(self.m[:, :n], self.psS[:, :n], inv, None, ALU.mult))
        b.op("vector", lambda e: e.tensor_mul(self.t1[:, :n], self.m[:, :n], self.m[:, :n]))
        b.op("vector", lambda e: e.scalar_tensor_tensor(
            self.t1[:, :n], self.psQ[:, :n], inv, self.t1[:, :n], ALU.mult, ALU.subtract))
        if comp:
            b.op("scalar", lambda e: e.activation(
                self.t2[:, :n], self.t1[:, :n], AF.Sqrt, bias=cst["eps2"][:, 0:1],
                scale=1.0 + EPS))
            b.op("vector", lambda e: e.reciprocal(self.A2[:, :n], self.t2[:, :n]))
        b.op("scalar", lambda e: e.activation(
            self.t1[:, :n], self.t1[:, :n], AF.Sqrt, bias=cst["eps"][:, 0:1], scale=1.0))
        b.op("vector", lambda e: e.reciprocal(self.A[:, :n], self.t1[:, :n]))



def emit_eig_tables(b, cst, ld_col, fr_col):
    eg = {}
    mag = b.sbuf("eg_mag", [64, 1], F32)
    b.op("scalar", lambda e: e.activation(mag[:, :], ld_col[:, :], AF.Sigmoid, bias=0.0, scale=1.0))
    s2 = b.sbuf("eg_s2", [64, 1], F32)
    b.op("scalar", lambda e: e.activation(s2[:, :], fr_col[:, :], AF.Sin, bias=0.0, scale=0.5))
    u = b.sbuf("eg_u", [64, 1], F32)
    b.op("vector", lambda e: e.tensor_scalar(u[:, :], fr_col[:, :], -0.5, PI2_HI, ALU.mult, ALU.add))
    b.op("vector", lambda e: e.tensor_scalar(u[:, :], u[:, :], PI2_LO, None, ALU.add))
    c2 = b.sbuf("eg_c2", [64, 1], F32)
    b.op("scalar", lambda e: e.activation(c2[:, :], u[:, :], AF.Sin, bias=0.0, scale=1.0))
    sth = b.sbuf("eg_sth", [64, 1], F32)
    b.op("vector", lambda e: e.tensor_mul(sth[:, :], s2[:, :], c2[:, :]))
    b.op("vector", lambda e: e.tensor_scalar(sth[:, :], sth[:, :], 2.0, None, ALU.mult))
    cth = b.sbuf("eg_cth", [64, 1], F32)
    b.op("vector", lambda e: e.tensor_mul(cth[:, :], s2[:, :], s2[:, :]))
    b.op("vector", lambda e: e.tensor_scalar(cth[:, :], cth[:, :], -2.0, 1.0, ALU.mult, ALU.add))
    a = b.sbuf("eg_a", [64, 1], F32)
    b.op("vector", lambda e: e.tensor_mul(a[:, :], cth[:, :], mag[:, :]))
    bb = b.sbuf("eg_b", [64, 1], F32)
    b.op("vector", lambda e: e.tensor_mul(bb[:, :], sth[:, :], mag[:, :]))
    rmag = b.sbuf("eg_rmag", [64, 1], F32)
    b.op("vector", lambda e: e.reciprocal(rmag[:, :], mag[:, :]))
    ia = b.sbuf("eg_ia", [64, 1], F32)
    b.op("vector", lambda e: e.tensor_mul(ia[:, :], cth[:, :], rmag[:, :]))
    ib = b.sbuf("eg_ib", [64, 1], F32)
    b.op("vector", lambda e: e.tensor_scalar(ib[:, :], sth[:, :], -1.0, None, ALU.mult))
    b.op("vector", lambda e: e.tensor_mul(ib[:, :], ib[:, :], rmag[:, :]))

    W17r = b.sbuf("eg_W17r", [64, 17], F32)
    W17i = b.sbuf("eg_W17i", [64, 17], F32)
    U16r = b.sbuf("eg_U16r", [64, 16], F32)
    U16i = b.sbuf("eg_U16i", [64, 16], F32)
    tmp = b.sbuf("eg_tmp", [64, 1], F32)
    for (Wr, Wi, n, ar, br) in ((W17r, W17i, 17, a, bb), (U16r, U16i, 16, ia, ib)):
        b.op("gpsimd", lambda e, Wr=Wr: e.memset(Wr[:, 0:1], 1.0))
        b.op("gpsimd", lambda e, Wi=Wi: e.memset(Wi[:, 0:1], 0.0))
        for j in range(1, n):
            b.op("vector", lambda e, j=j, Wr=Wr, ar=ar: e.tensor_scalar(
                Wr[:, j:j + 1], Wr[:, j - 1:j], ar[:, 0:1], None, ALU.mult))
            b.op("vector", lambda e, j=j, Wi=Wi, br=br: e.tensor_scalar(
                tmp[:, :], Wi[:, j - 1:j], br[:, 0:1], None, ALU.mult))
            b.op("vector", lambda e, j=j, Wr=Wr: e.tensor_sub(
                Wr[:, j:j + 1], Wr[:, j:j + 1], tmp[:, :]))
            b.op("vector", lambda e, j=j, Wi=Wi, ar=ar: e.tensor_scalar(
                Wi[:, j:j + 1], Wi[:, j - 1:j], ar[:, 0:1], None, ALU.mult))
            b.op("vector", lambda e, j=j, Wr=Wr, Wi=Wi, br=br: e.tensor_scalar(
                tmp[:, :], Wr[:, j - 1:j], br[:, 0:1], None, ALU.mult))
            b.op("vector", lambda e, j=j, Wi=Wi: e.tensor_add(
                Wi[:, j:j + 1], Wi[:, j:j + 1], tmp[:, :]))
    mask = b.sbuf("eg_mask", [64, 2 * TT], F32)
    b.op("gpsimd", lambda e: e.memset(mask[:, :], 1.0))
    b.op("gpsimd", lambda e: e.memset(mask[:, 0:2 * TT:SEG], 0.0))
    eg.update(W17r=W17r, W17i=W17i, U16r=U16r, U16i=U16i, mask=mask)
    return eg


def emit_eigenstate(b, eg, bt, cout, n, carry_ap, scratch):
    W17r, W17i = eg["W17r"], eg["W17i"]
    U16r, U16i = eg["U16r"], eg["U16i"]
    y, p, z, G = scratch["y"], scratch["p"], scratch["z"], scratch["G"]
    ta, tb = scratch["ta"], scratch["tb"]
    nseg = n // SEG
    nch = n // CH

    def b16(t, rev=False):
        ap = t[:, 16:0:-1] if rev else t[:, 0:16]
        return ap.unsqueeze(1).broadcast_to([64, 2 * nseg, 16])

    def b16h(t, rev=False):
        ap = t[:, 16:0:-1] if rev else t[:, 0:16]
        return ap.unsqueeze(1).broadcast_to([64, nseg, 16])

    def r3(ap):
        return ap.rearrange("p (g f) -> p g f", f=16)

    tc, td = scratch["tc"], scratch["td"]
    b.op("vector", lambda e: e.tensor_mul(r3(y[:, 0:n]), r3(bt[:, 0:n]), b16h(W17r)))
    b.op("vector", lambda e: e.tensor_mul(r3(ta[:, 0:n]), r3(bt[:, n:2 * n]), b16h(W17i)))
    b.op("vector", lambda e: e.tensor_sub(y[:, 0:n], y[:, 0:n], ta[:, 0:n]))
    b.op("gpsimd", lambda e: e.tensor_mul(r3(y[:, n:2 * n]), r3(bt[:, n:2 * n]), b16h(W17r)))
    b.op("gpsimd", lambda e: e.tensor_mul(r3(tc[:, 0:n]), r3(bt[:, 0:n]), b16h(W17i)))
    b.op("gpsimd", lambda e: e.tensor_add(y[:, n:2 * n], y[:, n:2 * n], tc[:, 0:n]))
    b.op("vector", lambda e: e.tensor_tensor_scan(
        p[:, 0:n], eg["mask"][:, 0:n], y[:, n - 1::-1], 0.0, ALU.mult, ALU.add))
    b.op("vector", lambda e: e.tensor_tensor_scan(
        p[:, n:2 * n], eg["mask"][:, 0:n], y[:, 2 * n - 1:n - 1:-1], 0.0, ALU.mult, ALU.add))
    sufr = p[:, n - 1::-1]
    sufi = p[:, 2 * n - 1:n - 1:-1]
    b.op("vector", lambda e: e.tensor_mul(r3(z[:, 0:n]), r3(sufr), b16h(U16r)))
    b.op("vector", lambda e: e.tensor_mul(r3(ta[:, 0:n]), r3(sufi), b16h(U16i)))
    b.op("vector", lambda e: e.tensor_sub(z[:, 0:n], z[:, 0:n], ta[:, 0:n]))
    b.op("gpsimd", lambda e: e.tensor_mul(r3(z[:, n:2 * n]), r3(sufi), b16h(U16r)))
    b.op("gpsimd", lambda e: e.tensor_mul(r3(tc[:, 0:n]), r3(sufr), b16h(U16i)))
    b.op("gpsimd", lambda e: e.tensor_add(z[:, n:2 * n], z[:, n:2 * n], tc[:, 0:n]))
    a16r = W17r[:, 16:17]
    a16i = W17i[:, 16:17]

    def Gs(m, imag):
        base = nseg if imag else 0
        return G[:, base + m::4][:, 0:nch]

    def Ts(m, imag):
        base = n if imag else 0
        return z[:, base + m * 16::CH][:, 0:nch]
    b.op("gpsimd", lambda e: e.memset(Gs(3, False), 0.0))
    b.op("gpsimd", lambda e: e.memset(Gs(3, True), 0.0))
    b.op("vector", lambda e: e.tensor_copy(Gs(2, False), Ts(3, False)))
    b.op("vector", lambda e: e.tensor_copy(Gs(2, True), Ts(3, True)))
    for m in (1, 0):
        b.op("vector", lambda e, m=m: e.scalar_tensor_tensor(
            Gs(m, False), Gs(m + 1, False), a16r[:, 0:1], Ts(m + 1, False), ALU.mult, ALU.add))
        b.op("vector", lambda e, m=m: e.tensor_scalar(
            ta[:, 0:nch], Gs(m + 1, True), a16i[:, 0:1], None, ALU.mult))
        b.op("vector", lambda e, m=m: e.tensor_sub(Gs(m, False), Gs(m, False), ta[:, 0:nch]))
        b.op("vector", lambda e, m=m: e.scalar_tensor_tensor(
            Gs(m, True), Gs(m + 1, True), a16r[:, 0:1], Ts(m + 1, True), ALU.mult, ALU.add))
        b.op("vector", lambda e, m=m: e.tensor_scalar(
            ta[:, 0:nch], Gs(m + 1, False), a16i[:, 0:1], None, ALU.mult))
        b.op("vector", lambda e, m=m: e.tensor_add(Gs(m, True), Gs(m, True), ta[:, 0:nch]))
    GrB = G[:, 0:nseg].unsqueeze(2).broadcast_to([64, nseg, 16])
    GiB = G[:, nseg:2 * nseg].unsqueeze(2).broadcast_to([64, nseg, 16])
    b.op("vector", lambda e: e.tensor_mul(r3(ta[:, 0:n]), b16h(W17r, True), GrB))
    b.op("vector", lambda e: e.tensor_mul(r3(tb[:, 0:n]), b16h(W17i, True), GiB))
    b.op("vector", lambda e: e.tensor_sub(ta[:, 0:n], ta[:, 0:n], tb[:, 0:n]))
    b.op("vector", lambda e: e.tensor_add(cout[:, 0:n], z[:, 0:n], ta[:, 0:n]))
    b.op("gpsimd", lambda e: e.tensor_mul(r3(tc[:, 0:n]), b16h(W17r, True), GiB))
    b.op("gpsimd", lambda e: e.tensor_mul(r3(td[:, 0:n]), b16h(W17i, True), GrB))
    b.op("gpsimd", lambda e: e.tensor_add(tc[:, 0:n], tc[:, 0:n], td[:, 0:n]))
    b.op("gpsimd", lambda e: e.tensor_add(cout[:, n:2 * n], z[:, n:2 * n], tc[:, 0:n]))
    if carry_ap is not None:
        b.op("vector", lambda e: e.tensor_add(cout[:, 0:1], cout[:, 0:1], carry_ap[:, 0:1]))
        b.op("gpsimd", lambda e: e.tensor_add(cout[:, n:n + 1], cout[:, n:n + 1], carry_ap[:, 1:2]))
    if nch > 1:
        b.op("vector", lambda e: e.tensor_add(
            cout[:, CH::CH][:, 0:nch - 1], cout[:, CH::CH][:, 0:nch - 1],
            bt[:, CH - 1::CH][:, 0:nch - 1]))
        b.op("gpsimd", lambda e: e.tensor_add(
            cout[:, n + CH::CH][:, 0:nch - 1], cout[:, n + CH::CH][:, 0:nch - 1],
            bt[:, n + CH - 1::CH][:, 0:nch - 1]))


def mk_eig_scratch(b, n):
    y = b.sbuf("eg_y", [64, 2 * n], F32)
    return {
        "y": y,
        "p": b.sbuf("eg_p", [64, 2 * n], F32),
        "z": y,
        "G": b.sbuf("eg_G", [64, 2 * (n // SEG)], F32),
        "ta": b.sbuf("eg_ta", [64, n], F32),
        "tb": b.sbuf("eg_tb", [64, n], F32),
        "tc": b.sbuf("eg_tc", [64, n], F32),
        "td": b.sbuf("eg_td", [64, n], F32),
    }



def emit_l1a(b, cst, d):
    b.push()
    w_gate = load_w(b, "w_gate_s", d["w_gate"], 8, 8)
    w_in_r = b.sbuf("w_in_r_s", [128, 8, 64], BF16)
    b.op("gpsimd", lambda e: e.dma_start(out=w_in_r[:, :, :], in_=d["w_in_r"][:, :, :]), dma=True,
         extra={"sdma": b.val["sdma"], "sync": b.val["sync"]})
    w_in_i = b.sbuf("w_in_i_s", [128, 8, 64], BF16)
    b.op("gpsimd", lambda e: e.dma_start(out=w_in_i[:, :, :], in_=d["w_in_i"][:, :, :]), dma=True)
    wor = b.sbuf("wor_s", [64, 8, 128], BF16)
    b.op("gpsimd", lambda e: e.dma_start(out=wor[:, :, :], in_=d["w_out_r"][:, :, :]), dma=True)
    woi = b.sbuf("woi_s", [64, 8, 128], BF16)
    b.op("gpsimd", lambda e: e.dma_start(out=woi[:, :, :], in_=d["w_out_i"][:, :, :]), dma=True)
    b_gate = load_cols(b, "b_gate_s", d["b_gate"], 8)
    ln1g = load_cols(b, "ln1g_s", d["ln1g"], 8)
    ln1b = load_cols(b, "ln1b_s", d["ln1b"], 8)
    ld = b.sbuf("ld_s", [64, 1], F32)
    b.op("gpsimd", lambda e: e.dma_start(out=ld[:, :], in_=d["ldecay"][:, :]), dma=True)
    fr = b.sbuf("fr_s", [64, 1], F32)
    b.op("gpsimd", lambda e: e.dma_start(out=fr[:, :], in_=d["freq"][:, :]), dma=True)
    flag = load_cols(b, "flag_s", d["flag"], 1)
    wcp = b.sbuf("wcp", [64, 64], F32)
    b.op("gpsimd", lambda e: e.memset(wcp[:, :], 0.0),
         extra={"sdma": b.val["sdma"], "sync": b.val["sync"]})
    for h in range(4):
        o = 16 * h
        b.op("gpsimd", lambda e, o=o, h=h: e.dma_start(
            out=wcp[o:o + 16, o:o + 16], in_=d["coupT"][h, :, :]), dma=True)
    eg = emit_eig_tables(b, cst, ld, fr)
    ln = LNScratch(b, TT, cst)
    egs = mk_eig_scratch(b, TT)

    xt = [b.sbuf("xt0", [128, NCHUNK, TT], F32), b.sbuf("xt1", [128, NCHUNK, TT], F32),
          b.sbuf("xt2", [128, NCHUNK, TT], F32)]
    y1 = [b.sbuf("y10", [128, NCHUNK, TT], F32), b.sbuf("y11", [128, NCHUNK, TT], F32),
          b.sbuf("y12", [128, NCHUNK, TT], F32)]
    xnb = [b.sbuf("xnb0", [128, NCHUNK, TT], BF16), b.sbuf("xnb1", [128, NCHUNK, TT], BF16)]
    btb = [b.sbuf("bt0", [64, 2 * TT], F32), b.sbuf("bt1", [64, 2 * TT], F32)]
    ct = egs["p"]
    cpb = [b.sbuf("cpb0", [64, 2 * TT], BF16), b.sbuf("cpb1", [64, 2 * TT], BF16)]
    g8 = b.sbuf("g8", [128, NCHUNK, TT], F32)
    ho = b.sbuf("ho", [128, 2, TT], F32)
    carryT = b.sbuf("carryT", [64, 2 * NTILE], F32)
    carryB = b.sbuf("carryB", [64, 2], F32)
    zcol = b.sbuf("zcol", [64, 2], F32)
    ps1 = b.psum_pair("ps1", [128, TT], F32)
    ps2 = b.psum_pair("ps2", [128, TT], F32)
    psBr = b.psum("psBr", [64, TT], F32)
    psBi = b.psum("psBi", [64, TT], F32)
    b.op("gpsimd", lambda e: e.memset(zcol[:, :], 0.0))
    ldm = {}
    stm = {}
    sd0 = b.val["sdma"]

    def load(t):
        if 0 <= t < NTILE:
            b.op("sync", lambda e, t=t: e.dma_start(
                out=xt[t % 3][:, :, :], in_=d["xs"][:, :, t * TT:(t + 1) * TT]), dma=True,
                extra={"sdma": sd0})
            ldm[t] = b.val["sync"]

    mB = ln.m[:, :].unsqueeze(1).broadcast_to([128, NCHUNK, TT])
    A2B = None
    AB = ln.A[:, :].unsqueeze(1).broadcast_to([128, NCHUNK, TT])
    gB = ln1g[:, :].unsqueeze(2).broadcast_to([128, NCHUNK, TT])
    bB = ln1b[:, :].unsqueeze(2).broadcast_to([128, NCHUNK, TT])

    def preA(t):
        nonlocal A2B
        x = xt[t % 3]
        yy = y1[t % 3]
        xb = xnb[t % 2]
        bb = btb[t % 2]
        ln.stats(None, TT, comp=True, first={"sync": ldm[t]}, x_all=x[:, :, :])
        if A2B is None:
            A2B = ln.A2[:, :].unsqueeze(1).broadcast_to([128, NCHUNK, TT])
        b.op("vector", lambda e, x=x, yy=yy: e.tensor_sub(yy[:, :, :], x[:, :, :], mB),
             extra=({"sdma": stm[t - 3]} if t - 3 in stm else None))
        b.op("vector", lambda e, yy=yy, xb=xb: e.tensor_mul(xb[:, :, :], yy[:, :, :], A2B))
        b.op("gpsimd", lambda e, yy=yy: e.tensor_mul(yy[:, :, :], yy[:, :, :], AB))
        b.op("vector", lambda e, xb=xb: e.tensor_mul(xb[:, :, :], xb[:, :, :], gB))
        b.op("vector", lambda e, xb=xb: e.tensor_add(xb[:, :, :], xb[:, :, :], bB))
        applySnap[t] = b.now()
        b.mm(psBr[:, :], [(w_in_r[:, k, :], xb[:, k, :]) for k in range(8)])
        b.op("vector", lambda e, bb=bb: e.tensor_copy(bb[:, 0:TT], psBr[:, :]))
        b.mm(psBi[:, :], [(w_in_i[:, k, :], xb[:, k, :]) for k in range(8)])
        b.op("vector", lambda e, bb=bb: e.tensor_copy(bb[:, TT:2 * TT], psBi[:, :]))
        b.op("vector", lambda e, t=t, bb=bb: e.tensor_copy(
            carryT[:, 2 * t:2 * t + 1], bb[:, TT - 1:TT]))
        b.op("vector", lambda e, t=t, bb=bb: e.tensor_copy(
            carryT[:, 2 * t + 1:2 * t + 2], bb[:, 2 * TT - 1:2 * TT]))
        if t == 0:
            b.op("vector", lambda e: e.tensor_mul(
                carryB[:, :], carryT[:, 0:2], flag[0:64, 0:1].broadcast_to([64, 2])))

    def preB(t):
        bb = btb[t % 2]
        cp = cpb[t % 2]
        if t == 0:
            ccol = zcol
        elif t == 1:
            ccol = carryB
        else:
            ccol = carryT[:, 2 * (t - 1):2 * t]
        emit_eigenstate(b, eg, bb, ct, TT, ccol[:, 0:2], egs)
        b.mm(ln.psS[0:64, :], [(wcp[:, :], ct[:, 0:TT])])
        b.op("vector", lambda e, cp=cp: e.tensor_copy(cp[:, 0:TT], ln.psS[0:64, :]))
        b.mm(ln.psQ[0:64, :], [(wcp[:, :], ct[:, TT:2 * TT])])
        b.op("vector", lambda e, cp=cp: e.tensor_copy(cp[:, TT:2 * TT], ln.psQ[0:64, :]))

    def gatePipe(t, start):
        xb = xnb[t % 2]
        return b.pipe(8, [
            ("tensor", lambda e, i, xb=xb: b.mmg(e, ps1[i % 2][:, :], [(w_gate[:, k, i, :], xb[:, k, :]) for k in range(8)])),
            ("scalar", lambda e, i: e.activation(
                g8[:, i, :], ps1[i % 2][:, :], AF.Sigmoid, bias=b_gate[:, i:i + 1], scale=1.0)),
        ], start=start)

    def woPipe(t, start):
        cp = cpb[t % 2]
        yy = y1[t % 3]
        return b.pipe(8, [
            ("tensor", lambda e, i, cp=cp: b.mmg(e, ps2[i % 2][:, :], [(wor[:, i, :], cp[:, 0:TT]), (woi[:, i, :], cp[:, TT:2 * TT])])),
            ("scalar", lambda e, i: e.activation(
                ho[:, i % 2, :], ps2[i % 2][:, :], AF.Copy, bias=0.0, scale=1.0)),
            ("gpsimd", lambda e, i: e.tensor_mul(ho[:, i % 2, :], ho[:, i % 2, :], g8[:, i, :])),
            ("gpsimd", lambda e, i, yy=yy: e.tensor_add(yy[:, i, :], yy[:, i, :], ho[:, i % 2, :])),
        ], start=start)

    def post(t):
        yy = y1[t % 3]
        b.op("scalar", lambda e, t=t, yy=yy: e.dma_start(
            out=d["x1"][:, :, t * TT:(t + 1) * TT], in_=yy[:, :, :]), dma=True)
        stm[t] = b.val["sdma"]
        load(t + 3)

    load(0)
    load(1)
    load(2)
    applySnap = {}
    preASnap = {}
    preBEnd = {}
    gateEnd = {}
    woEnd = {}
    b.scope_push()
    preA(0)
    b.scope_pop()
    preASnap[0] = b.now()
    for t in range(NTILE):
        gm = gatePipe(t, mmax(applySnap[t], gateEnd.get(t - 1), woEnd.get(t - 1),
                              {"sdma": stm[t - 2]} if t - 2 in stm else None))
        gateEnd[t] = b.now()
        b.scope_push(base=mmax(preASnap[t], preBEnd.get(t - 1)))
        preB(t)
        b.scope_pop()
        preBEnd[t] = b.now()
        woPipe(t, mmax(preBEnd[t], gm, woEnd.get(t - 1)))
        woEnd[t] = b.now()
        if t + 1 < NTILE:
            b.scope_push(base=mmax(preASnap[t], preBEnd[t]))
            preA(t + 1)
            b.scope_pop()
            preASnap[t + 1] = b.now()
        if t - 1 >= 0:
            b.scope_push()
            post(t - 1)
            b.scope_pop()
    b.scope_push()
    post(NTILE - 1)
    b.scope_pop()
    b.pop()


def emit_l1b(b, cst, d):
    N = 256
    NT = TW // N
    b.push()
    w1 = load_w(b, "w1_s", d["w_mlp1"], 8, 32)
    w2 = load_w(b, "w2_s", d["w_mlp2"], 32, 8)
    b1 = load_cols(b, "b1_s", d["b_mlp1"], 32)
    b2 = load_cols(b, "b2_s", d["b_mlp2"], 8)
    ln2g = load_cols(b, "ln2g_s", d["ln2g"], 8)
    ln2b = load_cols(b, "ln2b_s", d["ln2b"], 8)
    ln = LNScratch(b, N, cst)
    x1t = [b.sbuf("x1t0", [128, NCHUNK, N], F32), b.sbuf("x1t1", [128, NCHUNK, N], F32)]
    xst = [b.sbuf("xst0", [128, NCHUNK, N], F32), b.sbuf("xst1", [128, NCHUNK, N], F32)]
    xn2b = [b.sbuf("xn2b0", [128, NCHUNK, N], BF16), b.sbuf("xn2b1", [128, NCHUNK, N], BF16)]
    h1b = b.sbuf("h1b", [128, 32, N], BF16)
    ao = [b.sbuf("ao0", [128, NCHUNK, N], F32), b.sbuf("ao1", [128, NCHUNK, N], F32)]
    ps1 = b.psum_pair("ps1", [128, N], F32)
    ldm = {}
    stm = {}
    sd0 = b.val["sdma"]

    def loadx1(t):
        if 0 <= t < NT:
            b.op("sync", lambda e, t=t: e.dma_start(
                out=x1t[t % 2][:, :, :], in_=d["x1"][:, :, t * N:(t + 1) * N]), dma=True,
                extra={"sdma": sd0})
            ldm[t] = b.val["sync"]

    def loadxs(t):
        if 0 <= t < NT:
            b.op("sync", lambda e, t=t: e.dma_start(
                out=xst[t % 2][:, :, :], in_=d["xs"][:, :, t * N:(t + 1) * N]), dma=True,
                extra={"sdma": sd0})

    def load(t):
        loadx1(t)
        loadxs(t)

    mB = ln.m[:, :N].unsqueeze(1).broadcast_to([128, NCHUNK, N])
    AB = ln.A[:, :N].unsqueeze(1).broadcast_to([128, NCHUNK, N])
    gB = ln2g[:, :].unsqueeze(2).broadcast_to([128, NCHUNK, N])
    bB = ln2b[:, :].unsqueeze(2).broadcast_to([128, NCHUNK, N])

    def pre(t):
        x1 = x1t[t % 2]
        xb = xn2b[t % 2]
        ln.stats(None, N, x_all=x1[:, :, :], first={"sync": ldm[t]})
        b.op("vector", lambda e, x1=x1, xb=xb: e.tensor_sub(xb[:, :, :], x1[:, :, :], mB))
        b.op("vector", lambda e, xb=xb: e.tensor_mul(xb[:, :, :], xb[:, :, :], AB))
        b.op("vector", lambda e, xb=xb: e.tensor_mul(xb[:, :, :], xb[:, :, :], gB))
        b.op("vector", lambda e, xb=xb: e.tensor_add(xb[:, :, :], xb[:, :, :], bB))

    def pipe(t, start):
        x1 = x1t[t % 2]
        xb = xn2b[t % 2]
        a = ao[t % 2]
        m1 = b.pipe(32, [
            ("tensor", lambda e, i, xb=xb: b.mmg(e, ps1[i % 2][:, 0:N], [(w1[:, k, i, :], xb[:, k, :]) for k in range(8)])),
            ("scalar", lambda e, i: e.activation(
                h1b[:, i, :], ps1[i % 2][:, 0:N], AF.Silu, bias=b1[:, i:i + 1], scale=1.0)),
        ], start=start)
        b.pipe(8, [
            ("tensor", lambda e, i: b.mmg(e, ps1[i % 2][:, 0:N], [(w2[:, k, i, :], h1b[:, k, :]) for k in range(32)])),
            ("scalar", lambda e, i, a=a: e.activation(
                a[:, i, :], ps1[i % 2][:, 0:N], AF.Identity, bias=b2[:, i:i + 1], scale=1.0)),
            ("gpsimd", lambda e, i, a=a, x1=x1: e.tensor_add(a[:, i, :], a[:, i, :], x1[:, i, :])),
        ], start=mmax(start, m1))

    def post(t):
        xs = xst[t % 2]
        a = ao[t % 2]
        ln.stats(lambda c, a=a: a[:, c, :], N, x_all=a[:, :, :])
        b.op("vector", lambda e, a=a: e.tensor_sub(a[:, :, :], a[:, :, :], mB))
        b.op("vector", lambda e, a=a: e.tensor_mul(a[:, :, :], a[:, :, :], AB))
        b.op("vector", lambda e, a=a, xs=xs: e.tensor_add(a[:, :, :], a[:, :, :], xs[:, :, :]))
        b.op("scalar", lambda e, t=t, a=a: e.dma_start(
            out=d["s1"][:, :, t * N:(t + 1) * N], in_=a[:, :, :]), dma=True)
        stm[t] = b.val["sdma"]
        loadxs(t + 2)

    load(0)
    load(1)
    preSnap = {}
    pipeEnd = {}
    b.scope_push()
    pre(0)
    b.scope_pop()
    preSnap[0] = b.now()
    for t in range(NT):
        pipe(t, mmax(preSnap[t], pipeEnd.get(t - 1),
                     {"sdma": stm[t - 2]} if t - 2 in stm else None))
        pipeEnd[t] = b.now()
        if t + 1 < NT:
            b.scope_push(base=preSnap[t])
            pre(t + 1)
            b.scope_pop()
            preSnap[t + 1] = b.now()
        loadx1(t + 2)
        if t - 1 >= 0:
            b.scope_push()
            post(t - 1)
            b.scope_pop()
    b.scope_push()
    post(NT - 1)
    b.scope_pop()
    b.pop()


def emit_l2(b, cst, d):
    emit_l2a(b, cst, d)
    emit_l2b(b, cst, d)


def emit_l2a(b, cst, d):
    SC = 1.0 / 127.0
    b.push()
    wg = load_w(b, "wg_s", d["wg"], 8, 24)
    wu = load_w(b, "wu_s", d["wu"], 8, 24)
    ln = LNScratch(b, TT, cst)
    xt = [b.sbuf("xt0", [128, NCHUNK, TT], F32), b.sbuf("xt1", [128, NCHUNK, TT], F32)]
    y2b = [b.sbuf("y2b0", [128, NCHUNK, TT], BF16), b.sbuf("y2b1", [128, NCHUNK, TT], BF16)]
    sg_t = b.sbuf("sg_t", [128, 2, TT], F32)
    ht = b.sbuf("ht", [128, 24, TT], BF16)
    ps1 = b.psum_pair("ps1", [128, TT], F32)
    ps2 = b.psum_pair("ps2", [128, TT], F32)
    ldm = {}
    stm = {}
    sd0 = b.val["sdma"]

    def load(t):
        if 0 <= t < NTILE:
            b.op("sync", lambda e, t=t: e.dma_start(
                out=xt[t % 2][:, :, :], in_=d["s1"][:, :, t * TT:(t + 1) * TT]), dma=True,
                extra={"sdma": sd0})
            ldm[t] = b.val["sync"]

    mB = ln.m[:, :].unsqueeze(1).broadcast_to([128, NCHUNK, TT])
    AB = ln.A[:, :].unsqueeze(1).broadcast_to([128, NCHUNK, TT])

    def pre(t):
        x = xt[t % 2]
        yb = y2b[t % 2]
        ln.stats(None, TT, x_all=x[:, :, :])
        b.op("vector", lambda e, x=x, yb=yb: e.tensor_sub(yb[:, :, :], x[:, :, :], mB))
        b.op("vector", lambda e, yb=yb: e.tensor_mul(yb[:, :, :], yb[:, :, :], AB))

    def pipe(t, start):
        yb = y2b[t % 2]

        def l2mm(e, i, yb=yb):
            b.mmg(e, ps1[i % 2][:, :], [(wg[:, k, i, :], yb[:, k, :]) for k in range(8)])
            return b.mmg(e, ps2[i % 2][:, :], [(wu[:, k, i, :], yb[:, k, :]) for k in range(8)])
        b.pipe(24, [
            ("tensor", l2mm),
            ("scalar", lambda e, i: e.activation(
                sg_t[:, i % 2, :], ps1[i % 2][:, :], AF.Silu, bias=0.0, scale=SC)),
            ("vector", lambda e, i: e.scalar_tensor_tensor(
                ht[:, i, :], ps2[i % 2][:, :], SC, sg_t[:, i % 2, :], ALU.mult, ALU.mult)),
        ], start=start)

    def post(t):
        b.op("scalar", lambda e, t=t: e.dma_start(
            out=d["hq"][:, :, t * TT:(t + 1) * TT], in_=ht[:, :, :]), dma=True)
        stm[t] = b.val["sdma"]
        load(t + 2)


    load(0)
    load(1)
    preSnap = {}
    pipeEnd = {}
    b.scope_push()
    pre(0)
    b.scope_pop()
    preSnap[0] = b.now()
    pipe(0, preSnap[0])
    pipeEnd[0] = b.now()
    for t in range(NTILE):
        if t + 1 < NTILE:
            b.scope_push()
            pre(t + 1)
            b.scope_pop()
            preSnap[t + 1] = b.now()
        b.scope_push()
        post(t)
        b.scope_pop()
        if t + 1 < NTILE:
            pipe(t + 1, mmax(preSnap[t + 1], pipeEnd[t], {"sdma": stm[t]}))
            pipeEnd[t + 1] = b.now()
    b.pop()


def emit_l2b(b, cst, d):
    SC = 1.0 / 127.0
    b.push()
    wd = load_w(b, "wd_s", d["wd"], 24, 8)
    ln = LNScratch(b, TT, cst)
    xt = [b.sbuf("xt0", [128, NCHUNK, TT], F32), b.sbuf("xt1", [128, NCHUNK, TT], F32)]
    ht = [b.sbuf("ht0", [128, 24, TT], BF16), b.sbuf("ht1", [128, 24, TT], BF16)]
    Ft = [b.sbuf("Ft0", [128, NCHUNK, TT], F32), b.sbuf("Ft1", [128, NCHUNK, TT], F32)]
    ps1 = b.psum_pair("ps1", [128, TT], F32)
    ldm = {}
    stm = {}
    sd0 = b.val["sdma"]

    def loadxt(t):
        if 0 <= t < NTILE:
            b.op("sync", lambda e, t=t: e.dma_start(
                out=xt[t % 2][:, :, :], in_=d["s1"][:, :, t * TT:(t + 1) * TT]), dma=True,
                extra={"sdma": sd0})

    def loadht(t):
        if 0 <= t < NTILE:
            b.op("sync", lambda e, t=t: e.dma_start(
                out=ht[t % 2][:, :, :], in_=d["hq"][:, :, t * TT:(t + 1) * TT]), dma=True,
                extra={"sdma": sd0})
            ldm[t] = b.val["sync"]

    def load(t):
        loadxt(t)
        loadht(t)

    def pipe(t, start):
        h = ht[t % 2]
        f = Ft[t % 2]
        ex = {"sync": ldm[t], "gdma": b.val["gdma"]}
        if t - 2 in stm:
            ex["sdma"] = stm[t - 2]
        b.pipe(8, [
            ("tensor", lambda e, i, h=h: b.mmg(
                e, ps1[i % 2][:, :], [(wd[:, k, i, :], h[:, k, :]) for k in range(24)])),
            ("scalar", lambda e, i, f=f: e.activation(
                f[:, i, :], ps1[i % 2][:, :], AF.Identity, bias=0.0, scale=SC)),
        ], start=dict(start), extra=ex)

    mB = ln.m[:, :].unsqueeze(1).broadcast_to([128, NCHUNK, TT])
    AB = ln.A[:, :].unsqueeze(1).broadcast_to([128, NCHUNK, TT])

    def post(t):
        x = xt[t % 2]
        f = Ft[t % 2]
        ln.stats(None, TT, x_all=f[:, :, :])
        b.op("vector", lambda e, f=f: e.tensor_sub(f[:, :, :], f[:, :, :], mB))
        b.op("vector", lambda e, f=f: e.tensor_mul(f[:, :, :], f[:, :, :], AB))
        b.op("vector", lambda e, f=f, x=x: e.tensor_add(f[:, :, :], f[:, :, :], x[:, :, :]))
        b.op("scalar", lambda e, t=t, f=f: e.dma_start(
            out=d["s2"][:, :, t * TT:(t + 1) * TT], in_=f[:, :, :]), dma=True)
        stm[t] = b.val["sdma"]
        loadxt(t + 2)

    load(0)
    load(1)
    pipeEnd = {}
    postEnd = {}
    pipe(0, b.now())
    pipeEnd[0] = b.now()
    for t in range(NTILE):
        if t + 1 < NTILE:
            pipe(t + 1, mmax(pipeEnd[t], postEnd.get(t - 1)))
            pipeEnd[t + 1] = b.now()
            loadht(t + 2)
        b.scope_push()
        post(t)
        b.scope_pop()
        postEnd[t] = b.now()
    b.pop()


def emit_l3(b, cst, d):
    b.push()
    wk = load_w(b, "wk_s", d["wk"], 8, 8)
    wv = load_w(b, "wv_s", d["wv"], 8, 8)
    wr = load_w(b, "wr_s", d["wr"], 8, 8)
    wo = load_w(b, "wo_s", d["wo"], 8, 8)
    lnw = load_cols(b, "lnw_s", d["lnw"], 8)
    lnb = load_cols(b, "lnb_s", d["lnb"], 8)
    tmk = load_cols(b, "tmk_s", d["tmk"], 8)
    tmv = load_cols(b, "tmv_s", d["tmv"], 8)
    tmr = load_cols(b, "tmr_s", d["tmr"], 8)
    dw = load_cols(b, "dw_s", d["dw"], 8)
    flag = load_cols(b, "flag_s", d["flag"], 1)
    dec = b.sbuf("dec", [128, 8], F32)
    b.op("scalar", lambda e: e.activation(dec[:, :], dw[:, :], AF.Sigmoid, bias=0.0, scale=1.0),
         extra={"sync": b.val["sync"]})
    omk = b.sbuf("omk", [128, 8], F32)
    omv = b.sbuf("omv", [128, 8], F32)
    omr = b.sbuf("omr", [128, 8], F32)
    for om, tm in ((omk, tmk), (omv, tmv), (omr, tmr)):
        b.op("vector", lambda e, om=om, tm=tm: e.tensor_scalar(
            om[:, :], tm[:, :], -1.0, 1.0, ALU.mult, ALU.add))
    ln = LNScratch(b, TT, cst)
    xt = [b.sbuf("xt0", [128, NCHUNK, TT], F32), b.sbuf("xt1", [128, NCHUNK, TT], F32)]
    y3 = b.sbuf("y3", [128, NCHUNK, TT], F32)
    xn = b.sbuf("xn", [128, NCHUNK, TT + 1], F32)
    mixk = b.sbuf("mixk", [128, NCHUNK, TT], BF16)
    mixv = b.sbuf("mixv", [128, NCHUNK, TT], BF16)
    mixr = b.sbuf("mixr", [128, NCHUNK, TT], BF16)
    kf = b.sbuf("kf", [128, 2, TT], F32)
    kv = b.sbuf("kv", [128, 2, TT], F32)
    ssq = b.sbuf("ssq", [128, 2, TT], F32)
    ssb = b.sbuf("ssb", [128, NCHUNK, TT], BF16)
    rf = b.sbuf("rf", [128, NCHUNK, TT], F32)
    scol = b.sbuf("scol", [128, 8], F32)
    xcar = b.sbuf("xcar", [128, 8], F32)
    psK = b.psum_pair("psK", [128, TT], F32)
    psV = b.psum_pair("psV", [128, TT], F32)
    psR = b.psum_pair("psR", [128, TT], F32)
    b.op("gpsimd", lambda e: e.memset(scol[:, :], 0.0),
         extra={"sdma": b.val["sdma"], "sync": b.val["sync"]})
    b.op("gpsimd", lambda e: e.memset(xcar[:, :], 0.0))
    ldm = {}
    stm = {}

    sd0 = b.val["sdma"]

    def load(t):
        if 0 <= t < NTILE:
            b.op("sync", lambda e, t=t: e.dma_start(
                out=xt[t % 2][:, :, :], in_=d["s2"][:, :, t * TT:(t + 1) * TT]), dma=True,
                extra={"sdma": sd0})
            ldm[t] = b.val["sync"]
    mB = ln.m[:, :].unsqueeze(1).broadcast_to([128, NCHUNK, TT])
    AB = ln.A[:, :].unsqueeze(1).broadcast_to([128, NCHUNK, TT])
    gB = lnw[:, :].unsqueeze(2).broadcast_to([128, NCHUNK, TT])
    bB = lnb[:, :].unsqueeze(2).broadcast_to([128, NCHUNK, TT])
    A2B = None
    load(0)
    for t in range(NTILE):
        load(t + 1)
        x = xt[t % 2]
        ln.stats(None, TT, comp=True, first={"sync": ldm[t]}, x_all=x[:, :, :])
        if A2B is None:
            A2B = ln.A2[:, :].unsqueeze(1).broadcast_to([128, NCHUNK, TT])
        b.op("vector", lambda e, x=x: e.tensor_sub(y3[:, :, :], x[:, :, :], mB),
             extra=({"sdma": stm[t - 1]} if t - 1 in stm else None))
        b.op("vector", lambda e: e.tensor_mul(xn[:, :, 1:TT + 1], y3[:, :, :], A2B))
        b.op("vector", lambda e: e.tensor_mul(xn[:, :, 1:TT + 1], xn[:, :, 1:TT + 1], gB))
        b.op("vector", lambda e: e.tensor_add(xn[:, :, 1:TT + 1], xn[:, :, 1:TT + 1], bB))
        b.op("gpsimd", lambda e: e.tensor_mul(y3[:, :, :], y3[:, :, :], AB))
        b.op("vector", lambda e: e.tensor_copy(xn[:, :, 0:1], xcar[:, :].unsqueeze(2)))
        b.op("vector", lambda e: e.tensor_copy(xcar[:, :], xn[:, :, TT:TT + 1].squeeze(2)))
        if t == 0:
            b.op("vector", lambda e: e.tensor_mul(
                xcar[:, :], xcar[:, :], flag[:, 0:1].broadcast_to([128, 8])))
        for mix, tm, om, meng in ((mixk, tmk, omk, "vector"), (mixv, tmv, omv, "gpsimd"),
                                  (mixr, tmr, omr, "vector")):
            tmB = tm[:, :].unsqueeze(2).broadcast_to([128, NCHUNK, TT])
            omB = om[:, :].unsqueeze(2).broadcast_to([128, NCHUNK, TT])
            sc = rf if meng == "vector" else ssb
            b.op(meng, lambda e, sc=sc, tmB=tmB: e.tensor_mul(sc[:, :, :], xn[:, :, 1:TT + 1], tmB))
            b.op(meng, lambda e, mix=mix, omB=omB: e.tensor_mul(mix[:, :, :], xn[:, :, 0:TT], omB))
            b.op(meng, lambda e, mix=mix, sc=sc: e.tensor_add(mix[:, :, :], mix[:, :, :], sc[:, :, :]))
        b.pipe(8, [
            ("tensor", lambda e, i: b.mmg(e, psK[i % 2][:, :], [(wk[:, k, i, :], mixk[:, k, :]) for k in range(8)])),
            ("scalar", lambda e, i: e.activation(kf[:, i % 2, :], psK[i % 2][:, :], AF.Copy, bias=0.0, scale=1.0)),
            ("tensor", lambda e, i: b.mmg(e, psV[i % 2][:, :], [(wv[:, k, i, :], mixv[:, k, :]) for k in range(8)])),
            ("vector", lambda e, i: e.tensor_mul(kv[:, i % 2, :], kf[:, i % 2, :], psV[i % 2][:, :])),
            ("vector", lambda e, i: e.tensor_tensor_scan(
                ssq[:, i % 2, :], dec[:, i:i + 1].broadcast_to([128, TT]), kv[:, i % 2, :],
                scol[:, i:i + 1], ALU.mult, ALU.add)),
            ("vector", lambda e, i: e.tensor_copy(scol[:, i:i + 1], ssq[:, i % 2, TT - 1:TT])),
            ("scalar", lambda e, i: e.activation(ssb[:, i, :], ssq[:, i % 2, :], AF.Copy, bias=0.0, scale=1.0)),
            ("tensor", lambda e, i: b.mmg(e, psR[i % 2][:, :], [(wr[:, k, i, :], mixr[:, k, :]) for k in range(8)])),
            ("scalar", lambda e, i: e.activation(rf[:, i, :], psR[i % 2][:, :], AF.Sigmoid, bias=0.0, scale=1.0)),
        ])
        if t == 0:
            b.op("vector", lambda e: e.tensor_mul(
                scol[:, :], scol[:, :], flag[:, 0:1].broadcast_to([128, 8])))
        b.pipe(8, [
            ("tensor", lambda e, i: b.mmg(e, psK[i % 2][:, :], [(wo[:, k, i, :], ssb[:, k, :]) for k in range(8)])),
            ("vector", lambda e, i: e.tensor_mul(rf[:, i, :], rf[:, i, :], psK[i % 2][:, :])),
            ("vector", lambda e, i: e.tensor_add(y3[:, i, :], y3[:, i, :], rf[:, i, :])),
        ])
        ln.stats(None, TT, x_all=y3[:, :, :])
        b.op("vector", lambda e: e.tensor_sub(y3[:, :, :], y3[:, :, :], mB))
        b.op("vector", lambda e: e.tensor_mul(y3[:, :, :], y3[:, :, :], AB))
        b.op("vector", lambda e, x=x: e.tensor_add(y3[:, :, :], y3[:, :, :], x[:, :, :]))
        b.op("scalar", lambda e, t=t: e.dma_start(out=d["s3"][:, :, t * TT:(t + 1) * TT], in_=y3[:, :, :]), dma=True)
        stm[t] = b.val["sdma"]
    b.pop()


def emit_l4(b, cst, d):
    b.push()
    u1 = load_w(b, "u1_s", d["u1"], 8, 2)
    v1 = load_w(b, "v1_s", d["v1"], 2, 32)
    u2 = load_w(b, "u2_s", d["u2"], 32, 2)
    v2 = load_w(b, "v2_s", d["v2"], 2, 8)
    b1 = load_cols(b, "b1_s", d["fb1"], 32)
    b2 = load_cols(b, "b2_s", d["fb2"], 8)
    ln = LNScratch(b, TT, cst)
    lnP = LNScratch(b, TT, cst)
    xt = [b.sbuf("xt0", [128, NCHUNK, TT], F32), b.sbuf("xt1", [128, NCHUNK, TT], F32),
          b.sbuf("xt2", [128, NCHUNK, TT], F32)]
    y4b = [b.sbuf("y4b0", [128, NCHUNK, TT], BF16), b.sbuf("y4b1", [128, NCHUNK, TT], BF16)]
    pb = b.sbuf("pb", [128, 2, TT], BF16)
    hb = b.sbuf("hb", [128, 32, TT], BF16)
    qb = b.sbuf("qb", [128, 2, TT], BF16)
    G = [b.sbuf("G40", [128, NCHUNK, TT], F32), b.sbuf("G41", [128, NCHUNK, TT], F32)]
    ps1 = b.psum_pair("ps41", [128, TT], F32)
    ldm = {}
    stm = {}
    sd0 = b.val["sdma"]

    def load(t):
        if 1 <= t < NTILE:
            b.op("sync", lambda e, t=t: e.dma_start(
                out=xt[t % 3][:, :, :], in_=d["s3"][:, :, t * TT:(t + 1) * TT]), dma=True,
                extra={"sdma": sd0})
            ldm[t] = b.val["sync"]

    mB = ln.m[:, :].unsqueeze(1).broadcast_to([128, NCHUNK, TT])
    AB = ln.A[:, :].unsqueeze(1).broadcast_to([128, NCHUNK, TT])
    mPB = lnP.m[:, :].unsqueeze(1).broadcast_to([128, NCHUNK, TT])
    APB = lnP.A[:, :].unsqueeze(1).broadcast_to([128, NCHUNK, TT])

    def pre(t):
        x = xt[t % 3]
        yb = y4b[t % 2]
        ln.stats(None, TT, x_all=x[:, :, :])
        b.op("vector", lambda e, x=x, yb=yb: e.tensor_sub(yb[:, :, :], x[:, :, :], mB))
        b.op("vector", lambda e, yb=yb: e.tensor_mul(yb[:, :, :], yb[:, :, :], AB))

    def pipe(t, start):
        yb = y4b[t % 2]
        g = G[t % 2]
        m1 = b.pipe(2, [
            ("tensor", lambda e, i, yb=yb: b.mmg(e, ps1[i % 2][:, :], [(u1[:, k, i, :], yb[:, k, :]) for k in range(8)])),
            ("scalar", lambda e, i: e.activation(pb[:, i, :], ps1[i % 2][:, :], AF.Copy, bias=0.0, scale=1.0)),
        ], start=start)
        m2 = b.pipe(32, [
            ("tensor", lambda e, i: b.mmg(e, ps1[i % 2][:, :], [(v1[:, k, i, :], pb[:, k, :]) for k in range(2)])),
            ("scalar", lambda e, i: e.activation(
                hb[:, i, :], ps1[i % 2][:, :], AF.Gelu_apprx_tanh, bias=b1[:, i:i + 1], scale=1.0)),
        ], start=mmax(start, m1))
        m3 = b.pipe(2, [
            ("tensor", lambda e, i: b.mmg(e, ps1[i % 2][:, :], [(u2[:, k, i, :], hb[:, k, :]) for k in range(32)])),
            ("scalar", lambda e, i: e.activation(qb[:, i, :], ps1[i % 2][:, :], AF.Copy, bias=0.0, scale=1.0)),
        ], start=mmax(start, m2))
        b.pipe(8, [
            ("tensor", lambda e, i: b.mmg(e, ps1[i % 2][:, :], [(v2[:, k, i, :], qb[:, k, :]) for k in range(2)])),
            ("scalar", lambda e, i, g=g: e.activation(
                g[:, i, :], ps1[i % 2][:, :], AF.Identity, bias=b2[:, i:i + 1], scale=1.0)),
        ], start=mmax(start, m3))

    def post(t):
        x = xt[t % 3]
        g = G[t % 2]
        lnP.stats(None, TT, x_all=g[:, :, :])
        b.op("vector", lambda e, g=g: e.tensor_sub(g[:, :, :], g[:, :, :], mPB))
        b.op("vector", lambda e, g=g: e.tensor_mul(g[:, :, :], g[:, :, :], APB))
        b.op("vector", lambda e, g=g, x=x: e.tensor_add(g[:, :, :], g[:, :, :], x[:, :, :]))
        b.op("scalar", lambda e, t=t, g=g: e.dma_start(
            out=d["out"][:, :, (t - 1) * TT:t * TT], in_=g[:, :, :]), dma=True)
        stm[t] = b.val["sdma"]
        load(t + 3)

    load(1)
    load(2)
    load(3)
    preSnap = {}
    pipeEnd = {}
    postEnd = {}
    b.scope_push()
    pre(1)
    b.scope_pop()
    preSnap[1] = b.now()
    for t in range(1, NTILE):
        pipe(t, mmax(preSnap[t], pipeEnd.get(t - 1), postEnd.get(t - 2),
                     {"sdma": stm[t - 2]} if t - 2 in stm else None))
        pipeEnd[t] = b.now()
        if t + 1 < NTILE:
            b.scope_push(base=mmax(preSnap[t], postEnd.get(t - 2)))
            pre(t + 1)
            b.scope_pop()
            preSnap[t + 1] = b.now()
        if t - 1 >= 1:
            b.scope_push()
            post(t - 1)
            b.scope_pop()
            postEnd[t - 1] = b.now()
    b.scope_push()
    post(NTILE - 1)
    b.scope_pop()
    b.pop()



def build_fused():
    b = B()
    d = {}
    d["xs"] = b.dram_in("xs", [128, NCHUNK, TW])
    d["flag"] = b.dram_in("flag", [128, 1])
    d["w_in_r"] = b.dram_in("w_in_r", [128, 8, 64])
    d["w_in_i"] = b.dram_in("w_in_i", [128, 8, 64])
    d["w_gate"] = b.dram_in("w_gate", [128, 8, 8, 128])
    d["w_out_r"] = b.dram_in("w_out_r", [64, 8, 128])
    d["w_out_i"] = b.dram_in("w_out_i", [64, 8, 128])
    d["coupT"] = b.dram_in("coupT", [4, 16, 16])
    d["b_gate"] = b.dram_in("b_gate", [128, 8])
    d["ln1g"] = b.dram_in("ln1g", [128, 8])
    d["ln1b"] = b.dram_in("ln1b", [128, 8])
    d["ldecay"] = b.dram_in("ldecay", [64, 1])
    d["freq"] = b.dram_in("freq", [64, 1])
    d["w_mlp1"] = b.dram_in("w_mlp1", [128, 8, 32, 128])
    d["w_mlp2"] = b.dram_in("w_mlp2", [128, 32, 8, 128])
    d["b_mlp1"] = b.dram_in("b_mlp1", [128, 32])
    d["b_mlp2"] = b.dram_in("b_mlp2", [128, 8])
    d["ln2g"] = b.dram_in("ln2g", [128, 8])
    d["ln2b"] = b.dram_in("ln2b", [128, 8])
    d["wg"] = b.dram_in("wg", [128, 8, 24, 128], I32)
    d["wu"] = b.dram_in("wu", [128, 8, 24, 128], I32)
    d["wd"] = b.dram_in("wd", [128, 24, 8, 128], I32)
    d["wk"] = b.dram_in("wk", [128, 8, 8, 128])
    d["wv"] = b.dram_in("wv", [128, 8, 8, 128])
    d["wr"] = b.dram_in("wr", [128, 8, 8, 128])
    d["wo"] = b.dram_in("wo", [128, 8, 8, 128])
    d["lnw"] = b.dram_in("lnw", [128, 8])
    d["lnb"] = b.dram_in("lnb", [128, 8])
    d["tmk"] = b.dram_in("tmk", [128, 8])
    d["tmv"] = b.dram_in("tmv", [128, 8])
    d["tmr"] = b.dram_in("tmr", [128, 8])
    d["dw"] = b.dram_in("dw", [128, 8])
    d["u1"] = b.dram_in("u1", [128, 8, 2, 128])
    d["v1"] = b.dram_in("v1", [128, 2, 32, 128])
    d["u2"] = b.dram_in("u2", [128, 32, 2, 128])
    d["v2"] = b.dram_in("v2", [128, 2, 8, 128])
    d["fb1"] = b.dram_in("fb1", [128, 32])
    d["fb2"] = b.dram_in("fb2", [128, 8])
    d["out"] = b.dram_out("out", [128, NCHUNK, TW - TT])
    d["x1"] = b.dram_tmp("x1_t", [128, NCHUNK, TW])
    d["s1"] = b.dram_tmp("s1_t", [128, NCHUNK, TW])
    d["hq"] = b.dram_tmp("hq_t", [128, 24, TW], BF16)
    d["s2"] = b.dram_tmp("s2_t", [128, NCHUNK, TW])
    d["s3"] = b.dram_tmp("s3_t", [128, NCHUNK, TW])
    b.begin()
    cst = emit_consts(b)
    emit_l1a(b, cst, d)
    emit_l1b(b, cst, d)
    emit_l2(b, cst, d)
    emit_l3(b, cst, d)
    emit_l4(b, cst, d)
    return b.end()



_NCS = {}


def _get_nc(name, fn):
    if name not in _NCS:
        _NCS[name] = fn()
    return _NCS[name]


def kernel(**ins):
    x = np.asarray(ins["L0_attn_x"])
    B_, T_, D_ = x.shape
    H = T_ // 2

    xs_cores = []
    flags = []
    for b_i in range(B_):
        for h in range(2):
            main = x[b_i, h * H:(h + 1) * H, :]
            if h == 1:
                wu = x[b_i, H - TT:H, :]
                fl = np.ones((128, 1), np.float32)
            else:
                wu = np.zeros((TT, D_), np.float32)
                fl = np.zeros((128, 1), np.float32)
            slab = np.concatenate([wu, main], axis=0)
            xs_cores.append(stage_stream(slab))
            flags.append(fl)

    stat = dict(
        w_in_r=stage_w64_out(np.asarray(ins["L0_attn_w_in"])[0:64, :]),
        w_in_i=stage_w64_out(np.asarray(ins["L0_attn_w_in"])[64:128, :]),
        w_gate=stage_w(ins["L0_attn_w_gate"]),
        w_out_r=stage_w64_in(np.asarray(ins["L0_attn_w_out"])[:, 0:64]),
        w_out_i=stage_w64_in(np.asarray(ins["L0_attn_w_out"])[:, 64:128]),
        coupT=np.ascontiguousarray(np.asarray(ins["L0_attn_coupling"]).transpose(0, 2, 1)),
        b_gate=stage_cols(ins["L0_attn_b_gate"]),
        ln1g=stage_cols(ins["L0_attn_ln1_g"]),
        ln1b=stage_cols(ins["L0_attn_ln1_b"]),
        ldecay=np.ascontiguousarray(np.asarray(ins["L0_attn_log_decay"]).reshape(64, 1)),
        freq=np.ascontiguousarray(np.asarray(ins["L0_attn_frequency"]).reshape(64, 1)),
        w_mlp1=stage_w(ins["L0_attn_w_mlp1"]),
        w_mlp2=stage_w(ins["L0_attn_w_mlp2"]),
        b_mlp1=stage_cols(ins["L0_attn_b_mlp1"]),
        b_mlp2=stage_cols(ins["L0_attn_b_mlp2"]),
        ln2g=stage_cols(ins["L0_attn_ln2_g"]),
        ln2b=stage_cols(ins["L0_attn_ln2_b"]),
        wg=stage_w(ins["L0_ffn_w_gate"]),
        wu=stage_w(ins["L0_ffn_w_up"]),
        wd=stage_w(ins["L0_ffn_w_down"]),
        wk=stage_w(ins["L1_attn_Wk"]),
        wv=stage_w(ins["L1_attn_Wv"]),
        wr=stage_w(ins["L1_attn_Wr"]),
        wo=stage_w(ins["L1_attn_Wo"]),
        lnw=stage_cols(ins["L1_attn_ln_w"]),
        lnb=stage_cols(ins["L1_attn_ln_b"]),
        tmk=stage_cols(ins["L1_attn_tmk"]),
        tmv=stage_cols(ins["L1_attn_tmv"]),
        tmr=stage_cols(ins["L1_attn_tmr"]),
        dw=stage_cols(ins["L1_attn_decay_w"]),
        u1=stage_w(np.asarray(ins["L1_ffn_U1"]).T),
        v1=stage_w(np.asarray(ins["L1_ffn_V1"]).T),
        u2=stage_w(np.asarray(ins["L1_ffn_U2"]).T),
        v2=stage_w(np.asarray(ins["L1_ffn_V2"]).T),
        fb1=stage_cols(ins["L1_ffn_b1"]),
        fb2=stage_cols(ins["L1_ffn_b2"]),
    )

    nc = _get_nc("fused", build_fused)
    maps = [dict(stat, xs=xs_cores[i], flag=flags[i]) for i in range(8)]
    r = run_bass_kernel_spmd(nc, maps, core_ids=list(range(8)))

    out = np.empty((B_, T_, D_), np.float32)
    for b_i in range(B_):
        for h in range(2):
            arr = np.asarray(r.results[b_i * 2 + h]["out"])
            out[b_i, h * H:(h + 1) * H, :] = unstage_stream(arr)
    return out


# version 17
# speedup vs baseline: 1.1303x; 1.0312x over previous
import math
import contextlib
import numpy as np
import concourse.bass as bass
import concourse.mybir as mybir
from concourse.bass_utils import run_bass_kernel_spmd

F32 = mybir.dt.float32
BF16 = mybir.dt.bfloat16
I32 = mybir.dt.int32
AF = mybir.ActivationFunctionType
ALU = mybir.AluOpType

D = 1024
NCHUNK = 8
TT = 512
NTILE = 9
TW = TT * NTILE
EPS = 1e-5
C0 = (1.0 + EPS) ** -0.5
SEG = 16
CH = 64
PI2_HI = float(np.float32(math.pi / 2))
PI2_LO = math.pi / 2 - PI2_HI



def stage_w(W):
    W = np.ascontiguousarray(W)
    oc = W.shape[0] // 128
    ic = W.shape[1] // 128
    t = W.reshape(oc, 128, ic, 128).transpose(3, 2, 0, 1)
    return np.ascontiguousarray(t)


def stage_w64_out(W):
    W = np.ascontiguousarray(W)
    ic = W.shape[1] // 128
    return np.ascontiguousarray(W.reshape(64, ic, 128).transpose(2, 1, 0))


def stage_w64_in(W):
    W = np.ascontiguousarray(W)
    oc = W.shape[0] // 128
    return np.ascontiguousarray(W.reshape(oc, 128, 64).transpose(2, 0, 1))


def stage_cols(v):
    return np.ascontiguousarray(np.asarray(v).reshape(-1, 128).T)


def stage_stream(x_slab):
    T = x_slab.shape[0]
    t = x_slab.T.reshape(NCHUNK, 128, T)
    return np.ascontiguousarray(t.transpose(1, 0, 2))


def unstage_stream(arr):
    return np.ascontiguousarray(arr.transpose(1, 0, 2).reshape(D, -1).T)



ENGS = ("tensor", "vector", "scalar", "gpsimd", "sync", "gdma", "sdma")
COMPUTE_LANES = ("tensor", "vector", "scalar", "gpsimd", "gdma")


class B:
    def __init__(self):
        self.nc = bass.Bass("TRN2", target_bir_lowering=False, debug=False)
        self.stack = contextlib.ExitStack()
        self.sects = []
        self._blk = None
        self.sems = {}
        self.val = {e: 0 for e in ENGS}
        self.scopes = []
        self.uid = 0

    def dram_in(self, name, shape, dtype=F32):
        return self.nc.dram_tensor(name, shape, dtype, kind="ExternalInput").ap()

    def dram_out(self, name, shape, dtype=F32):
        return self.nc.dram_tensor(name, shape, dtype, kind="ExternalOutput").ap()

    def dram_tmp(self, name, shape, dtype=F32):
        return self.nc.dram_tensor(name, shape, dtype, kind="Internal").ap()

    def _st(self):
        return self.sects[-1] if self.sects else self.stack

    def sbuf(self, name, shape, dtype=F32):
        self.uid += 1
        return self._st().enter_context(
            self.nc.sbuf_tensor(f"{name}_{self.uid}", shape, dtype))

    def psum_pair(self, name, shape, dtype=F32):
        return [self.psum(name + "a", shape, dtype), self.psum(name + "b", shape, dtype)]

    def psum(self, name, shape, dtype=F32):
        self.uid += 1
        shape = list(shape)
        if dtype == F32 and shape[-1] < 512:
            shape[-1] = 512
        return self._st().enter_context(
            self.nc.psum_tensor(f"{name}_{self.uid}", shape, dtype))

    def scope_push(self, base=None):
        self.scopes.append({"base": dict(self.val) if base is None else dict(base), "seen": {}})

    def scope_pop(self):
        self.scopes.pop()

    def push(self):
        self.sects.append(contextlib.ExitStack())

    def pop(self):
        self.sects.pop().close()

    def begin(self):
        for e in ENGS:
            self.sems[e] = self.stack.enter_context(self.nc.semaphore(f"sem_{e}"))
        self._blk = self.stack.enter_context(self.nc.Block())
        self.bodies = {e: [] for e in ("tensor", "vector", "scalar", "gpsimd", "sync")}

    def now(self):
        return dict(self.val)

    def op(self, eng, fn, dma=False, after=None, extra=None):
        key = eng
        if dma:
            key = {"gpsimd": "gdma", "scalar": "sdma"}.get(eng, eng)
        if after is None:
            if self.scopes:
                sc = self.scopes[-1]
                after = dict(sc["base"])
                for l2, v2 in sc["seen"].items():
                    after[l2] = max(after.get(l2, 0), v2)
            else:
                after = {e: self.val[e] for e in COMPUTE_LANES}
        if extra:
            after = dict(after)
            for k2, v2 in extra.items():
                after[k2] = max(after.get(k2, 0), v2)
        if dma:
            after = dict(after)
            after[key] = max(after.get(key, 0), self.val[key])
        waits = [(self.sems[e], v) for e, v in after.items() if v > 0]
        inc = 16 if dma else 1
        sem = self.sems[key]

        def body(e):
            for smh, v in waits:
                e.wait_ge(smh, v)
            ins = fn(e)
            ins.then_inc(sem, inc)
        self.bodies[eng].append(body)
        self.val[key] += inc
        if self.scopes:
            self.scopes[-1]["seen"][key] = self.val[key]

    def mm(self, ps_ap, pairs, after=None, extra=None):
        def f(e):
            last = None
            n = len(pairs)
            for i, (l, r) in enumerate(pairs):
                last = e.matmul(ps_ap, l, r, start=(i == 0), stop=(i == n - 1))
            return last
        self.op("tensor", f, after=after, extra=extra)

    def pipe(self, n, stages, guard=2, extra=None, start=None):
        if start is None:
            start = {e: self.val[e] for e in COMPUTE_LANES}
        else:
            start = dict(start)
        if extra:
            for k2, v2 in extra.items():
                start[k2] = max(start.get(k2, 0), v2)
        last_eng = stages[-1][0]
        last_marks = []
        s0_mark = None
        touched = {}
        for i in range(n):
            prev = None
            for k, (eng, fn) in enumerate(stages):
                after = {}
                if i == 0:
                    after.update(start)
                if k == 0:
                    if i >= guard:
                        after[last_eng] = max(after.get(last_eng, 0), last_marks[i - guard])
                    elif i == 1:
                        after[stages[0][0]] = max(after.get(stages[0][0], 0), s0_mark)
                else:
                    pe, pv = prev
                    if pe == eng:
                        after[eng] = max(after.get(eng, 0), pv)
                    else:
                        after[pe] = max(after.get(pe, 0), pv)
                self.op(eng, lambda e, fn=fn, i=i: fn(e, i), after=after)
                prev = (eng, self.val[eng])
                touched[eng] = self.val[eng]
                if k == 0:
                    s0_mark = self.val[eng]
            last_marks.append(self.val[last_eng])
        return touched

    def mmg(self, e, ps_ap, pairs):
        last = None
        n = len(pairs)
        for i, (l, r) in enumerate(pairs):
            last = e.matmul(ps_ap, l, r, start=(i == 0), stop=(i == n - 1))
        return last

    def end(self):
        vals = self.now()
        for eng, lst in self.bodies.items():
            if not lst:
                continue

            def mega(e, lst=lst):
                for f in lst:
                    f(e)
            getattr(self._blk, eng)(mega)

        def fin(e):
            for en in ENGS:
                if vals[en]:
                    e.wait_ge(self.sems[en], vals[en])
        self._blk.sync(fin)
        while self.sects:
            self.pop()
        self.stack.close()
        return self.nc


def mmax(*ds):
    out = {}
    for d_ in ds:
        if not d_:
            continue
        for k, v in d_.items():
            out[k] = max(out.get(k, 0), v)
    return out



def emit_consts(b):
    cst = {}
    t = b.sbuf("c_eps", [128, 1], F32)
    b.op("gpsimd", lambda e: e.memset(t[:, :], EPS))
    cst["eps"] = t
    t2 = b.sbuf("c_eps2", [128, 1], F32)
    b.op("gpsimd", lambda e: e.memset(t2[:, :], EPS * EPS))
    cst["eps2"] = t2
    ones = b.sbuf("c_ones", [128, 128], BF16)
    b.op("gpsimd", lambda e: e.memset(ones[:, :], 1.0))
    cst["ones"] = ones
    onesf = b.sbuf("c_onesf", [128, 128], F32)
    b.op("gpsimd", lambda e: e.memset(onesf[:, :], 1.0))
    cst["onesf"] = onesf
    return cst


def load_w(b, name, dram_ap, ic, oc, dtype=BF16):
    w = b.sbuf(name, [128, ic, oc, 128], dtype)
    b.op("gpsimd", lambda e: e.dma_start(out=w[:, :, :, :], in_=dram_ap[:, :, :, :]),
         dma=True, extra={"sdma": b.val["sdma"], "sync": b.val["sync"]})
    return w


def load_cols(b, name, dram_ap, n, dtype=F32):
    t = b.sbuf(name, [128, n], dtype)
    b.op("gpsimd", lambda e: e.dma_start(out=t[:, :], in_=dram_ap[:, :]), dma=True,
         extra={"sdma": b.val["sdma"], "sync": b.val["sync"]})
    return t


class LNScratch:
    def __init__(self, b, n, cst, alloc_xb=True):
        self.b = b
        self.cst = cst
        self.nn = n
        self.xb = b.sbuf("ln_xb", [128, NCHUNK, n], BF16) if alloc_xb else None
        self.m = b.sbuf("ln_m", [128, n], F32)
        self.A = b.sbuf("ln_A", [128, n], F32)
        self.t1 = b.sbuf("ln_t1", [128, n], F32)
        self.t2 = None
        self.A2 = None
        self.psS = b.psum("ln_psS", [128, n], F32)
        self.psQ = b.psum("ln_psQ", [128, n], F32)

    def mkcomp(self, n):
        if self.A2 is None:
            self.t2 = self.b.sbuf("ln_t2", [128, self.nn], F32)
            self.A2 = self.b.sbuf("ln_A2", [128, self.nn], F32)

    def statsA_f32(self, n, xten, sq, nch=NCHUNK, first=None):
        b, cst = self.b, self.cst
        b.mm(self.psS[:, :n], [(cst["onesf"][:, :], xten[:, c, :n]) for c in range(nch)],
             extra=first)
        psS_mark = b.val["tensor"]
        b.op("scalar", lambda e: e.activation(
            sq[:, 0:nch, :n], xten[:, 0:nch, :n], AF.Square, bias=0.0, scale=1.0),
            after={"scalar": b.val["scalar"], "tensor": psS_mark}, extra=first)
        b.mm(self.psQ[:, :n], [(cst["ones"][:, :], sq[:, c, :n]) for c in range(nch)])

    def statsA(self, n, x_all, nch=NCHUNK, first=None):
        b, cst = self.b, self.cst
        b.op("scalar", lambda e: e.activation(
            self.xb[:, 0:nch, :n], x_all, AF.Copy, bias=0.0, scale=1.0), extra=first)
        b.mm(self.psS[:, :n], [(cst["ones"][:, :], self.xb[:, c, :n]) for c in range(nch)])
        psS_mark = b.val["tensor"]
        b.op("scalar", lambda e: e.activation(
            self.xb[:, 0:nch, :n], self.xb[:, 0:nch, :n], AF.Square, bias=0.0, scale=1.0),
            after={"scalar": b.val["scalar"], "tensor": psS_mark})
        b.mm(self.psQ[:, :n], [(cst["ones"][:, :], self.xb[:, c, :n]) for c in range(nch)])

    def statsB(self, n, nch=NCHUNK, comp=False):
        b, cst = self.b, self.cst
        if comp:
            self.mkcomp(n)
        inv = 1.0 / (128 * nch)
        b.op("vector", lambda e: e.tensor_scalar(self.m[:, :n], self.psS[:, :n], inv, None, ALU.mult))
        b.op("vector", lambda e: e.tensor_mul(self.t1[:, :n], self.m[:, :n], self.m[:, :n]))
        b.op("vector", lambda e: e.scalar_tensor_tensor(
            self.t1[:, :n], self.psQ[:, :n], inv, self.t1[:, :n], ALU.mult, ALU.subtract))
        if comp:
            b.op("scalar", lambda e: e.activation(
                self.t2[:, :n], self.t1[:, :n], AF.Sqrt, bias=cst["eps2"][:, 0:1],
                scale=1.0 + EPS))
            b.op("vector", lambda e: e.reciprocal(self.A2[:, :n], self.t2[:, :n]))
        b.op("scalar", lambda e: e.activation(
            self.t1[:, :n], self.t1[:, :n], AF.Sqrt, bias=cst["eps"][:, 0:1], scale=1.0))
        b.op("vector", lambda e: e.reciprocal(self.A[:, :n], self.t1[:, :n]))

    def stats(self, x_chunks, n, nch=NCHUNK, comp=False, first=None, x_all=None):
        self.statsA(n, x_all, nch=nch, first=first)
        self.statsB(n, nch=nch, comp=comp)



def emit_eig_tables(b, cst, ld_col, fr_col):
    eg = {}
    mag = b.sbuf("eg_mag", [64, 1], F32)
    b.op("scalar", lambda e: e.activation(mag[:, :], ld_col[:, :], AF.Sigmoid, bias=0.0, scale=1.0))
    s2 = b.sbuf("eg_s2", [64, 1], F32)
    b.op("scalar", lambda e: e.activation(s2[:, :], fr_col[:, :], AF.Sin, bias=0.0, scale=0.5))
    u = b.sbuf("eg_u", [64, 1], F32)
    b.op("vector", lambda e: e.tensor_scalar(u[:, :], fr_col[:, :], -0.5, PI2_HI, ALU.mult, ALU.add))
    b.op("vector", lambda e: e.tensor_scalar(u[:, :], u[:, :], PI2_LO, None, ALU.add))
    c2 = b.sbuf("eg_c2", [64, 1], F32)
    b.op("scalar", lambda e: e.activation(c2[:, :], u[:, :], AF.Sin, bias=0.0, scale=1.0))
    sth = b.sbuf("eg_sth", [64, 1], F32)
    b.op("vector", lambda e: e.tensor_mul(sth[:, :], s2[:, :], c2[:, :]))
    b.op("vector", lambda e: e.tensor_scalar(sth[:, :], sth[:, :], 2.0, None, ALU.mult))
    cth = b.sbuf("eg_cth", [64, 1], F32)
    b.op("vector", lambda e: e.tensor_mul(cth[:, :], s2[:, :], s2[:, :]))
    b.op("vector", lambda e: e.tensor_scalar(cth[:, :], cth[:, :], -2.0, 1.0, ALU.mult, ALU.add))
    a = b.sbuf("eg_a", [64, 1], F32)
    b.op("vector", lambda e: e.tensor_mul(a[:, :], cth[:, :], mag[:, :]))
    bb = b.sbuf("eg_b", [64, 1], F32)
    b.op("vector", lambda e: e.tensor_mul(bb[:, :], sth[:, :], mag[:, :]))
    rmag = b.sbuf("eg_rmag", [64, 1], F32)
    b.op("vector", lambda e: e.reciprocal(rmag[:, :], mag[:, :]))
    ia = b.sbuf("eg_ia", [64, 1], F32)
    b.op("vector", lambda e: e.tensor_mul(ia[:, :], cth[:, :], rmag[:, :]))
    ib = b.sbuf("eg_ib", [64, 1], F32)
    b.op("vector", lambda e: e.tensor_scalar(ib[:, :], sth[:, :], -1.0, None, ALU.mult))
    b.op("vector", lambda e: e.tensor_mul(ib[:, :], ib[:, :], rmag[:, :]))

    W17r = b.sbuf("eg_W17r", [64, 17], F32)
    W17i = b.sbuf("eg_W17i", [64, 17], F32)
    U16r = b.sbuf("eg_U16r", [64, 16], F32)
    U16i = b.sbuf("eg_U16i", [64, 16], F32)
    tmp = b.sbuf("eg_tmp", [64, 1], F32)
    for (Wr, Wi, n, ar, br) in ((W17r, W17i, 17, a, bb), (U16r, U16i, 16, ia, ib)):
        b.op("gpsimd", lambda e, Wr=Wr: e.memset(Wr[:, 0:1], 1.0))
        b.op("gpsimd", lambda e, Wi=Wi: e.memset(Wi[:, 0:1], 0.0))
        for j in range(1, n):
            b.op("vector", lambda e, j=j, Wr=Wr, ar=ar: e.tensor_scalar(
                Wr[:, j:j + 1], Wr[:, j - 1:j], ar[:, 0:1], None, ALU.mult))
            b.op("vector", lambda e, j=j, Wi=Wi, br=br: e.tensor_scalar(
                tmp[:, :], Wi[:, j - 1:j], br[:, 0:1], None, ALU.mult))
            b.op("vector", lambda e, j=j, Wr=Wr: e.tensor_sub(
                Wr[:, j:j + 1], Wr[:, j:j + 1], tmp[:, :]))
            b.op("vector", lambda e, j=j, Wi=Wi, ar=ar: e.tensor_scalar(
                Wi[:, j:j + 1], Wi[:, j - 1:j], ar[:, 0:1], None, ALU.mult))
            b.op("vector", lambda e, j=j, Wr=Wr, Wi=Wi, br=br: e.tensor_scalar(
                tmp[:, :], Wr[:, j - 1:j], br[:, 0:1], None, ALU.mult))
            b.op("vector", lambda e, j=j, Wi=Wi: e.tensor_add(
                Wi[:, j:j + 1], Wi[:, j:j + 1], tmp[:, :]))
    mask = b.sbuf("eg_mask", [64, 2 * TT], F32)
    b.op("gpsimd", lambda e: e.memset(mask[:, :], 1.0))
    b.op("gpsimd", lambda e: e.memset(mask[:, 0:2 * TT:SEG], 0.0))
    eg.update(W17r=W17r, W17i=W17i, U16r=U16r, U16i=U16i, mask=mask)
    return eg


def emit_eigenstate(b, eg, bt, cout, n, carry_ap, scratch):
    W17r, W17i = eg["W17r"], eg["W17i"]
    U16r, U16i = eg["U16r"], eg["U16i"]
    y, p, z, G = scratch["y"], scratch["p"], scratch["z"], scratch["G"]
    ta, tb = scratch["ta"], scratch["tb"]
    nseg = n // SEG
    nch = n // CH

    def b16(t, rev=False):
        ap = t[:, 16:0:-1] if rev else t[:, 0:16]
        return ap.unsqueeze(1).broadcast_to([64, 2 * nseg, 16])

    def b16h(t, rev=False):
        ap = t[:, 16:0:-1] if rev else t[:, 0:16]
        return ap.unsqueeze(1).broadcast_to([64, nseg, 16])

    def r3(ap):
        return ap.rearrange("p (g f) -> p g f", f=16)

    tc, td = scratch["tc"], scratch["td"]
    b.op("vector", lambda e: e.tensor_mul(r3(y[:, 0:n]), r3(bt[:, 0:n]), b16h(W17r)))
    b.op("vector", lambda e: e.tensor_mul(r3(ta[:, 0:n]), r3(bt[:, n:2 * n]), b16h(W17i)))
    b.op("vector", lambda e: e.tensor_sub(y[:, 0:n], y[:, 0:n], ta[:, 0:n]))
    b.op("gpsimd", lambda e: e.tensor_mul(r3(y[:, n:2 * n]), r3(bt[:, n:2 * n]), b16h(W17r)))
    b.op("gpsimd", lambda e: e.tensor_mul(r3(tc[:, 0:n]), r3(bt[:, 0:n]), b16h(W17i)))
    b.op("gpsimd", lambda e: e.tensor_add(y[:, n:2 * n], y[:, n:2 * n], tc[:, 0:n]))
    b.op("vector", lambda e: e.tensor_tensor_scan(
        p[:, 0:n], eg["mask"][:, 0:n], y[:, n - 1::-1], 0.0, ALU.mult, ALU.add))
    b.op("vector", lambda e: e.tensor_tensor_scan(
        p[:, n:2 * n], eg["mask"][:, 0:n], y[:, 2 * n - 1:n - 1:-1], 0.0, ALU.mult, ALU.add))
    sufr = p[:, n - 1::-1]
    sufi = p[:, 2 * n - 1:n - 1:-1]
    b.op("vector", lambda e: e.tensor_mul(r3(z[:, 0:n]), r3(sufr), b16h(U16r)))
    b.op("vector", lambda e: e.tensor_mul(r3(ta[:, 0:n]), r3(sufi), b16h(U16i)))
    b.op("vector", lambda e: e.tensor_sub(z[:, 0:n], z[:, 0:n], ta[:, 0:n]))
    b.op("gpsimd", lambda e: e.tensor_mul(r3(z[:, n:2 * n]), r3(sufi), b16h(U16r)))
    b.op("gpsimd", lambda e: e.tensor_mul(r3(tc[:, 0:n]), r3(sufr), b16h(U16i)))
    b.op("gpsimd", lambda e: e.tensor_add(z[:, n:2 * n], z[:, n:2 * n], tc[:, 0:n]))
    a16r = W17r[:, 16:17]
    a16i = W17i[:, 16:17]

    def Gs(m, imag):
        base = nseg if imag else 0
        return G[:, base + m::4][:, 0:nch]

    def Ts(m, imag):
        base = n if imag else 0
        return z[:, base + m * 16::CH][:, 0:nch]
    b.op("gpsimd", lambda e: e.memset(Gs(3, False), 0.0))
    b.op("gpsimd", lambda e: e.memset(Gs(3, True), 0.0))
    b.op("vector", lambda e: e.tensor_copy(Gs(2, False), Ts(3, False)))
    b.op("vector", lambda e: e.tensor_copy(Gs(2, True), Ts(3, True)))
    for m in (1, 0):
        b.op("vector", lambda e, m=m: e.scalar_tensor_tensor(
            Gs(m, False), Gs(m + 1, False), a16r[:, 0:1], Ts(m + 1, False), ALU.mult, ALU.add))
        b.op("vector", lambda e, m=m: e.tensor_scalar(
            ta[:, 0:nch], Gs(m + 1, True), a16i[:, 0:1], None, ALU.mult))
        b.op("vector", lambda e, m=m: e.tensor_sub(Gs(m, False), Gs(m, False), ta[:, 0:nch]))
        b.op("vector", lambda e, m=m: e.scalar_tensor_tensor(
            Gs(m, True), Gs(m + 1, True), a16r[:, 0:1], Ts(m + 1, True), ALU.mult, ALU.add))
        b.op("vector", lambda e, m=m: e.tensor_scalar(
            ta[:, 0:nch], Gs(m + 1, False), a16i[:, 0:1], None, ALU.mult))
        b.op("vector", lambda e, m=m: e.tensor_add(Gs(m, True), Gs(m, True), ta[:, 0:nch]))
    GrB = G[:, 0:nseg].unsqueeze(2).broadcast_to([64, nseg, 16])
    GiB = G[:, nseg:2 * nseg].unsqueeze(2).broadcast_to([64, nseg, 16])
    b.op("vector", lambda e: e.tensor_mul(r3(ta[:, 0:n]), b16h(W17r, True), GrB))
    b.op("vector", lambda e: e.tensor_mul(r3(tb[:, 0:n]), b16h(W17i, True), GiB))
    b.op("vector", lambda e: e.tensor_sub(ta[:, 0:n], ta[:, 0:n], tb[:, 0:n]))
    b.op("vector", lambda e: e.tensor_add(cout[:, 0:n], z[:, 0:n], ta[:, 0:n]))
    b.op("gpsimd", lambda e: e.tensor_mul(r3(tc[:, 0:n]), b16h(W17r, True), GiB))
    b.op("gpsimd", lambda e: e.tensor_mul(r3(td[:, 0:n]), b16h(W17i, True), GrB))
    b.op("gpsimd", lambda e: e.tensor_add(tc[:, 0:n], tc[:, 0:n], td[:, 0:n]))
    b.op("gpsimd", lambda e: e.tensor_add(cout[:, n:2 * n], z[:, n:2 * n], tc[:, 0:n]))
    if carry_ap is not None:
        b.op("vector", lambda e: e.tensor_add(cout[:, 0:1], cout[:, 0:1], carry_ap[:, 0:1]))
        b.op("gpsimd", lambda e: e.tensor_add(cout[:, n:n + 1], cout[:, n:n + 1], carry_ap[:, 1:2]))
    if nch > 1:
        b.op("vector", lambda e: e.tensor_add(
            cout[:, CH::CH][:, 0:nch - 1], cout[:, CH::CH][:, 0:nch - 1],
            bt[:, CH - 1::CH][:, 0:nch - 1]))
        b.op("gpsimd", lambda e: e.tensor_add(
            cout[:, n + CH::CH][:, 0:nch - 1], cout[:, n + CH::CH][:, 0:nch - 1],
            bt[:, n + CH - 1::CH][:, 0:nch - 1]))


def mk_eig_scratch(b, n):
    y = b.sbuf("eg_y", [64, 2 * n], F32)
    return {
        "y": y,
        "p": b.sbuf("eg_p", [64, 2 * n], F32),
        "z": y,
        "G": b.sbuf("eg_G", [64, 2 * (n // SEG)], F32),
        "ta": b.sbuf("eg_ta", [64, n], F32),
        "tb": b.sbuf("eg_tb", [64, n], F32),
        "tc": b.sbuf("eg_tc", [64, n], F32),
        "td": b.sbuf("eg_td", [64, n], F32),
    }



def emit_l1a(b, cst, d):
    b.push()
    w_gate = load_w(b, "w_gate_s", d["w_gate"], 8, 8)
    w_in_r = b.sbuf("w_in_r_s", [128, 8, 64], BF16)
    b.op("gpsimd", lambda e: e.dma_start(out=w_in_r[:, :, :], in_=d["w_in_r"][:, :, :]), dma=True,
         extra={"sdma": b.val["sdma"], "sync": b.val["sync"]})
    w_in_i = b.sbuf("w_in_i_s", [128, 8, 64], BF16)
    b.op("gpsimd", lambda e: e.dma_start(out=w_in_i[:, :, :], in_=d["w_in_i"][:, :, :]), dma=True)
    wor = b.sbuf("wor_s", [64, 8, 128], BF16)
    b.op("gpsimd", lambda e: e.dma_start(out=wor[:, :, :], in_=d["w_out_r"][:, :, :]), dma=True)
    woi = b.sbuf("woi_s", [64, 8, 128], BF16)
    b.op("gpsimd", lambda e: e.dma_start(out=woi[:, :, :], in_=d["w_out_i"][:, :, :]), dma=True)
    b_gate = load_cols(b, "b_gate_s", d["b_gate"], 8)
    ln1g = load_cols(b, "ln1g_s", d["ln1g"], 8)
    ln1b = load_cols(b, "ln1b_s", d["ln1b"], 8)
    ld = b.sbuf("ld_s", [64, 1], F32)
    b.op("gpsimd", lambda e: e.dma_start(out=ld[:, :], in_=d["ldecay"][:, :]), dma=True)
    fr = b.sbuf("fr_s", [64, 1], F32)
    b.op("gpsimd", lambda e: e.dma_start(out=fr[:, :], in_=d["freq"][:, :]), dma=True)
    flag = load_cols(b, "flag_s", d["flag"], 1)
    wcp = b.sbuf("wcp", [64, 64], F32)
    b.op("gpsimd", lambda e: e.memset(wcp[:, :], 0.0),
         extra={"sdma": b.val["sdma"], "sync": b.val["sync"]})
    for h in range(4):
        o = 16 * h
        b.op("gpsimd", lambda e, o=o, h=h: e.dma_start(
            out=wcp[o:o + 16, o:o + 16], in_=d["coupT"][h, :, :]), dma=True)
    eg = emit_eig_tables(b, cst, ld, fr)
    ln = LNScratch(b, TT, cst)
    egs = mk_eig_scratch(b, TT)

    xt = [b.sbuf("xt0", [128, NCHUNK, TT], F32), b.sbuf("xt1", [128, NCHUNK, TT], F32),
          b.sbuf("xt2", [128, NCHUNK, TT], F32)]
    y1 = [b.sbuf("y10", [128, NCHUNK, TT], F32), b.sbuf("y11", [128, NCHUNK, TT], F32),
          b.sbuf("y12", [128, NCHUNK, TT], F32)]
    xnb = [b.sbuf("xnb0", [128, NCHUNK, TT], BF16), b.sbuf("xnb1", [128, NCHUNK, TT], BF16)]
    btb = [b.sbuf("bt0", [64, 2 * TT], F32), b.sbuf("bt1", [64, 2 * TT], F32)]
    ct = egs["p"]
    cpb = [b.sbuf("cpb0", [64, 2 * TT], BF16), b.sbuf("cpb1", [64, 2 * TT], BF16)]
    g8 = b.sbuf("g8", [128, NCHUNK, TT], F32)
    ho = b.sbuf("ho", [128, 2, TT], F32)
    carryT = b.sbuf("carryT", [64, 2 * NTILE], F32)
    carryB = b.sbuf("carryB", [64, 2], F32)
    zcol = b.sbuf("zcol", [64, 2], F32)
    ps1 = b.psum_pair("ps1", [128, TT], F32)
    ps2 = b.psum_pair("ps2", [128, TT], F32)
    psBr = b.psum("psBr", [64, TT], F32)
    psBi = b.psum("psBi", [64, TT], F32)
    b.op("gpsimd", lambda e: e.memset(zcol[:, :], 0.0))
    ldm = {}
    stm = {}
    sd0 = b.val["sdma"]

    def load(t):
        if 0 <= t < NTILE:
            b.op("sync", lambda e, t=t: e.dma_start(
                out=xt[t % 3][:, :, :], in_=d["xs"][:, :, t * TT:(t + 1) * TT]), dma=True,
                extra={"sdma": sd0})
            ldm[t] = b.val["sync"]

    mB = ln.m[:, :].unsqueeze(1).broadcast_to([128, NCHUNK, TT])
    A2B = None
    AB = ln.A[:, :].unsqueeze(1).broadcast_to([128, NCHUNK, TT])
    gB = ln1g[:, :].unsqueeze(2).broadcast_to([128, NCHUNK, TT])
    bB = ln1b[:, :].unsqueeze(2).broadcast_to([128, NCHUNK, TT])

    def statsPart(t):
        x = xt[t % 3]
        ln.statsA(TT, x[:, :, :], first={"sync": ldm[t]})
        statsEnd[t] = b.now()

    def applyPart(t):
        nonlocal A2B
        x = xt[t % 3]
        yy = y1[t % 3]
        xb = xnb[t % 2]
        bb = btb[t % 2]
        ln.statsB(TT, comp=True)
        if A2B is None:
            A2B = ln.A2[:, :].unsqueeze(1).broadcast_to([128, NCHUNK, TT])
        b.op("vector", lambda e, x=x, yy=yy: e.tensor_sub(yy[:, :, :], x[:, :, :], mB),
             extra=({"sdma": stm[t - 3]} if t - 3 in stm else None))
        b.op("vector", lambda e, yy=yy, xb=xb: e.tensor_mul(xb[:, :, :], yy[:, :, :], A2B))
        b.op("vector", lambda e, xb=xb: e.tensor_mul(xb[:, :, :], xb[:, :, :], gB))
        b.op("vector", lambda e, xb=xb: e.tensor_add(xb[:, :, :], xb[:, :, :], bB))
        applySnap[t] = b.now()
        b.op("gpsimd", lambda e, yy=yy: e.tensor_mul(yy[:, :, :], yy[:, :, :], AB))
        b.mm(psBr[:, :], [(w_in_r[:, k, :], xb[:, k, :]) for k in range(8)])
        b.op("vector", lambda e, bb=bb: e.tensor_copy(bb[:, 0:TT], psBr[:, :]))
        b.mm(psBi[:, :], [(w_in_i[:, k, :], xb[:, k, :]) for k in range(8)])
        b.op("vector", lambda e, bb=bb: e.tensor_copy(bb[:, TT:2 * TT], psBi[:, :]))
        b.op("vector", lambda e, t=t, bb=bb: e.tensor_copy(
            carryT[:, 2 * t:2 * t + 1], bb[:, TT - 1:TT]))
        b.op("vector", lambda e, t=t, bb=bb: e.tensor_copy(
            carryT[:, 2 * t + 1:2 * t + 2], bb[:, 2 * TT - 1:2 * TT]))
        if t == 0:
            b.op("vector", lambda e: e.tensor_mul(
                carryB[:, :], carryT[:, 0:2], flag[0:64, 0:1].broadcast_to([64, 2])))

    def preB(t):
        bb = btb[t % 2]
        cp = cpb[t % 2]
        if t == 0:
            ccol = zcol
        elif t == 1:
            ccol = carryB
        else:
            ccol = carryT[:, 2 * (t - 1):2 * t]
        emit_eigenstate(b, eg, bb, ct, TT, ccol[:, 0:2], egs)
        b.mm(psBr[:, :], [(wcp[:, :], ct[:, 0:TT])])
        b.op("vector", lambda e, cp=cp: e.tensor_copy(cp[:, 0:TT], psBr[:, :]))
        b.mm(psBi[:, :], [(wcp[:, :], ct[:, TT:2 * TT])])
        b.op("vector", lambda e, cp=cp: e.tensor_copy(cp[:, TT:2 * TT], psBi[:, :]))

    def gatePipe(t, start):
        xb = xnb[t % 2]
        return b.pipe(8, [
            ("tensor", lambda e, i, xb=xb: b.mmg(e, ps1[i % 2][:, :], [(w_gate[:, k, i, :], xb[:, k, :]) for k in range(8)])),
            ("scalar", lambda e, i: e.activation(
                g8[:, i, :], ps1[i % 2][:, :], AF.Sigmoid, bias=b_gate[:, i:i + 1], scale=1.0)),
        ], start=start)

    def woPipe(t, start):
        cp = cpb[t % 2]
        yy = y1[t % 3]
        return b.pipe(8, [
            ("tensor", lambda e, i, cp=cp: b.mmg(e, ps2[i % 2][:, :], [(wor[:, i, :], cp[:, 0:TT]), (woi[:, i, :], cp[:, TT:2 * TT])])),
            ("scalar", lambda e, i: e.activation(
                ho[:, i % 2, :], ps2[i % 2][:, :], AF.Copy, bias=0.0, scale=1.0)),
            ("gpsimd", lambda e, i: e.tensor_mul(ho[:, i % 2, :], ho[:, i % 2, :], g8[:, i, :])),
            ("gpsimd", lambda e, i, yy=yy: e.tensor_add(yy[:, i, :], yy[:, i, :], ho[:, i % 2, :])),
        ], start=start)

    def post(t):
        yy = y1[t % 3]
        b.op("scalar", lambda e, t=t, yy=yy: e.dma_start(
            out=d["x1"][:, :, t * TT:(t + 1) * TT], in_=yy[:, :, :]), dma=True)
        stm[t] = b.val["sdma"]
        load(t + 3)

    load(0)
    load(1)
    load(2)
    applySnap = {}
    statsEnd = {}
    applyEnd = {}
    preBEnd = {}
    gateEnd = {}
    woEnd = {}
    b.scope_push()
    statsPart(0)
    b.scope_pop()
    b.scope_push()
    applyPart(0)
    b.scope_pop()
    applyEnd[0] = b.now()
    for t in range(NTILE):
        if t + 1 < NTILE:
            b.scope_push(base=applyEnd[t])
            statsPart(t + 1)
            b.scope_pop()
        gm = gatePipe(t, mmax(applySnap[t], gateEnd.get(t - 1), woEnd.get(t - 1),
                              {"sdma": stm[t - 2]} if t - 2 in stm else None))
        gateEnd[t] = b.now()
        b.scope_push(base=mmax(applyEnd[t], preBEnd.get(t - 1)))
        preB(t)
        b.scope_pop()
        preBEnd[t] = b.now()
        woPipe(t, mmax(preBEnd[t], gm, woEnd.get(t - 1)))
        woEnd[t] = b.now()
        if t + 1 < NTILE:
            b.scope_push(base=mmax(statsEnd[t + 1], preBEnd[t]))
            applyPart(t + 1)
            b.scope_pop()
            applyEnd[t + 1] = b.now()
        if t - 1 >= 0:
            b.scope_push()
            post(t - 1)
            b.scope_pop()
    b.scope_push()
    post(NTILE - 1)
    b.scope_pop()
    b.pop()


def emit_l1b(b, cst, d):
    N = 256
    NT = TW // N
    b.push()
    w1 = load_w(b, "w1_s", d["w_mlp1"], 8, 32)
    w2 = load_w(b, "w2_s", d["w_mlp2"], 32, 8)
    b1 = load_cols(b, "b1_s", d["b_mlp1"], 32)
    b2 = load_cols(b, "b2_s", d["b_mlp2"], 8)
    ln2g = load_cols(b, "ln2g_s", d["ln2g"], 8)
    ln2b = load_cols(b, "ln2b_s", d["ln2b"], 8)
    ln = LNScratch(b, N, cst, alloc_xb=False)
    lnP = LNScratch(b, N, cst, alloc_xb=False)
    x1t = [b.sbuf("x1t0", [128, NCHUNK, N], F32), b.sbuf("x1t1", [128, NCHUNK, N], F32)]
    xst = [b.sbuf("xst0", [128, NCHUNK, N], F32), b.sbuf("xst1", [128, NCHUNK, N], F32)]
    xn2b = [b.sbuf("xn2b0", [128, NCHUNK, N], BF16), b.sbuf("xn2b1", [128, NCHUNK, N], BF16)]
    h1b = b.sbuf("h1b", [128, 32, N], BF16)
    ao = [b.sbuf("ao0", [128, NCHUNK, N], F32), b.sbuf("ao1", [128, NCHUNK, N], F32)]
    ps1 = b.psum_pair("ps1", [128, N], F32)
    ldm = {}
    stm = {}
    sd0 = b.val["sdma"]

    def loadx1(t):
        if 0 <= t < NT:
            b.op("sync", lambda e, t=t: e.dma_start(
                out=x1t[t % 2][:, :, :], in_=d["x1"][:, :, t * N:(t + 1) * N]), dma=True,
                extra={"sdma": sd0})
            ldm[t] = b.val["sync"]

    def loadxs(t):
        if 0 <= t < NT:
            b.op("sync", lambda e, t=t: e.dma_start(
                out=xst[t % 2][:, :, :], in_=d["xs"][:, :, t * N:(t + 1) * N]), dma=True,
                extra={"sdma": sd0})

    def load(t):
        loadx1(t)
        loadxs(t)

    mB = ln.m[:, :N].unsqueeze(1).broadcast_to([128, NCHUNK, N])
    AB = ln.A[:, :N].unsqueeze(1).broadcast_to([128, NCHUNK, N])
    gB = ln2g[:, :].unsqueeze(2).broadcast_to([128, NCHUNK, N])
    bB = ln2b[:, :].unsqueeze(2).broadcast_to([128, NCHUNK, N])
    mPB = lnP.m[:, :N].unsqueeze(1).broadcast_to([128, NCHUNK, N])
    APB = lnP.A[:, :N].unsqueeze(1).broadcast_to([128, NCHUNK, N])

    def preA_(t):
        ln.statsA_f32(N, x1t[t % 2], h1b[:, 0:8, :], first={"sync": ldm[t]})
        preAEnd[t] = b.now()

    def preRest(t):
        x1 = x1t[t % 2]
        xb = xn2b[t % 2]
        ln.statsB(N)
        b.op("vector", lambda e, x1=x1, xb=xb: e.tensor_sub(xb[:, :, :], x1[:, :, :], mB))
        b.op("vector", lambda e, xb=xb: e.tensor_mul(xb[:, :, :], xb[:, :, :], AB))
        b.op("vector", lambda e, xb=xb: e.tensor_mul(xb[:, :, :], xb[:, :, :], gB))
        b.op("vector", lambda e, xb=xb: e.tensor_add(xb[:, :, :], xb[:, :, :], bB))

    def pipe(t, start):
        x1 = x1t[t % 2]
        xb = xn2b[t % 2]
        a = ao[t % 2]
        m1 = b.pipe(32, [
            ("tensor", lambda e, i, xb=xb: b.mmg(e, ps1[i % 2][:, 0:N], [(w1[:, k, i, :], xb[:, k, :]) for k in range(8)])),
            ("scalar", lambda e, i: e.activation(
                h1b[:, i, :], ps1[i % 2][:, 0:N], AF.Silu, bias=b1[:, i:i + 1], scale=1.0)),
        ], start=start)
        b.pipe(8, [
            ("tensor", lambda e, i: b.mmg(e, ps1[i % 2][:, 0:N], [(w2[:, k, i, :], h1b[:, k, :]) for k in range(32)])),
            ("scalar", lambda e, i, a=a: e.activation(
                a[:, i, :], ps1[i % 2][:, 0:N], AF.Identity, bias=b2[:, i:i + 1], scale=1.0)),
            ("gpsimd", lambda e, i, a=a, x1=x1: e.tensor_add(a[:, i, :], a[:, i, :], x1[:, i, :])),
        ], start=mmax(start, m1))

    def postA(t):
        lnP.statsA_f32(N, ao[t % 2], h1b[:, 8:16, :])
        postAEnd[t] = b.now()

    def postRest(t):
        xs = xst[t % 2]
        a = ao[t % 2]
        lnP.statsB(N)
        b.op("vector", lambda e, a=a: e.tensor_sub(a[:, :, :], a[:, :, :], mPB))
        b.op("vector", lambda e, a=a: e.tensor_mul(a[:, :, :], a[:, :, :], APB))
        b.op("vector", lambda e, a=a, xs=xs: e.tensor_add(a[:, :, :], a[:, :, :], xs[:, :, :]))
        b.op("scalar", lambda e, t=t, a=a: e.dma_start(
            out=d["s1"][:, :, t * N:(t + 1) * N], in_=a[:, :, :]), dma=True)
        stm[t] = b.val["sdma"]
        loadxs(t + 2)

    load(0)
    load(1)
    preAEnd = {}
    preEnd = {}
    postAEnd = {}
    postEnd = {}
    pipeEnd = {}
    b.scope_push()
    preA_(0)
    b.scope_pop()
    b.scope_push()
    preRest(0)
    b.scope_pop()
    preEnd[0] = b.now()
    for t in range(NT):
        if t - 1 >= 0:
            b.scope_push(base=mmax(pipeEnd[t - 1], postEnd.get(t - 2)))
            postA(t - 1)
            b.scope_pop()
        if t + 1 < NT:
            b.scope_push(base=preEnd[t])
            preA_(t + 1)
            b.scope_pop()
        pipe(t, mmax(preEnd[t], pipeEnd.get(t - 1),
                     {"sdma": stm[t - 2]} if t - 2 in stm else None))
        pipeEnd[t] = b.now()
        if t + 1 < NT:
            b.scope_push(base=preAEnd[t + 1])
            preRest(t + 1)
            b.scope_pop()
            preEnd[t + 1] = b.now()
        loadx1(t + 2)
        if t - 1 >= 0:
            b.scope_push(base=postAEnd[t - 1])
            postRest(t - 1)
            b.scope_pop()
            postEnd[t - 1] = b.now()
    b.scope_push(base=b.now())
    postA(NT - 1)
    b.scope_pop()
    b.scope_push(base=postAEnd[NT - 1])
    postRest(NT - 1)
    b.scope_pop()
    b.pop()


def emit_l2(b, cst, d):
    emit_l2a(b, cst, d)
    emit_l2b(b, cst, d)


def emit_l2a(b, cst, d):
    SC = 1.0 / 127.0
    b.push()
    wg = load_w(b, "wg_s", d["wg"], 8, 24)
    wu = load_w(b, "wu_s", d["wu"], 8, 24)
    ln = LNScratch(b, TT, cst)
    xt = [b.sbuf("xt0", [128, NCHUNK, TT], F32), b.sbuf("xt1", [128, NCHUNK, TT], F32)]
    y2b = [b.sbuf("y2b0", [128, NCHUNK, TT], BF16), b.sbuf("y2b1", [128, NCHUNK, TT], BF16)]
    sg_t = b.sbuf("sg_t", [128, 2, TT], F32)
    ht = b.sbuf("ht", [128, 24, TT], BF16)
    ps1 = b.psum_pair("ps1", [128, TT], F32)
    ps2 = b.psum_pair("ps2", [128, TT], F32)
    ldm = {}
    stm = {}
    sd0 = b.val["sdma"]

    def load(t):
        if 0 <= t < NTILE:
            b.op("sync", lambda e, t=t: e.dma_start(
                out=xt[t % 2][:, :, :], in_=d["s1"][:, :, t * TT:(t + 1) * TT]), dma=True,
                extra={"sdma": sd0})
            ldm[t] = b.val["sync"]

    mB = ln.m[:, :].unsqueeze(1).broadcast_to([128, NCHUNK, TT])
    AB = ln.A[:, :].unsqueeze(1).broadcast_to([128, NCHUNK, TT])

    def pre(t):
        x = xt[t % 2]
        yb = y2b[t % 2]
        ln.stats(None, TT, x_all=x[:, :, :])
        b.op("vector", lambda e, x=x, yb=yb: e.tensor_sub(yb[:, :, :], x[:, :, :], mB))
        b.op("vector", lambda e, yb=yb: e.tensor_mul(yb[:, :, :], yb[:, :, :], AB))

    def pipe(t, start):
        yb = y2b[t % 2]

        def l2mm(e, i, yb=yb):
            b.mmg(e, ps1[i % 2][:, :], [(wg[:, k, i, :], yb[:, k, :]) for k in range(8)])
            return b.mmg(e, ps2[i % 2][:, :], [(wu[:, k, i, :], yb[:, k, :]) for k in range(8)])
        b.pipe(24, [
            ("tensor", l2mm),
            ("scalar", lambda e, i: e.activation(
                sg_t[:, i % 2, :], ps1[i % 2][:, :], AF.Silu, bias=0.0, scale=SC)),
            ("vector", lambda e, i: e.scalar_tensor_tensor(
                ht[:, i, :], ps2[i % 2][:, :], SC, sg_t[:, i % 2, :], ALU.mult, ALU.mult)),
        ], start=start)

    def post(t):
        b.op("scalar", lambda e, t=t: e.dma_start(
            out=d["hq"][:, :, t * TT:(t + 1) * TT], in_=ht[:, :, :]), dma=True)
        stm[t] = b.val["sdma"]
        load(t + 2)


    load(0)
    load(1)
    preSnap = {}
    pipeEnd = {}
    b.scope_push()
    pre(0)
    b.scope_pop()
    preSnap[0] = b.now()
    pipe(0, preSnap[0])
    pipeEnd[0] = b.now()
    for t in range(NTILE):
        if t + 1 < NTILE:
            b.scope_push()
            pre(t + 1)
            b.scope_pop()
            preSnap[t + 1] = b.now()
        b.scope_push()
        post(t)
        b.scope_pop()
        if t + 1 < NTILE:
            pipe(t + 1, mmax(preSnap[t + 1], pipeEnd[t], {"sdma": stm[t]}))
            pipeEnd[t + 1] = b.now()
    b.pop()


def emit_l2b(b, cst, d):
    SC = 1.0 / 127.0
    b.push()
    wd = load_w(b, "wd_s", d["wd"], 24, 8)
    ln = LNScratch(b, TT, cst)
    xt = [b.sbuf("xt0", [128, NCHUNK, TT], F32), b.sbuf("xt1", [128, NCHUNK, TT], F32)]
    ht = [b.sbuf("ht0", [128, 24, TT], BF16), b.sbuf("ht1", [128, 24, TT], BF16)]
    Ft = [b.sbuf("Ft0", [128, NCHUNK, TT], F32), b.sbuf("Ft1", [128, NCHUNK, TT], F32)]
    ps1 = b.psum_pair("ps1", [128, TT], F32)
    ldm = {}
    stm = {}
    sd0 = b.val["sdma"]

    def loadxt(t):
        if 0 <= t < NTILE:
            b.op("sync", lambda e, t=t: e.dma_start(
                out=xt[t % 2][:, :, :], in_=d["s1"][:, :, t * TT:(t + 1) * TT]), dma=True,
                extra={"sdma": sd0})

    def loadht(t):
        if 0 <= t < NTILE:
            b.op("sync", lambda e, t=t: e.dma_start(
                out=ht[t % 2][:, :, :], in_=d["hq"][:, :, t * TT:(t + 1) * TT]), dma=True,
                extra={"sdma": sd0})
            ldm[t] = b.val["sync"]

    def load(t):
        loadxt(t)
        loadht(t)

    def pipe(t, start):
        h = ht[t % 2]
        f = Ft[t % 2]
        ex = {"sync": ldm[t], "gdma": b.val["gdma"]}
        if t - 2 in stm:
            ex["sdma"] = stm[t - 2]
        b.pipe(8, [
            ("tensor", lambda e, i, h=h: b.mmg(
                e, ps1[i % 2][:, :], [(wd[:, k, i, :], h[:, k, :]) for k in range(24)])),
            ("scalar", lambda e, i, f=f: e.activation(
                f[:, i, :], ps1[i % 2][:, :], AF.Identity, bias=0.0, scale=SC)),
        ], start=dict(start), extra=ex)

    mB = ln.m[:, :].unsqueeze(1).broadcast_to([128, NCHUNK, TT])
    AB = ln.A[:, :].unsqueeze(1).broadcast_to([128, NCHUNK, TT])

    def post(t):
        x = xt[t % 2]
        f = Ft[t % 2]
        ln.stats(None, TT, x_all=f[:, :, :])
        b.op("vector", lambda e, f=f: e.tensor_sub(f[:, :, :], f[:, :, :], mB))
        b.op("vector", lambda e, f=f: e.tensor_mul(f[:, :, :], f[:, :, :], AB))
        b.op("vector", lambda e, f=f, x=x: e.tensor_add(f[:, :, :], f[:, :, :], x[:, :, :]))
        b.op("scalar", lambda e, t=t, f=f: e.dma_start(
            out=d["s2"][:, :, t * TT:(t + 1) * TT], in_=f[:, :, :]), dma=True)
        stm[t] = b.val["sdma"]
        loadxt(t + 2)

    load(0)
    load(1)
    pipeEnd = {}
    postEnd = {}
    pipe(0, b.now())
    pipeEnd[0] = b.now()
    for t in range(NTILE):
        if t + 1 < NTILE:
            pipe(t + 1, mmax(pipeEnd[t], postEnd.get(t - 1)))
            pipeEnd[t + 1] = b.now()
            loadht(t + 2)
        b.scope_push()
        post(t)
        b.scope_pop()
        postEnd[t] = b.now()
    b.pop()


def emit_l3(b, cst, d):
    b.push()
    wk = load_w(b, "wk_s", d["wk"], 8, 8)
    wv = load_w(b, "wv_s", d["wv"], 8, 8)
    wr = load_w(b, "wr_s", d["wr"], 8, 8)
    wo = load_w(b, "wo_s", d["wo"], 8, 8)
    lnw = load_cols(b, "lnw_s", d["lnw"], 8)
    lnb = load_cols(b, "lnb_s", d["lnb"], 8)
    tmk = load_cols(b, "tmk_s", d["tmk"], 8)
    tmv = load_cols(b, "tmv_s", d["tmv"], 8)
    tmr = load_cols(b, "tmr_s", d["tmr"], 8)
    dw = load_cols(b, "dw_s", d["dw"], 8)
    flag = load_cols(b, "flag_s", d["flag"], 1)
    dec = b.sbuf("dec", [128, 8], F32)
    b.op("scalar", lambda e: e.activation(dec[:, :], dw[:, :], AF.Sigmoid, bias=0.0, scale=1.0),
         extra={"sync": b.val["sync"]})
    omk = b.sbuf("omk", [128, 8], F32)
    omv = b.sbuf("omv", [128, 8], F32)
    omr = b.sbuf("omr", [128, 8], F32)
    for om, tm in ((omk, tmk), (omv, tmv), (omr, tmr)):
        b.op("vector", lambda e, om=om, tm=tm: e.tensor_scalar(
            om[:, :], tm[:, :], -1.0, 1.0, ALU.mult, ALU.add))
    ln = LNScratch(b, TT, cst)
    xt = [b.sbuf("xt0", [128, NCHUNK, TT], F32), b.sbuf("xt1", [128, NCHUNK, TT], F32)]
    y3 = b.sbuf("y3", [128, NCHUNK, TT], F32)
    xn = b.sbuf("xn", [128, NCHUNK, TT + 1], F32)
    mixk = b.sbuf("mixk", [128, NCHUNK, TT], BF16)
    mixv = b.sbuf("mixv", [128, NCHUNK, TT], BF16)
    mixr = b.sbuf("mixr", [128, NCHUNK, TT], BF16)
    kf = b.sbuf("kf", [128, 2, TT], F32)
    kv = b.sbuf("kv", [128, 2, TT], F32)
    ssq = b.sbuf("ssq", [128, 2, TT], F32)
    ssb = b.sbuf("ssb", [128, NCHUNK, TT], BF16)
    rf = b.sbuf("rf", [128, NCHUNK, TT], F32)
    scol = b.sbuf("scol", [128, 8], F32)
    xcar = b.sbuf("xcar", [128, 8], F32)
    psK = b.psum_pair("psK", [128, TT], F32)
    psV = b.psum_pair("psV", [128, TT], F32)
    psR = b.psum_pair("psR", [128, TT], F32)
    b.op("gpsimd", lambda e: e.memset(scol[:, :], 0.0),
         extra={"sdma": b.val["sdma"], "sync": b.val["sync"]})
    b.op("gpsimd", lambda e: e.memset(xcar[:, :], 0.0))
    ldm = {}
    stm = {}

    sd0 = b.val["sdma"]

    def load(t):
        if 0 <= t < NTILE:
            b.op("sync", lambda e, t=t: e.dma_start(
                out=xt[t % 2][:, :, :], in_=d["s2"][:, :, t * TT:(t + 1) * TT]), dma=True,
                extra={"sdma": sd0})
            ldm[t] = b.val["sync"]
    mB = ln.m[:, :].unsqueeze(1).broadcast_to([128, NCHUNK, TT])
    AB = ln.A[:, :].unsqueeze(1).broadcast_to([128, NCHUNK, TT])
    gB = lnw[:, :].unsqueeze(2).broadcast_to([128, NCHUNK, TT])
    bB = lnb[:, :].unsqueeze(2).broadcast_to([128, NCHUNK, TT])
    A2B = None
    load(0)
    for t in range(NTILE):
        load(t + 1)
        x = xt[t % 2]
        ln.stats(None, TT, comp=True, first={"sync": ldm[t]}, x_all=x[:, :, :])
        if A2B is None:
            A2B = ln.A2[:, :].unsqueeze(1).broadcast_to([128, NCHUNK, TT])
        b.op("vector", lambda e, x=x: e.tensor_sub(y3[:, :, :], x[:, :, :], mB),
             extra=({"sdma": stm[t - 1]} if t - 1 in stm else None))
        b.op("vector", lambda e: e.tensor_mul(xn[:, :, 1:TT + 1], y3[:, :, :], A2B))
        b.op("vector", lambda e: e.tensor_mul(xn[:, :, 1:TT + 1], xn[:, :, 1:TT + 1], gB))
        b.op("vector", lambda e: e.tensor_add(xn[:, :, 1:TT + 1], xn[:, :, 1:TT + 1], bB))
        b.op("gpsimd", lambda e: e.tensor_mul(y3[:, :, :], y3[:, :, :], AB))
        b.op("vector", lambda e: e.tensor_copy(xn[:, :, 0:1], xcar[:, :].unsqueeze(2)))
        b.op("vector", lambda e: e.tensor_copy(xcar[:, :], xn[:, :, TT:TT + 1].squeeze(2)))
        if t == 0:
            b.op("vector", lambda e: e.tensor_mul(
                xcar[:, :], xcar[:, :], flag[:, 0:1].broadcast_to([128, 8])))
        for mix, tm, om, meng in ((mixk, tmk, omk, "vector"), (mixv, tmv, omv, "gpsimd"),
                                  (mixr, tmr, omr, "vector")):
            tmB = tm[:, :].unsqueeze(2).broadcast_to([128, NCHUNK, TT])
            omB = om[:, :].unsqueeze(2).broadcast_to([128, NCHUNK, TT])
            sc = rf if meng == "vector" else ssb
            b.op(meng, lambda e, sc=sc, tmB=tmB: e.tensor_mul(sc[:, :, :], xn[:, :, 1:TT + 1], tmB))
            b.op(meng, lambda e, mix=mix, omB=omB: e.tensor_mul(mix[:, :, :], xn[:, :, 0:TT], omB))
            b.op(meng, lambda e, mix=mix, sc=sc: e.tensor_add(mix[:, :, :], mix[:, :, :], sc[:, :, :]))
        b.pipe(8, [
            ("tensor", lambda e, i: b.mmg(e, psK[i % 2][:, :], [(wk[:, k, i, :], mixk[:, k, :]) for k in range(8)])),
            ("scalar", lambda e, i: e.activation(kf[:, i % 2, :], psK[i % 2][:, :], AF.Copy, bias=0.0, scale=1.0)),
            ("tensor", lambda e, i: b.mmg(e, psV[i % 2][:, :], [(wv[:, k, i, :], mixv[:, k, :]) for k in range(8)])),
            ("vector", lambda e, i: e.tensor_mul(kv[:, i % 2, :], kf[:, i % 2, :], psV[i % 2][:, :])),
            ("vector", lambda e, i: e.tensor_tensor_scan(
                ssq[:, i % 2, :], dec[:, i:i + 1].broadcast_to([128, TT]), kv[:, i % 2, :],
                scol[:, i:i + 1], ALU.mult, ALU.add)),
            ("vector", lambda e, i: e.tensor_copy(scol[:, i:i + 1], ssq[:, i % 2, TT - 1:TT])),
            ("scalar", lambda e, i: e.activation(ssb[:, i, :], ssq[:, i % 2, :], AF.Copy, bias=0.0, scale=1.0)),
            ("tensor", lambda e, i: b.mmg(e, psR[i % 2][:, :], [(wr[:, k, i, :], mixr[:, k, :]) for k in range(8)])),
            ("scalar", lambda e, i: e.activation(rf[:, i, :], psR[i % 2][:, :], AF.Sigmoid, bias=0.0, scale=1.0)),
        ])
        if t == 0:
            b.op("vector", lambda e: e.tensor_mul(
                scol[:, :], scol[:, :], flag[:, 0:1].broadcast_to([128, 8])))
        b.pipe(8, [
            ("tensor", lambda e, i: b.mmg(e, psK[i % 2][:, :], [(wo[:, k, i, :], ssb[:, k, :]) for k in range(8)])),
            ("vector", lambda e, i: e.tensor_mul(rf[:, i, :], rf[:, i, :], psK[i % 2][:, :])),
            ("vector", lambda e, i: e.tensor_add(y3[:, i, :], y3[:, i, :], rf[:, i, :])),
        ])
        ln.stats(None, TT, x_all=y3[:, :, :])
        b.op("vector", lambda e: e.tensor_sub(y3[:, :, :], y3[:, :, :], mB))
        b.op("vector", lambda e: e.tensor_mul(y3[:, :, :], y3[:, :, :], AB))
        b.op("vector", lambda e, x=x: e.tensor_add(y3[:, :, :], y3[:, :, :], x[:, :, :]))
        b.op("scalar", lambda e, t=t: e.dma_start(out=d["s3"][:, :, t * TT:(t + 1) * TT], in_=y3[:, :, :]), dma=True)
        stm[t] = b.val["sdma"]
    b.pop()


def emit_l4(b, cst, d):
    b.push()
    u1 = load_w(b, "u1_s", d["u1"], 8, 2)
    v1 = load_w(b, "v1_s", d["v1"], 2, 32)
    u2 = load_w(b, "u2_s", d["u2"], 32, 2)
    v2 = load_w(b, "v2_s", d["v2"], 2, 8)
    b1 = load_cols(b, "b1_s", d["fb1"], 32)
    b2 = load_cols(b, "b2_s", d["fb2"], 8)
    ln = LNScratch(b, TT, cst)
    lnP = LNScratch(b, TT, cst)
    xt = [b.sbuf("xt0", [128, NCHUNK, TT], F32), b.sbuf("xt1", [128, NCHUNK, TT], F32),
          b.sbuf("xt2", [128, NCHUNK, TT], F32)]
    y4b = [b.sbuf("y4b0", [128, NCHUNK, TT], BF16), b.sbuf("y4b1", [128, NCHUNK, TT], BF16)]
    pb = b.sbuf("pb", [128, 2, TT], BF16)
    hb = b.sbuf("hb", [128, 32, TT], BF16)
    qb = b.sbuf("qb", [128, 2, TT], BF16)
    G = [b.sbuf("G40", [128, NCHUNK, TT], F32), b.sbuf("G41", [128, NCHUNK, TT], F32)]
    ps1 = b.psum_pair("ps41", [128, TT], F32)
    ldm = {}
    stm = {}
    sd0 = b.val["sdma"]

    def load(t):
        if 1 <= t < NTILE:
            b.op("sync", lambda e, t=t: e.dma_start(
                out=xt[t % 3][:, :, :], in_=d["s3"][:, :, t * TT:(t + 1) * TT]), dma=True,
                extra={"sdma": sd0})
            ldm[t] = b.val["sync"]

    mB = ln.m[:, :].unsqueeze(1).broadcast_to([128, NCHUNK, TT])
    AB = ln.A[:, :].unsqueeze(1).broadcast_to([128, NCHUNK, TT])
    mPB = lnP.m[:, :].unsqueeze(1).broadcast_to([128, NCHUNK, TT])
    APB = lnP.A[:, :].unsqueeze(1).broadcast_to([128, NCHUNK, TT])

    def pre(t):
        x = xt[t % 3]
        yb = y4b[t % 2]
        ln.stats(None, TT, x_all=x[:, :, :])
        b.op("vector", lambda e, x=x, yb=yb: e.tensor_sub(yb[:, :, :], x[:, :, :], mB))
        b.op("vector", lambda e, yb=yb: e.tensor_mul(yb[:, :, :], yb[:, :, :], AB))

    def pipe(t, start):
        yb = y4b[t % 2]
        g = G[t % 2]
        m1 = b.pipe(2, [
            ("tensor", lambda e, i, yb=yb: b.mmg(e, ps1[i % 2][:, :], [(u1[:, k, i, :], yb[:, k, :]) for k in range(8)])),
            ("scalar", lambda e, i: e.activation(pb[:, i, :], ps1[i % 2][:, :], AF.Copy, bias=0.0, scale=1.0)),
        ], start=start)
        m2 = b.pipe(32, [
            ("tensor", lambda e, i: b.mmg(e, ps1[i % 2][:, :], [(v1[:, k, i, :], pb[:, k, :]) for k in range(2)])),
            ("scalar", lambda e, i: e.activation(
                hb[:, i, :], ps1[i % 2][:, :], AF.Gelu_apprx_tanh, bias=b1[:, i:i + 1], scale=1.0)),
        ], start=mmax(start, m1))
        m3 = b.pipe(2, [
            ("tensor", lambda e, i: b.mmg(e, ps1[i % 2][:, :], [(u2[:, k, i, :], hb[:, k, :]) for k in range(32)])),
            ("scalar", lambda e, i: e.activation(qb[:, i, :], ps1[i % 2][:, :], AF.Copy, bias=0.0, scale=1.0)),
        ], start=mmax(start, m2))
        b.pipe(8, [
            ("tensor", lambda e, i: b.mmg(e, ps1[i % 2][:, :], [(v2[:, k, i, :], qb[:, k, :]) for k in range(2)])),
            ("scalar", lambda e, i, g=g: e.activation(
                g[:, i, :], ps1[i % 2][:, :], AF.Identity, bias=b2[:, i:i + 1], scale=1.0)),
        ], start=mmax(start, m3))

    def post(t):
        x = xt[t % 3]
        g = G[t % 2]
        lnP.stats(None, TT, x_all=g[:, :, :])
        b.op("vector", lambda e, g=g: e.tensor_sub(g[:, :, :], g[:, :, :], mPB))
        b.op("vector", lambda e, g=g: e.tensor_mul(g[:, :, :], g[:, :, :], APB))
        b.op("vector", lambda e, g=g, x=x: e.tensor_add(g[:, :, :], g[:, :, :], x[:, :, :]))
        b.op("scalar", lambda e, t=t, g=g: e.dma_start(
            out=d["out"][:, :, (t - 1) * TT:t * TT], in_=g[:, :, :]), dma=True)
        stm[t] = b.val["sdma"]
        load(t + 3)

    load(1)
    load(2)
    load(3)
    preSnap = {}
    pipeEnd = {}
    postEnd = {}
    b.scope_push()
    pre(1)
    b.scope_pop()
    preSnap[1] = b.now()
    for t in range(1, NTILE):
        pipe(t, mmax(preSnap[t], pipeEnd.get(t - 1), postEnd.get(t - 2),
                     {"sdma": stm[t - 2]} if t - 2 in stm else None))
        pipeEnd[t] = b.now()
        if t + 1 < NTILE:
            b.scope_push(base=mmax(preSnap[t], postEnd.get(t - 2)))
            pre(t + 1)
            b.scope_pop()
            preSnap[t + 1] = b.now()
        if t - 1 >= 1:
            b.scope_push()
            post(t - 1)
            b.scope_pop()
            postEnd[t - 1] = b.now()
    b.scope_push()
    post(NTILE - 1)
    b.scope_pop()
    b.pop()



def build_fused():
    b = B()
    d = {}
    d["xs"] = b.dram_in("xs", [128, NCHUNK, TW])
    d["flag"] = b.dram_in("flag", [128, 1])
    d["w_in_r"] = b.dram_in("w_in_r", [128, 8, 64])
    d["w_in_i"] = b.dram_in("w_in_i", [128, 8, 64])
    d["w_gate"] = b.dram_in("w_gate", [128, 8, 8, 128])
    d["w_out_r"] = b.dram_in("w_out_r", [64, 8, 128])
    d["w_out_i"] = b.dram_in("w_out_i", [64, 8, 128])
    d["coupT"] = b.dram_in("coupT", [4, 16, 16])
    d["b_gate"] = b.dram_in("b_gate", [128, 8])
    d["ln1g"] = b.dram_in("ln1g", [128, 8])
    d["ln1b"] = b.dram_in("ln1b", [128, 8])
    d["ldecay"] = b.dram_in("ldecay", [64, 1])
    d["freq"] = b.dram_in("freq", [64, 1])
    d["w_mlp1"] = b.dram_in("w_mlp1", [128, 8, 32, 128])
    d["w_mlp2"] = b.dram_in("w_mlp2", [128, 32, 8, 128])
    d["b_mlp1"] = b.dram_in("b_mlp1", [128, 32])
    d["b_mlp2"] = b.dram_in("b_mlp2", [128, 8])
    d["ln2g"] = b.dram_in("ln2g", [128, 8])
    d["ln2b"] = b.dram_in("ln2b", [128, 8])
    d["wg"] = b.dram_in("wg", [128, 8, 24, 128], I32)
    d["wu"] = b.dram_in("wu", [128, 8, 24, 128], I32)
    d["wd"] = b.dram_in("wd", [128, 24, 8, 128], I32)
    d["wk"] = b.dram_in("wk", [128, 8, 8, 128])
    d["wv"] = b.dram_in("wv", [128, 8, 8, 128])
    d["wr"] = b.dram_in("wr", [128, 8, 8, 128])
    d["wo"] = b.dram_in("wo", [128, 8, 8, 128])
    d["lnw"] = b.dram_in("lnw", [128, 8])
    d["lnb"] = b.dram_in("lnb", [128, 8])
    d["tmk"] = b.dram_in("tmk", [128, 8])
    d["tmv"] = b.dram_in("tmv", [128, 8])
    d["tmr"] = b.dram_in("tmr", [128, 8])
    d["dw"] = b.dram_in("dw", [128, 8])
    d["u1"] = b.dram_in("u1", [128, 8, 2, 128])
    d["v1"] = b.dram_in("v1", [128, 2, 32, 128])
    d["u2"] = b.dram_in("u2", [128, 32, 2, 128])
    d["v2"] = b.dram_in("v2", [128, 2, 8, 128])
    d["fb1"] = b.dram_in("fb1", [128, 32])
    d["fb2"] = b.dram_in("fb2", [128, 8])
    d["out"] = b.dram_out("out", [128, NCHUNK, TW - TT])
    d["x1"] = b.dram_tmp("x1_t", [128, NCHUNK, TW])
    d["s1"] = b.dram_tmp("s1_t", [128, NCHUNK, TW])
    d["hq"] = b.dram_tmp("hq_t", [128, 24, TW], BF16)
    d["s2"] = b.dram_tmp("s2_t", [128, NCHUNK, TW])
    d["s3"] = b.dram_tmp("s3_t", [128, NCHUNK, TW])
    b.begin()
    cst = emit_consts(b)
    emit_l1a(b, cst, d)
    emit_l1b(b, cst, d)
    emit_l2(b, cst, d)
    emit_l3(b, cst, d)
    emit_l4(b, cst, d)
    return b.end()



_NCS = {}


def _get_nc(name, fn):
    if name not in _NCS:
        _NCS[name] = fn()
    return _NCS[name]


def kernel(**ins):
    x = np.asarray(ins["L0_attn_x"])
    B_, T_, D_ = x.shape
    H = T_ // 2

    xs_cores = []
    flags = []
    for b_i in range(B_):
        for h in range(2):
            main = x[b_i, h * H:(h + 1) * H, :]
            if h == 1:
                wu = x[b_i, H - TT:H, :]
                fl = np.ones((128, 1), np.float32)
            else:
                wu = np.zeros((TT, D_), np.float32)
                fl = np.zeros((128, 1), np.float32)
            slab = np.concatenate([wu, main], axis=0)
            xs_cores.append(stage_stream(slab))
            flags.append(fl)

    stat = dict(
        w_in_r=stage_w64_out(np.asarray(ins["L0_attn_w_in"])[0:64, :]),
        w_in_i=stage_w64_out(np.asarray(ins["L0_attn_w_in"])[64:128, :]),
        w_gate=stage_w(ins["L0_attn_w_gate"]),
        w_out_r=stage_w64_in(np.asarray(ins["L0_attn_w_out"])[:, 0:64]),
        w_out_i=stage_w64_in(np.asarray(ins["L0_attn_w_out"])[:, 64:128]),
        coupT=np.ascontiguousarray(np.asarray(ins["L0_attn_coupling"]).transpose(0, 2, 1)),
        b_gate=stage_cols(ins["L0_attn_b_gate"]),
        ln1g=stage_cols(ins["L0_attn_ln1_g"]),
        ln1b=stage_cols(ins["L0_attn_ln1_b"]),
        ldecay=np.ascontiguousarray(np.asarray(ins["L0_attn_log_decay"]).reshape(64, 1)),
        freq=np.ascontiguousarray(np.asarray(ins["L0_attn_frequency"]).reshape(64, 1)),
        w_mlp1=stage_w(ins["L0_attn_w_mlp1"]),
        w_mlp2=stage_w(ins["L0_attn_w_mlp2"]),
        b_mlp1=stage_cols(ins["L0_attn_b_mlp1"]),
        b_mlp2=stage_cols(ins["L0_attn_b_mlp2"]),
        ln2g=stage_cols(ins["L0_attn_ln2_g"]),
        ln2b=stage_cols(ins["L0_attn_ln2_b"]),
        wg=stage_w(ins["L0_ffn_w_gate"]),
        wu=stage_w(ins["L0_ffn_w_up"]),
        wd=stage_w(ins["L0_ffn_w_down"]),
        wk=stage_w(ins["L1_attn_Wk"]),
        wv=stage_w(ins["L1_attn_Wv"]),
        wr=stage_w(ins["L1_attn_Wr"]),
        wo=stage_w(ins["L1_attn_Wo"]),
        lnw=stage_cols(ins["L1_attn_ln_w"]),
        lnb=stage_cols(ins["L1_attn_ln_b"]),
        tmk=stage_cols(ins["L1_attn_tmk"]),
        tmv=stage_cols(ins["L1_attn_tmv"]),
        tmr=stage_cols(ins["L1_attn_tmr"]),
        dw=stage_cols(ins["L1_attn_decay_w"]),
        u1=stage_w(np.asarray(ins["L1_ffn_U1"]).T),
        v1=stage_w(np.asarray(ins["L1_ffn_V1"]).T),
        u2=stage_w(np.asarray(ins["L1_ffn_U2"]).T),
        v2=stage_w(np.asarray(ins["L1_ffn_V2"]).T),
        fb1=stage_cols(ins["L1_ffn_b1"]),
        fb2=stage_cols(ins["L1_ffn_b2"]),
    )

    nc = _get_nc("fused", build_fused)
    maps = [dict(stat, xs=xs_cores[i], flag=flags[i]) for i in range(8)]
    r = run_bass_kernel_spmd(nc, maps, core_ids=list(range(8)))

    out = np.empty((B_, T_, D_), np.float32)
    for b_i in range(B_):
        for h in range(2):
            arr = np.asarray(r.results[b_i * 2 + h]["out"])
            out[b_i, h * H:(h + 1) * H, :] = unstage_stream(arr)
    return out


# version 18 (final)
# speedup vs baseline: 1.1680x; 1.0333x over previous
import math
import contextlib
import numpy as np
import concourse.bass as bass
import concourse.mybir as mybir
from concourse.bass_utils import run_bass_kernel_spmd

F32 = mybir.dt.float32
BF16 = mybir.dt.bfloat16
I32 = mybir.dt.int32
AF = mybir.ActivationFunctionType
ALU = mybir.AluOpType

D = 1024
NCHUNK = 8
TT = 512
NTILE = 9
TW = TT * NTILE
EPS = 1e-5
C0 = (1.0 + EPS) ** -0.5
SEG = 16
CH = 64
PI2_HI = float(np.float32(math.pi / 2))
PI2_LO = math.pi / 2 - PI2_HI



def stage_w(W):
    W = np.ascontiguousarray(W)
    oc = W.shape[0] // 128
    ic = W.shape[1] // 128
    t = W.reshape(oc, 128, ic, 128).transpose(3, 2, 0, 1)
    return np.ascontiguousarray(t)


def stage_w64_out(W):
    W = np.ascontiguousarray(W)
    ic = W.shape[1] // 128
    return np.ascontiguousarray(W.reshape(64, ic, 128).transpose(2, 1, 0))


def stage_w64_in(W):
    W = np.ascontiguousarray(W)
    oc = W.shape[0] // 128
    return np.ascontiguousarray(W.reshape(oc, 128, 64).transpose(2, 0, 1))


def stage_cols(v):
    return np.ascontiguousarray(np.asarray(v).reshape(-1, 128).T)


def stage_stream(x_slab):
    T = x_slab.shape[0]
    t = x_slab.T.reshape(NCHUNK, 128, T)
    return np.ascontiguousarray(t.transpose(1, 0, 2))


def unstage_stream(arr):
    return np.ascontiguousarray(arr.transpose(1, 0, 2).reshape(D, -1).T)



ENGS = ("tensor", "vector", "scalar", "gpsimd", "sync", "gdma", "sdma")
COMPUTE_LANES = ("tensor", "vector", "scalar", "gpsimd", "gdma")


class B:
    def __init__(self):
        self.nc = bass.Bass("TRN2", target_bir_lowering=False, debug=False)
        self.stack = contextlib.ExitStack()
        self.sects = []
        self._blk = None
        self.sems = {}
        self.val = {e: 0 for e in ENGS}
        self.scopes = []
        self.uid = 0

    def dram_in(self, name, shape, dtype=F32):
        return self.nc.dram_tensor(name, shape, dtype, kind="ExternalInput").ap()

    def dram_out(self, name, shape, dtype=F32):
        return self.nc.dram_tensor(name, shape, dtype, kind="ExternalOutput").ap()

    def dram_tmp(self, name, shape, dtype=F32):
        return self.nc.dram_tensor(name, shape, dtype, kind="Internal").ap()

    def _st(self):
        return self.sects[-1] if self.sects else self.stack

    def sbuf(self, name, shape, dtype=F32):
        self.uid += 1
        return self._st().enter_context(
            self.nc.sbuf_tensor(f"{name}_{self.uid}", shape, dtype))

    def psum_pair(self, name, shape, dtype=F32):
        return [self.psum(name + "a", shape, dtype), self.psum(name + "b", shape, dtype)]

    def psum(self, name, shape, dtype=F32):
        self.uid += 1
        shape = list(shape)
        if dtype == F32 and shape[-1] < 512:
            shape[-1] = 512
        return self._st().enter_context(
            self.nc.psum_tensor(f"{name}_{self.uid}", shape, dtype))

    def scope_push(self, base=None):
        self.scopes.append({"base": dict(self.val) if base is None else dict(base), "seen": {}})

    def scope_pop(self):
        self.scopes.pop()

    def push(self):
        self.sects.append(contextlib.ExitStack())

    def pop(self):
        self.sects.pop().close()

    def begin(self):
        for e in ENGS:
            self.sems[e] = self.stack.enter_context(self.nc.semaphore(f"sem_{e}"))
        self._blk = self.stack.enter_context(self.nc.Block())
        self.bodies = {e: [] for e in ("tensor", "vector", "scalar", "gpsimd", "sync")}

    def now(self):
        return dict(self.val)

    def op(self, eng, fn, dma=False, after=None, extra=None):
        key = eng
        if dma:
            key = {"gpsimd": "gdma", "scalar": "sdma"}.get(eng, eng)
        if after is None:
            if self.scopes:
                sc = self.scopes[-1]
                after = dict(sc["base"])
                for l2, v2 in sc["seen"].items():
                    after[l2] = max(after.get(l2, 0), v2)
            else:
                after = {e: self.val[e] for e in COMPUTE_LANES}
        if extra:
            after = dict(after)
            for k2, v2 in extra.items():
                after[k2] = max(after.get(k2, 0), v2)
        if dma:
            after = dict(after)
            after[key] = max(after.get(key, 0), self.val[key])
        waits = [(self.sems[e], v) for e, v in after.items() if v > 0]
        inc = 16 if dma else 1
        sem = self.sems[key]

        def body(e):
            for smh, v in waits:
                e.wait_ge(smh, v)
            ins = fn(e)
            ins.then_inc(sem, inc)
        self.bodies[eng].append(body)
        self.val[key] += inc
        if self.scopes:
            self.scopes[-1]["seen"][key] = self.val[key]

    def mm(self, ps_ap, pairs, after=None, extra=None):
        def f(e):
            last = None
            n = len(pairs)
            for i, (l, r) in enumerate(pairs):
                last = e.matmul(ps_ap, l, r, start=(i == 0), stop=(i == n - 1))
            return last
        self.op("tensor", f, after=after, extra=extra)

    def pipe(self, n, stages, guard=2, extra=None, start=None):
        if start is None:
            start = {e: self.val[e] for e in COMPUTE_LANES}
        else:
            start = dict(start)
        if extra:
            for k2, v2 in extra.items():
                start[k2] = max(start.get(k2, 0), v2)
        last_eng = stages[-1][0]
        last_marks = []
        s0_mark = None
        touched = {}
        for i in range(n):
            prev = None
            for k, (eng, fn) in enumerate(stages):
                after = {}
                if i == 0:
                    after.update(start)
                if k == 0:
                    if i >= guard:
                        after[last_eng] = max(after.get(last_eng, 0), last_marks[i - guard])
                    elif i == 1:
                        after[stages[0][0]] = max(after.get(stages[0][0], 0), s0_mark)
                else:
                    pe, pv = prev
                    if pe == eng:
                        after[eng] = max(after.get(eng, 0), pv)
                    else:
                        after[pe] = max(after.get(pe, 0), pv)
                self.op(eng, lambda e, fn=fn, i=i: fn(e, i), after=after)
                prev = (eng, self.val[eng])
                touched[eng] = self.val[eng]
                if k == 0:
                    s0_mark = self.val[eng]
            last_marks.append(self.val[last_eng])
        return touched

    def mmg(self, e, ps_ap, pairs):
        last = None
        n = len(pairs)
        for i, (l, r) in enumerate(pairs):
            last = e.matmul(ps_ap, l, r, start=(i == 0), stop=(i == n - 1))
        return last

    def end(self):
        vals = self.now()
        for eng, lst in self.bodies.items():
            if not lst:
                continue

            def mega(e, lst=lst):
                for f in lst:
                    f(e)
            getattr(self._blk, eng)(mega)

        def fin(e):
            for en in ENGS:
                if vals[en]:
                    e.wait_ge(self.sems[en], vals[en])
        self._blk.sync(fin)
        while self.sects:
            self.pop()
        self.stack.close()
        return self.nc


def mmax(*ds):
    out = {}
    for d_ in ds:
        if not d_:
            continue
        for k, v in d_.items():
            out[k] = max(out.get(k, 0), v)
    return out



def emit_consts(b):
    cst = {}
    t = b.sbuf("c_eps", [128, 1], F32)
    b.op("gpsimd", lambda e: e.memset(t[:, :], EPS))
    cst["eps"] = t
    t2 = b.sbuf("c_eps2", [128, 1], F32)
    b.op("gpsimd", lambda e: e.memset(t2[:, :], EPS * EPS))
    cst["eps2"] = t2
    ones = b.sbuf("c_ones", [128, 128], BF16)
    b.op("gpsimd", lambda e: e.memset(ones[:, :], 1.0))
    cst["ones"] = ones
    onesf = b.sbuf("c_onesf", [128, 128], F32)
    b.op("gpsimd", lambda e: e.memset(onesf[:, :], 1.0))
    cst["onesf"] = onesf
    return cst


def load_w(b, name, dram_ap, ic, oc, dtype=BF16):
    w = b.sbuf(name, [128, ic, oc, 128], dtype)
    b.op("gpsimd", lambda e: e.dma_start(out=w[:, :, :, :], in_=dram_ap[:, :, :, :]),
         dma=True, extra={"sdma": b.val["sdma"], "sync": b.val["sync"]})
    return w


def load_cols(b, name, dram_ap, n, dtype=F32):
    t = b.sbuf(name, [128, n], dtype)
    b.op("gpsimd", lambda e: e.dma_start(out=t[:, :], in_=dram_ap[:, :]), dma=True,
         extra={"sdma": b.val["sdma"], "sync": b.val["sync"]})
    return t


class LNScratch:
    def __init__(self, b, n, cst, alloc_xb=True):
        self.b = b
        self.cst = cst
        self.nn = n
        self.xb = b.sbuf("ln_xb", [128, NCHUNK, n], BF16) if alloc_xb else None
        self.m = b.sbuf("ln_m", [128, n], F32)
        self.A = b.sbuf("ln_A", [128, n], F32)
        self.t1 = b.sbuf("ln_t1", [128, n], F32)
        self.t2 = None
        self.A2 = None
        self.psS = b.psum("ln_psS", [128, n], F32)
        self.psQ = b.psum("ln_psQ", [128, n], F32)

    def mkcomp(self, n):
        if self.A2 is None:
            self.t2 = self.b.sbuf("ln_t2", [128, self.nn], F32)
            self.A2 = self.b.sbuf("ln_A2", [128, self.nn], F32)

    def statsA_f32(self, n, xten, sq, nch=NCHUNK, first=None):
        b, cst = self.b, self.cst
        b.mm(self.psS[:, :n], [(cst["onesf"][:, :], xten[:, c, :n]) for c in range(nch)],
             extra=first)
        psS_mark = b.val["tensor"]
        b.op("scalar", lambda e: e.activation(
            sq[:, 0:nch, :n], xten[:, 0:nch, :n], AF.Square, bias=0.0, scale=1.0),
            after={"scalar": b.val["scalar"], "tensor": psS_mark}, extra=first)
        b.mm(self.psQ[:, :n], [(cst["ones"][:, :], sq[:, c, :n]) for c in range(nch)])

    def statsA(self, n, x_all, nch=NCHUNK, first=None):
        b, cst = self.b, self.cst
        b.op("scalar", lambda e: e.activation(
            self.xb[:, 0:nch, :n], x_all, AF.Copy, bias=0.0, scale=1.0), extra=first)
        b.mm(self.psS[:, :n], [(cst["ones"][:, :], self.xb[:, c, :n]) for c in range(nch)])
        psS_mark = b.val["tensor"]
        b.op("scalar", lambda e: e.activation(
            self.xb[:, 0:nch, :n], self.xb[:, 0:nch, :n], AF.Square, bias=0.0, scale=1.0),
            after={"scalar": b.val["scalar"], "tensor": psS_mark})
        b.mm(self.psQ[:, :n], [(cst["ones"][:, :], self.xb[:, c, :n]) for c in range(nch)])

    def statsB(self, n, nch=NCHUNK, comp=False):
        b, cst = self.b, self.cst
        if comp:
            self.mkcomp(n)
        inv = 1.0 / (128 * nch)
        b.op("vector", lambda e: e.tensor_scalar(self.m[:, :n], self.psS[:, :n], inv, None, ALU.mult))
        b.op("vector", lambda e: e.tensor_mul(self.t1[:, :n], self.m[:, :n], self.m[:, :n]))
        b.op("vector", lambda e: e.scalar_tensor_tensor(
            self.t1[:, :n], self.psQ[:, :n], inv, self.t1[:, :n], ALU.mult, ALU.subtract))
        if comp:
            b.op("scalar", lambda e: e.activation(
                self.t2[:, :n], self.t1[:, :n], AF.Sqrt, bias=cst["eps2"][:, 0:1],
                scale=1.0 + EPS))
            b.op("vector", lambda e: e.reciprocal(self.A2[:, :n], self.t2[:, :n]))
        b.op("scalar", lambda e: e.activation(
            self.t1[:, :n], self.t1[:, :n], AF.Sqrt, bias=cst["eps"][:, 0:1], scale=1.0))
        b.op("vector", lambda e: e.reciprocal(self.A[:, :n], self.t1[:, :n]))

    def stats(self, x_chunks, n, nch=NCHUNK, comp=False, first=None, x_all=None):
        self.statsA(n, x_all, nch=nch, first=first)
        self.statsB(n, nch=nch, comp=comp)



def emit_eig_tables(b, cst, ld_col, fr_col):
    eg = {}
    mag = b.sbuf("eg_mag", [64, 1], F32)
    b.op("scalar", lambda e: e.activation(mag[:, :], ld_col[:, :], AF.Sigmoid, bias=0.0, scale=1.0))
    s2 = b.sbuf("eg_s2", [64, 1], F32)
    b.op("scalar", lambda e: e.activation(s2[:, :], fr_col[:, :], AF.Sin, bias=0.0, scale=0.5))
    u = b.sbuf("eg_u", [64, 1], F32)
    b.op("vector", lambda e: e.tensor_scalar(u[:, :], fr_col[:, :], -0.5, PI2_HI, ALU.mult, ALU.add))
    b.op("vector", lambda e: e.tensor_scalar(u[:, :], u[:, :], PI2_LO, None, ALU.add))
    c2 = b.sbuf("eg_c2", [64, 1], F32)
    b.op("scalar", lambda e: e.activation(c2[:, :], u[:, :], AF.Sin, bias=0.0, scale=1.0))
    sth = b.sbuf("eg_sth", [64, 1], F32)
    b.op("vector", lambda e: e.tensor_mul(sth[:, :], s2[:, :], c2[:, :]))
    b.op("vector", lambda e: e.tensor_scalar(sth[:, :], sth[:, :], 2.0, None, ALU.mult))
    cth = b.sbuf("eg_cth", [64, 1], F32)
    b.op("vector", lambda e: e.tensor_mul(cth[:, :], s2[:, :], s2[:, :]))
    b.op("vector", lambda e: e.tensor_scalar(cth[:, :], cth[:, :], -2.0, 1.0, ALU.mult, ALU.add))
    a = b.sbuf("eg_a", [64, 1], F32)
    b.op("vector", lambda e: e.tensor_mul(a[:, :], cth[:, :], mag[:, :]))
    bb = b.sbuf("eg_b", [64, 1], F32)
    b.op("vector", lambda e: e.tensor_mul(bb[:, :], sth[:, :], mag[:, :]))
    rmag = b.sbuf("eg_rmag", [64, 1], F32)
    b.op("vector", lambda e: e.reciprocal(rmag[:, :], mag[:, :]))
    ia = b.sbuf("eg_ia", [64, 1], F32)
    b.op("vector", lambda e: e.tensor_mul(ia[:, :], cth[:, :], rmag[:, :]))
    ib = b.sbuf("eg_ib", [64, 1], F32)
    b.op("vector", lambda e: e.tensor_scalar(ib[:, :], sth[:, :], -1.0, None, ALU.mult))
    b.op("vector", lambda e: e.tensor_mul(ib[:, :], ib[:, :], rmag[:, :]))

    W17r = b.sbuf("eg_W17r", [64, 17], F32)
    W17i = b.sbuf("eg_W17i", [64, 17], F32)
    U16r = b.sbuf("eg_U16r", [64, 16], F32)
    U16i = b.sbuf("eg_U16i", [64, 16], F32)
    tmp = b.sbuf("eg_tmp", [64, 1], F32)
    for (Wr, Wi, n, ar, br) in ((W17r, W17i, 17, a, bb), (U16r, U16i, 16, ia, ib)):
        b.op("gpsimd", lambda e, Wr=Wr: e.memset(Wr[:, 0:1], 1.0))
        b.op("gpsimd", lambda e, Wi=Wi: e.memset(Wi[:, 0:1], 0.0))
        for j in range(1, n):
            b.op("vector", lambda e, j=j, Wr=Wr, ar=ar: e.tensor_scalar(
                Wr[:, j:j + 1], Wr[:, j - 1:j], ar[:, 0:1], None, ALU.mult))
            b.op("vector", lambda e, j=j, Wi=Wi, br=br: e.tensor_scalar(
                tmp[:, :], Wi[:, j - 1:j], br[:, 0:1], None, ALU.mult))
            b.op("vector", lambda e, j=j, Wr=Wr: e.tensor_sub(
                Wr[:, j:j + 1], Wr[:, j:j + 1], tmp[:, :]))
            b.op("vector", lambda e, j=j, Wi=Wi, ar=ar: e.tensor_scalar(
                Wi[:, j:j + 1], Wi[:, j - 1:j], ar[:, 0:1], None, ALU.mult))
            b.op("vector", lambda e, j=j, Wr=Wr, Wi=Wi, br=br: e.tensor_scalar(
                tmp[:, :], Wr[:, j - 1:j], br[:, 0:1], None, ALU.mult))
            b.op("vector", lambda e, j=j, Wi=Wi: e.tensor_add(
                Wi[:, j:j + 1], Wi[:, j:j + 1], tmp[:, :]))
    mask = b.sbuf("eg_mask", [64, 2 * TT], F32)
    b.op("gpsimd", lambda e: e.memset(mask[:, :], 1.0))
    b.op("gpsimd", lambda e: e.memset(mask[:, 0:2 * TT:SEG], 0.0))
    eg.update(W17r=W17r, W17i=W17i, U16r=U16r, U16i=U16i, mask=mask)
    return eg


def emit_eigenstate(b, eg, bt, cout, n, carry_ap, scratch):
    W17r, W17i = eg["W17r"], eg["W17i"]
    U16r, U16i = eg["U16r"], eg["U16i"]
    y, p, z, G = scratch["y"], scratch["p"], scratch["z"], scratch["G"]
    ta, tb = scratch["ta"], scratch["tb"]
    nseg = n // SEG
    nch = n // CH

    def b16(t, rev=False):
        ap = t[:, 16:0:-1] if rev else t[:, 0:16]
        return ap.unsqueeze(1).broadcast_to([64, 2 * nseg, 16])

    def b16h(t, rev=False):
        ap = t[:, 16:0:-1] if rev else t[:, 0:16]
        return ap.unsqueeze(1).broadcast_to([64, nseg, 16])

    def r3(ap):
        return ap.rearrange("p (g f) -> p g f", f=16)

    tc, td = scratch["tc"], scratch["td"]
    b.op("vector", lambda e: e.tensor_mul(r3(y[:, 0:n]), r3(bt[:, 0:n]), b16h(W17r)))
    b.op("vector", lambda e: e.tensor_mul(r3(ta[:, 0:n]), r3(bt[:, n:2 * n]), b16h(W17i)))
    b.op("vector", lambda e: e.tensor_sub(y[:, 0:n], y[:, 0:n], ta[:, 0:n]))
    b.op("gpsimd", lambda e: e.tensor_mul(r3(y[:, n:2 * n]), r3(bt[:, n:2 * n]), b16h(W17r)))
    b.op("gpsimd", lambda e: e.tensor_mul(r3(tc[:, 0:n]), r3(bt[:, 0:n]), b16h(W17i)))
    b.op("gpsimd", lambda e: e.tensor_add(y[:, n:2 * n], y[:, n:2 * n], tc[:, 0:n]))
    b.op("vector", lambda e: e.tensor_tensor_scan(
        p[:, 0:n], eg["mask"][:, 0:n], y[:, n - 1::-1], 0.0, ALU.mult, ALU.add))
    b.op("vector", lambda e: e.tensor_tensor_scan(
        p[:, n:2 * n], eg["mask"][:, 0:n], y[:, 2 * n - 1:n - 1:-1], 0.0, ALU.mult, ALU.add))
    sufr = p[:, n - 1::-1]
    sufi = p[:, 2 * n - 1:n - 1:-1]
    b.op("vector", lambda e: e.tensor_mul(r3(z[:, 0:n]), r3(sufr), b16h(U16r)))
    b.op("vector", lambda e: e.tensor_mul(r3(ta[:, 0:n]), r3(sufi), b16h(U16i)))
    b.op("vector", lambda e: e.tensor_sub(z[:, 0:n], z[:, 0:n], ta[:, 0:n]))
    b.op("gpsimd", lambda e: e.tensor_mul(r3(z[:, n:2 * n]), r3(sufi), b16h(U16r)))
    b.op("gpsimd", lambda e: e.tensor_mul(r3(tc[:, 0:n]), r3(sufr), b16h(U16i)))
    b.op("gpsimd", lambda e: e.tensor_add(z[:, n:2 * n], z[:, n:2 * n], tc[:, 0:n]))
    a16r = W17r[:, 16:17]
    a16i = W17i[:, 16:17]

    def Gs(m, imag):
        base = nseg if imag else 0
        return G[:, base + m::4][:, 0:nch]

    def Ts(m, imag):
        base = n if imag else 0
        return z[:, base + m * 16::CH][:, 0:nch]
    b.op("gpsimd", lambda e: e.memset(Gs(3, False), 0.0))
    b.op("gpsimd", lambda e: e.memset(Gs(3, True), 0.0))
    b.op("vector", lambda e: e.tensor_copy(Gs(2, False), Ts(3, False)))
    b.op("vector", lambda e: e.tensor_copy(Gs(2, True), Ts(3, True)))
    for m in (1, 0):
        b.op("vector", lambda e, m=m: e.scalar_tensor_tensor(
            Gs(m, False), Gs(m + 1, False), a16r[:, 0:1], Ts(m + 1, False), ALU.mult, ALU.add))
        b.op("vector", lambda e, m=m: e.tensor_scalar(
            ta[:, 0:nch], Gs(m + 1, True), a16i[:, 0:1], None, ALU.mult))
        b.op("vector", lambda e, m=m: e.tensor_sub(Gs(m, False), Gs(m, False), ta[:, 0:nch]))
        b.op("vector", lambda e, m=m: e.scalar_tensor_tensor(
            Gs(m, True), Gs(m + 1, True), a16r[:, 0:1], Ts(m + 1, True), ALU.mult, ALU.add))
        b.op("vector", lambda e, m=m: e.tensor_scalar(
            ta[:, 0:nch], Gs(m + 1, False), a16i[:, 0:1], None, ALU.mult))
        b.op("vector", lambda e, m=m: e.tensor_add(Gs(m, True), Gs(m, True), ta[:, 0:nch]))
    GrB = G[:, 0:nseg].unsqueeze(2).broadcast_to([64, nseg, 16])
    GiB = G[:, nseg:2 * nseg].unsqueeze(2).broadcast_to([64, nseg, 16])
    b.op("vector", lambda e: e.tensor_mul(r3(ta[:, 0:n]), b16h(W17r, True), GrB))
    b.op("vector", lambda e: e.tensor_mul(r3(tb[:, 0:n]), b16h(W17i, True), GiB))
    b.op("vector", lambda e: e.tensor_sub(ta[:, 0:n], ta[:, 0:n], tb[:, 0:n]))
    b.op("vector", lambda e: e.tensor_add(cout[:, 0:n], z[:, 0:n], ta[:, 0:n]))
    b.op("gpsimd", lambda e: e.tensor_mul(r3(tc[:, 0:n]), b16h(W17r, True), GiB))
    b.op("gpsimd", lambda e: e.tensor_mul(r3(td[:, 0:n]), b16h(W17i, True), GrB))
    b.op("gpsimd", lambda e: e.tensor_add(tc[:, 0:n], tc[:, 0:n], td[:, 0:n]))
    b.op("gpsimd", lambda e: e.tensor_add(cout[:, n:2 * n], z[:, n:2 * n], tc[:, 0:n]))
    if carry_ap is not None:
        b.op("vector", lambda e: e.tensor_add(cout[:, 0:1], cout[:, 0:1], carry_ap[:, 0:1]))
        b.op("gpsimd", lambda e: e.tensor_add(cout[:, n:n + 1], cout[:, n:n + 1], carry_ap[:, 1:2]))
    if nch > 1:
        b.op("vector", lambda e: e.tensor_add(
            cout[:, CH::CH][:, 0:nch - 1], cout[:, CH::CH][:, 0:nch - 1],
            bt[:, CH - 1::CH][:, 0:nch - 1]))
        b.op("gpsimd", lambda e: e.tensor_add(
            cout[:, n + CH::CH][:, 0:nch - 1], cout[:, n + CH::CH][:, 0:nch - 1],
            bt[:, n + CH - 1::CH][:, 0:nch - 1]))


def mk_eig_scratch(b, n):
    y = b.sbuf("eg_y", [64, 2 * n], F32)
    return {
        "y": y,
        "p": b.sbuf("eg_p", [64, 2 * n], F32),
        "z": y,
        "G": b.sbuf("eg_G", [64, 2 * (n // SEG)], F32),
        "ta": b.sbuf("eg_ta", [64, n], F32),
        "tb": b.sbuf("eg_tb", [64, n], F32),
        "tc": b.sbuf("eg_tc", [64, n], F32),
        "td": b.sbuf("eg_td", [64, n], F32),
    }



def emit_l1a(b, cst, d):
    b.push()
    w_gate = load_w(b, "w_gate_s", d["w_gate"], 8, 8)
    w_in_r = b.sbuf("w_in_r_s", [128, 8, 64], BF16)
    b.op("gpsimd", lambda e: e.dma_start(out=w_in_r[:, :, :], in_=d["w_in_r"][:, :, :]), dma=True,
         extra={"sdma": b.val["sdma"], "sync": b.val["sync"]})
    w_in_i = b.sbuf("w_in_i_s", [128, 8, 64], BF16)
    b.op("gpsimd", lambda e: e.dma_start(out=w_in_i[:, :, :], in_=d["w_in_i"][:, :, :]), dma=True)
    wor = b.sbuf("wor_s", [64, 8, 128], BF16)
    b.op("gpsimd", lambda e: e.dma_start(out=wor[:, :, :], in_=d["w_out_r"][:, :, :]), dma=True)
    woi = b.sbuf("woi_s", [64, 8, 128], BF16)
    b.op("gpsimd", lambda e: e.dma_start(out=woi[:, :, :], in_=d["w_out_i"][:, :, :]), dma=True)
    b_gate = load_cols(b, "b_gate_s", d["b_gate"], 8)
    ln1g = load_cols(b, "ln1g_s", d["ln1g"], 8)
    ln1b = load_cols(b, "ln1b_s", d["ln1b"], 8)
    ld = b.sbuf("ld_s", [64, 1], F32)
    b.op("gpsimd", lambda e: e.dma_start(out=ld[:, :], in_=d["ldecay"][:, :]), dma=True)
    fr = b.sbuf("fr_s", [64, 1], F32)
    b.op("gpsimd", lambda e: e.dma_start(out=fr[:, :], in_=d["freq"][:, :]), dma=True)
    flag = load_cols(b, "flag_s", d["flag"], 1)
    wcp = b.sbuf("wcp", [64, 64], F32)
    b.op("gpsimd", lambda e: e.memset(wcp[:, :], 0.0),
         extra={"sdma": b.val["sdma"], "sync": b.val["sync"]})
    for h in range(4):
        o = 16 * h
        b.op("gpsimd", lambda e, o=o, h=h: e.dma_start(
            out=wcp[o:o + 16, o:o + 16], in_=d["coupT"][h, :, :]), dma=True)
    eg = emit_eig_tables(b, cst, ld, fr)
    ln = LNScratch(b, TT, cst)
    egs = mk_eig_scratch(b, TT)

    xt = [b.sbuf("xt0", [128, NCHUNK, TT], F32), b.sbuf("xt1", [128, NCHUNK, TT], F32),
          b.sbuf("xt2", [128, NCHUNK, TT], F32)]
    y1 = [b.sbuf("y10", [128, NCHUNK, TT], F32), b.sbuf("y11", [128, NCHUNK, TT], F32),
          b.sbuf("y12", [128, NCHUNK, TT], F32)]
    xnb = [b.sbuf("xnb0", [128, NCHUNK, TT], BF16), b.sbuf("xnb1", [128, NCHUNK, TT], BF16)]
    btb = [b.sbuf("bt0", [64, 2 * TT], F32), b.sbuf("bt1", [64, 2 * TT], F32)]
    ct = egs["p"]
    cpb = [b.sbuf("cpb0", [64, 2 * TT], BF16), b.sbuf("cpb1", [64, 2 * TT], BF16)]
    g8 = b.sbuf("g8", [128, NCHUNK, TT], F32)
    ho = b.sbuf("ho", [128, 2, TT], F32)
    carryT = b.sbuf("carryT", [64, 2 * NTILE], F32)
    carryB = b.sbuf("carryB", [64, 2], F32)
    zcol = b.sbuf("zcol", [64, 2], F32)
    ps1 = b.psum_pair("ps1", [128, TT], F32)
    ps2 = b.psum_pair("ps2", [128, TT], F32)
    psBr = b.psum("psBr", [64, TT], F32)
    psBi = b.psum("psBi", [64, TT], F32)
    b.op("gpsimd", lambda e: e.memset(zcol[:, :], 0.0))
    ldm = {}
    stm = {}
    sd0 = b.val["sdma"]

    def load(t):
        if 0 <= t < NTILE:
            b.op("sync", lambda e, t=t: e.dma_start(
                out=xt[t % 3][:, :, :], in_=d["xs"][:, :, t * TT:(t + 1) * TT]), dma=True,
                extra={"sdma": sd0})
            ldm[t] = b.val["sync"]

    mB = ln.m[:, :].unsqueeze(1).broadcast_to([128, NCHUNK, TT])
    A2B = None
    AB = ln.A[:, :].unsqueeze(1).broadcast_to([128, NCHUNK, TT])
    gB = ln1g[:, :].unsqueeze(2).broadcast_to([128, NCHUNK, TT])
    bB = ln1b[:, :].unsqueeze(2).broadcast_to([128, NCHUNK, TT])

    def statsPart(t):
        x = xt[t % 3]
        ln.statsA(TT, x[:, :, :], first={"sync": ldm[t]})
        statsEnd[t] = b.now()

    def applyPart(t):
        nonlocal A2B
        x = xt[t % 3]
        yy = y1[t % 3]
        xb = xnb[t % 2]
        bb = btb[t % 2]
        ln.statsB(TT, comp=True)
        if A2B is None:
            A2B = ln.A2[:, :].unsqueeze(1).broadcast_to([128, NCHUNK, TT])
        b.op("vector", lambda e, x=x, yy=yy: e.tensor_sub(yy[:, :, :], x[:, :, :], mB),
             extra=({"sdma": stm[t - 3]} if t - 3 in stm else None))
        b.op("vector", lambda e, yy=yy, xb=xb: e.tensor_mul(xb[:, :, :], yy[:, :, :], A2B))
        b.op("vector", lambda e, xb=xb: e.tensor_mul(xb[:, :, :], xb[:, :, :], gB))
        b.op("vector", lambda e, xb=xb: e.tensor_add(xb[:, :, :], xb[:, :, :], bB))
        applySnap[t] = b.now()
        b.op("gpsimd", lambda e, yy=yy: e.tensor_mul(yy[:, :, :], yy[:, :, :], AB))
        b.mm(psBr[:, :], [(w_in_r[:, k, :], xb[:, k, :]) for k in range(8)])
        b.op("vector", lambda e, bb=bb: e.tensor_copy(bb[:, 0:TT], psBr[:, :]))
        b.mm(psBi[:, :], [(w_in_i[:, k, :], xb[:, k, :]) for k in range(8)])
        b.op("vector", lambda e, bb=bb: e.tensor_copy(bb[:, TT:2 * TT], psBi[:, :]))
        b.op("vector", lambda e, t=t, bb=bb: e.tensor_copy(
            carryT[:, 2 * t:2 * t + 1], bb[:, TT - 1:TT]))
        b.op("vector", lambda e, t=t, bb=bb: e.tensor_copy(
            carryT[:, 2 * t + 1:2 * t + 2], bb[:, 2 * TT - 1:2 * TT]))
        if t == 0:
            b.op("vector", lambda e: e.tensor_mul(
                carryB[:, :], carryT[:, 0:2], flag[0:64, 0:1].broadcast_to([64, 2])))

    def preB(t):
        bb = btb[t % 2]
        cp = cpb[t % 2]
        if t == 0:
            ccol = zcol
        elif t == 1:
            ccol = carryB
        else:
            ccol = carryT[:, 2 * (t - 1):2 * t]
        emit_eigenstate(b, eg, bb, ct, TT, ccol[:, 0:2], egs)
        b.mm(psBr[:, :], [(wcp[:, :], ct[:, 0:TT])])
        b.op("vector", lambda e, cp=cp: e.tensor_copy(cp[:, 0:TT], psBr[:, :]))
        b.mm(psBi[:, :], [(wcp[:, :], ct[:, TT:2 * TT])])
        b.op("vector", lambda e, cp=cp: e.tensor_copy(cp[:, TT:2 * TT], psBi[:, :]))

    def gatePipe(t, start):
        xb = xnb[t % 2]
        return b.pipe(8, [
            ("tensor", lambda e, i, xb=xb: b.mmg(e, ps1[i % 2][:, :], [(w_gate[:, k, i, :], xb[:, k, :]) for k in range(8)])),
            ("scalar", lambda e, i: e.activation(
                g8[:, i, :], ps1[i % 2][:, :], AF.Sigmoid, bias=b_gate[:, i:i + 1], scale=1.0)),
        ], start=start)

    def woPipe(t, start):
        cp = cpb[t % 2]
        yy = y1[t % 3]
        return b.pipe(8, [
            ("tensor", lambda e, i, cp=cp: b.mmg(e, ps2[i % 2][:, :], [(wor[:, i, :], cp[:, 0:TT]), (woi[:, i, :], cp[:, TT:2 * TT])])),
            ("scalar", lambda e, i: e.activation(
                ho[:, i % 2, :], ps2[i % 2][:, :], AF.Copy, bias=0.0, scale=1.0)),
            ("gpsimd", lambda e, i: e.tensor_mul(ho[:, i % 2, :], ho[:, i % 2, :], g8[:, i, :])),
            ("gpsimd", lambda e, i, yy=yy: e.tensor_add(yy[:, i, :], yy[:, i, :], ho[:, i % 2, :])),
        ], start=start)

    def post(t):
        yy = y1[t % 3]
        b.op("scalar", lambda e, t=t, yy=yy: e.dma_start(
            out=d["x1"][:, :, t * TT:(t + 1) * TT], in_=yy[:, :, :]), dma=True)
        stm[t] = b.val["sdma"]
        load(t + 3)

    load(0)
    load(1)
    load(2)
    applySnap = {}
    statsEnd = {}
    applyEnd = {}
    preBEnd = {}
    gateEnd = {}
    woEnd = {}
    b.scope_push()
    statsPart(0)
    b.scope_pop()
    b.scope_push()
    applyPart(0)
    b.scope_pop()
    applyEnd[0] = b.now()
    for t in range(NTILE):
        if t + 1 < NTILE:
            b.scope_push(base=applyEnd[t])
            statsPart(t + 1)
            b.scope_pop()
        gm = gatePipe(t, mmax(applySnap[t], gateEnd.get(t - 1), woEnd.get(t - 1),
                              {"sdma": stm[t - 2]} if t - 2 in stm else None))
        gateEnd[t] = b.now()
        b.scope_push(base=mmax(applyEnd[t], preBEnd.get(t - 1)))
        preB(t)
        b.scope_pop()
        preBEnd[t] = b.now()
        woPipe(t, mmax(preBEnd[t], gm, woEnd.get(t - 1)))
        woEnd[t] = b.now()
        if t + 1 < NTILE:
            b.scope_push(base=mmax(statsEnd[t + 1], preBEnd[t]))
            applyPart(t + 1)
            b.scope_pop()
            applyEnd[t + 1] = b.now()
        if t - 1 >= 0:
            b.scope_push()
            post(t - 1)
            b.scope_pop()
    b.scope_push()
    post(NTILE - 1)
    b.scope_pop()
    b.pop()


def emit_l1b(b, cst, d):
    N = 256
    NT = TW // N
    b.push()
    w1 = load_w(b, "w1_s", d["w_mlp1"], 8, 32)
    w2 = load_w(b, "w2_s", d["w_mlp2"], 32, 8)
    b1 = load_cols(b, "b1_s", d["b_mlp1"], 32)
    b2 = load_cols(b, "b2_s", d["b_mlp2"], 8)
    ln2g = load_cols(b, "ln2g_s", d["ln2g"], 8)
    ln2b = load_cols(b, "ln2b_s", d["ln2b"], 8)
    ln = LNScratch(b, N, cst, alloc_xb=False)
    lnP = LNScratch(b, N, cst, alloc_xb=False)
    x1t = [b.sbuf("x1t0", [128, NCHUNK, N], F32), b.sbuf("x1t1", [128, NCHUNK, N], F32)]
    xst = [b.sbuf("xst0", [128, NCHUNK, N], F32), b.sbuf("xst1", [128, NCHUNK, N], F32)]
    xn2b = [b.sbuf("xn2b0", [128, NCHUNK, N], BF16), b.sbuf("xn2b1", [128, NCHUNK, N], BF16)]
    h1b = b.sbuf("h1b", [128, 32, N], BF16)
    ao = [b.sbuf("ao0", [128, NCHUNK, N], F32), b.sbuf("ao1", [128, NCHUNK, N], F32)]
    ps1 = b.psum_pair("ps1", [128, N], F32)
    ldm = {}
    stm = {}
    sd0 = b.val["sdma"]

    def loadx1(t):
        if 0 <= t < NT:
            b.op("sync", lambda e, t=t: e.dma_start(
                out=x1t[t % 2][:, :, :], in_=d["x1"][:, :, t * N:(t + 1) * N]), dma=True,
                extra={"sdma": sd0})
            ldm[t] = b.val["sync"]

    def loadxs(t):
        if 0 <= t < NT:
            b.op("sync", lambda e, t=t: e.dma_start(
                out=xst[t % 2][:, :, :], in_=d["xs"][:, :, t * N:(t + 1) * N]), dma=True,
                extra={"sdma": sd0})

    def load(t):
        loadx1(t)
        loadxs(t)

    mB = ln.m[:, :N].unsqueeze(1).broadcast_to([128, NCHUNK, N])
    AB = ln.A[:, :N].unsqueeze(1).broadcast_to([128, NCHUNK, N])
    gB = ln2g[:, :].unsqueeze(2).broadcast_to([128, NCHUNK, N])
    bB = ln2b[:, :].unsqueeze(2).broadcast_to([128, NCHUNK, N])
    mPB = lnP.m[:, :N].unsqueeze(1).broadcast_to([128, NCHUNK, N])
    APB = lnP.A[:, :N].unsqueeze(1).broadcast_to([128, NCHUNK, N])

    def preA_(t):
        ln.statsA_f32(N, x1t[t % 2], h1b[:, 0:8, :], first={"sync": ldm[t]})
        preAEnd[t] = b.now()

    def preRest(t):
        x1 = x1t[t % 2]
        xb = xn2b[t % 2]
        ln.statsB(N)
        b.op("vector", lambda e, x1=x1, xb=xb: e.tensor_sub(xb[:, :, :], x1[:, :, :], mB))
        b.op("vector", lambda e, xb=xb: e.tensor_mul(xb[:, :, :], xb[:, :, :], AB))
        b.op("vector", lambda e, xb=xb: e.tensor_mul(xb[:, :, :], xb[:, :, :], gB))
        b.op("vector", lambda e, xb=xb: e.tensor_add(xb[:, :, :], xb[:, :, :], bB))

    def pipe(t, start):
        x1 = x1t[t % 2]
        xb = xn2b[t % 2]
        a = ao[t % 2]
        m1 = b.pipe(32, [
            ("tensor", lambda e, i, xb=xb: b.mmg(e, ps1[i % 2][:, 0:N], [(w1[:, k, i, :], xb[:, k, :]) for k in range(8)])),
            ("scalar", lambda e, i: e.activation(
                h1b[:, i, :], ps1[i % 2][:, 0:N], AF.Silu, bias=b1[:, i:i + 1], scale=1.0)),
        ], start=start)
        b.pipe(8, [
            ("tensor", lambda e, i: b.mmg(e, ps1[i % 2][:, 0:N], [(w2[:, k, i, :], h1b[:, k, :]) for k in range(32)])),
            ("scalar", lambda e, i, a=a: e.activation(
                a[:, i, :], ps1[i % 2][:, 0:N], AF.Identity, bias=b2[:, i:i + 1], scale=1.0)),
            ("gpsimd", lambda e, i, a=a, x1=x1: e.tensor_add(a[:, i, :], a[:, i, :], x1[:, i, :])),
        ], start=mmax(start, m1))

    def postA(t):
        lnP.statsA_f32(N, ao[t % 2], h1b[:, 8:16, :])
        postAEnd[t] = b.now()

    def postRest(t):
        xs = xst[t % 2]
        a = ao[t % 2]
        lnP.statsB(N)
        b.op("vector", lambda e, a=a: e.tensor_sub(a[:, :, :], a[:, :, :], mPB))
        b.op("vector", lambda e, a=a: e.tensor_mul(a[:, :, :], a[:, :, :], APB))
        b.op("vector", lambda e, a=a, xs=xs: e.tensor_add(a[:, :, :], a[:, :, :], xs[:, :, :]))
        b.op("scalar", lambda e, t=t, a=a: e.dma_start(
            out=d["s1"][:, :, t * N:(t + 1) * N], in_=a[:, :, :]), dma=True)
        stm[t] = b.val["sdma"]
        loadxs(t + 2)

    load(0)
    load(1)
    preAEnd = {}
    preEnd = {}
    postAEnd = {}
    postEnd = {}
    pipeEnd = {}
    b.scope_push()
    preA_(0)
    b.scope_pop()
    b.scope_push()
    preRest(0)
    b.scope_pop()
    preEnd[0] = b.now()
    for t in range(NT):
        if t - 1 >= 0:
            b.scope_push(base=mmax(pipeEnd[t - 1], postEnd.get(t - 2)))
            postA(t - 1)
            b.scope_pop()
        if t + 1 < NT:
            b.scope_push(base=preEnd[t])
            preA_(t + 1)
            b.scope_pop()
        pipe(t, mmax(preEnd[t], pipeEnd.get(t - 1),
                     {"sdma": stm[t - 2]} if t - 2 in stm else None))
        pipeEnd[t] = b.now()
        if t + 1 < NT:
            b.scope_push(base=preAEnd[t + 1])
            preRest(t + 1)
            b.scope_pop()
            preEnd[t + 1] = b.now()
        loadx1(t + 2)
        if t - 1 >= 0:
            b.scope_push(base=postAEnd[t - 1])
            postRest(t - 1)
            b.scope_pop()
            postEnd[t - 1] = b.now()
    b.scope_push(base=b.now())
    postA(NT - 1)
    b.scope_pop()
    b.scope_push(base=postAEnd[NT - 1])
    postRest(NT - 1)
    b.scope_pop()
    b.pop()


def emit_l2(b, cst, d):
    emit_l2a(b, cst, d)
    emit_l2b(b, cst, d)


def emit_l2a(b, cst, d):
    SC = 1.0 / 127.0
    b.push()
    wg = load_w(b, "wg_s", d["wg"], 8, 24)
    wu = load_w(b, "wu_s", d["wu"], 8, 24)
    ln = LNScratch(b, TT, cst)
    xt = [b.sbuf("xt0", [128, NCHUNK, TT], F32), b.sbuf("xt1", [128, NCHUNK, TT], F32)]
    y2b = [b.sbuf("y2b0", [128, NCHUNK, TT], BF16), b.sbuf("y2b1", [128, NCHUNK, TT], BF16)]
    sg_t = b.sbuf("sg_t", [128, 2, TT], F32)
    ht = b.sbuf("ht", [128, 24, TT], BF16)
    ps1 = b.psum_pair("ps1", [128, TT], F32)
    ps2 = b.psum_pair("ps2", [128, TT], F32)
    ldm = {}
    stm = {}
    sd0 = b.val["sdma"]

    def load(t):
        if 0 <= t < NTILE:
            b.op("sync", lambda e, t=t: e.dma_start(
                out=xt[t % 2][:, :, :], in_=d["s1"][:, :, t * TT:(t + 1) * TT]), dma=True,
                extra={"sdma": sd0})
            ldm[t] = b.val["sync"]

    mB = ln.m[:, :].unsqueeze(1).broadcast_to([128, NCHUNK, TT])
    AB = ln.A[:, :].unsqueeze(1).broadcast_to([128, NCHUNK, TT])

    def preA_(t):
        ln.statsA(TT, xt[t % 2][:, :, :], first={"sync": ldm[t]})
        preAEnd[t] = b.now()

    def preRest(t):
        x = xt[t % 2]
        yb = y2b[t % 2]
        ln.statsB(TT)
        b.op("vector", lambda e, x=x, yb=yb: e.tensor_sub(yb[:, :, :], x[:, :, :], mB))
        b.op("vector", lambda e, yb=yb: e.tensor_mul(yb[:, :, :], yb[:, :, :], AB))

    def pipe(t, start):
        yb = y2b[t % 2]

        def l2mm(e, i, yb=yb):
            b.mmg(e, ps1[i % 2][:, :], [(wg[:, k, i, :], yb[:, k, :]) for k in range(8)])
            return b.mmg(e, ps2[i % 2][:, :], [(wu[:, k, i, :], yb[:, k, :]) for k in range(8)])
        b.pipe(24, [
            ("tensor", l2mm),
            ("scalar", lambda e, i: e.activation(
                sg_t[:, i % 2, :], ps1[i % 2][:, :], AF.Silu, bias=0.0, scale=SC)),
            ("vector", lambda e, i: e.scalar_tensor_tensor(
                ht[:, i, :], ps2[i % 2][:, :], SC, sg_t[:, i % 2, :], ALU.mult, ALU.mult)),
        ], start=start)

    def post(t):
        b.op("scalar", lambda e, t=t: e.dma_start(
            out=d["hq"][:, :, t * TT:(t + 1) * TT], in_=ht[:, :, :]), dma=True)
        stm[t] = b.val["sdma"]
        load(t + 2)


    load(0)
    load(1)
    preAEnd = {}
    preEnd = {}
    pipeEnd = {}
    b.scope_push()
    preA_(0)
    b.scope_pop()
    b.scope_push()
    preRest(0)
    b.scope_pop()
    preEnd[0] = b.now()
    pipe(0, preEnd[0])
    pipeEnd[0] = b.now()
    for t in range(NTILE):
        if t + 1 < NTILE:
            b.scope_push(base=preEnd[t])
            preA_(t + 1)
            b.scope_pop()
        b.scope_push()
        post(t)
        b.scope_pop()
        if t + 1 < NTILE:
            b.scope_push(base=preAEnd[t + 1])
            preRest(t + 1)
            b.scope_pop()
            preEnd[t + 1] = b.now()
            pipe(t + 1, mmax(preEnd[t + 1], pipeEnd[t], {"sdma": stm[t]}))
            pipeEnd[t + 1] = b.now()
    b.pop()


def emit_l2b(b, cst, d):
    SC = 1.0 / 127.0
    b.push()
    wd = load_w(b, "wd_s", d["wd"], 24, 8)
    ln = LNScratch(b, TT, cst)
    xt = [b.sbuf("xt0", [128, NCHUNK, TT], F32), b.sbuf("xt1", [128, NCHUNK, TT], F32)]
    ht = [b.sbuf("ht0", [128, 24, TT], BF16), b.sbuf("ht1", [128, 24, TT], BF16)]
    Ft = [b.sbuf("Ft0", [128, NCHUNK, TT], F32), b.sbuf("Ft1", [128, NCHUNK, TT], F32)]
    ps1 = b.psum_pair("ps1", [128, TT], F32)
    ldm = {}
    stm = {}
    sd0 = b.val["sdma"]

    def loadxt(t):
        if 0 <= t < NTILE:
            b.op("sync", lambda e, t=t: e.dma_start(
                out=xt[t % 2][:, :, :], in_=d["s1"][:, :, t * TT:(t + 1) * TT]), dma=True,
                extra={"sdma": sd0})

    def loadht(t):
        if 0 <= t < NTILE:
            b.op("sync", lambda e, t=t: e.dma_start(
                out=ht[t % 2][:, :, :], in_=d["hq"][:, :, t * TT:(t + 1) * TT]), dma=True,
                extra={"sdma": sd0})
            ldm[t] = b.val["sync"]

    def load(t):
        loadxt(t)
        loadht(t)

    def pipe(t, start):
        h = ht[t % 2]
        f = Ft[t % 2]
        ex = {"sync": ldm[t], "gdma": b.val["gdma"]}
        if t - 2 in stm:
            ex["sdma"] = stm[t - 2]
        b.pipe(8, [
            ("tensor", lambda e, i, h=h: b.mmg(
                e, ps1[i % 2][:, :], [(wd[:, k, i, :], h[:, k, :]) for k in range(24)])),
            ("scalar", lambda e, i, f=f: e.activation(
                f[:, i, :], ps1[i % 2][:, :], AF.Identity, bias=0.0, scale=SC)),
        ], start=dict(start), extra=ex)

    mB = ln.m[:, :].unsqueeze(1).broadcast_to([128, NCHUNK, TT])
    AB = ln.A[:, :].unsqueeze(1).broadcast_to([128, NCHUNK, TT])

    def postA(t):
        ln.statsA(TT, Ft[t % 2][:, :, :])
        postAEnd[t] = b.now()

    def postRest(t):
        x = xt[t % 2]
        f = Ft[t % 2]
        ln.statsB(TT)
        b.op("vector", lambda e, f=f: e.tensor_sub(f[:, :, :], f[:, :, :], mB))
        b.op("vector", lambda e, f=f: e.tensor_mul(f[:, :, :], f[:, :, :], AB))
        b.op("vector", lambda e, f=f, x=x: e.tensor_add(f[:, :, :], f[:, :, :], x[:, :, :]))
        b.op("scalar", lambda e, t=t, f=f: e.dma_start(
            out=d["s2"][:, :, t * TT:(t + 1) * TT], in_=f[:, :, :]), dma=True)
        stm[t] = b.val["sdma"]
        loadxt(t + 2)

    load(0)
    load(1)
    pipeEnd = {}
    postAEnd = {}
    postEnd = {}
    pipe(0, b.now())
    pipeEnd[0] = b.now()
    for t in range(NTILE):
        b.scope_push(base=mmax(pipeEnd[t], postEnd.get(t - 1)))
        postA(t)
        b.scope_pop()
        if t + 1 < NTILE:
            pipe(t + 1, mmax(pipeEnd[t], postEnd.get(t - 1)))
            pipeEnd[t + 1] = b.now()
            loadht(t + 2)
        b.scope_push(base=postAEnd[t])
        postRest(t)
        b.scope_pop()
        postEnd[t] = b.now()
    b.pop()


def emit_l3(b, cst, d):
    b.push()
    wk = load_w(b, "wk_s", d["wk"], 8, 8)
    wv = load_w(b, "wv_s", d["wv"], 8, 8)
    wr = load_w(b, "wr_s", d["wr"], 8, 8)
    wo = load_w(b, "wo_s", d["wo"], 8, 8)
    lnw = load_cols(b, "lnw_s", d["lnw"], 8)
    lnb = load_cols(b, "lnb_s", d["lnb"], 8)
    tmk = load_cols(b, "tmk_s", d["tmk"], 8)
    tmv = load_cols(b, "tmv_s", d["tmv"], 8)
    tmr = load_cols(b, "tmr_s", d["tmr"], 8)
    dw = load_cols(b, "dw_s", d["dw"], 8)
    flag = load_cols(b, "flag_s", d["flag"], 1)
    dec = b.sbuf("dec", [128, 8], F32)
    b.op("scalar", lambda e: e.activation(dec[:, :], dw[:, :], AF.Sigmoid, bias=0.0, scale=1.0),
         extra={"sync": b.val["sync"]})
    omk = b.sbuf("omk", [128, 8], F32)
    omv = b.sbuf("omv", [128, 8], F32)
    omr = b.sbuf("omr", [128, 8], F32)
    for om, tm in ((omk, tmk), (omv, tmv), (omr, tmr)):
        b.op("vector", lambda e, om=om, tm=tm: e.tensor_scalar(
            om[:, :], tm[:, :], -1.0, 1.0, ALU.mult, ALU.add))
    ln = LNScratch(b, TT, cst)
    xt = [b.sbuf("xt0", [128, NCHUNK, TT], F32), b.sbuf("xt1", [128, NCHUNK, TT], F32)]
    y3 = b.sbuf("y3", [128, NCHUNK, TT], F32)
    xn = b.sbuf("xn", [128, NCHUNK, TT + 1], F32)
    mixk = b.sbuf("mixk", [128, NCHUNK, TT], BF16)
    mixv = b.sbuf("mixv", [128, NCHUNK, TT], BF16)
    mixr = b.sbuf("mixr", [128, NCHUNK, TT], BF16)
    kf = b.sbuf("kf", [128, 2, TT], F32)
    kv = b.sbuf("kv", [128, 2, TT], F32)
    ssq = b.sbuf("ssq", [128, 2, TT], F32)
    ssb = b.sbuf("ssb", [128, NCHUNK, TT], BF16)
    rf = b.sbuf("rf", [128, NCHUNK, TT], F32)
    scol = b.sbuf("scol", [128, 8], F32)
    xcar = b.sbuf("xcar", [128, 8], F32)
    psK = b.psum_pair("psK", [128, TT], F32)
    psV = b.psum_pair("psV", [128, TT], F32)
    psR = b.psum_pair("psR", [128, TT], F32)
    b.op("gpsimd", lambda e: e.memset(scol[:, :], 0.0),
         extra={"sdma": b.val["sdma"], "sync": b.val["sync"]})
    b.op("gpsimd", lambda e: e.memset(xcar[:, :], 0.0))
    ldm = {}
    stm = {}

    sd0 = b.val["sdma"]

    def load(t):
        if 0 <= t < NTILE:
            b.op("sync", lambda e, t=t: e.dma_start(
                out=xt[t % 2][:, :, :], in_=d["s2"][:, :, t * TT:(t + 1) * TT]), dma=True,
                extra={"sdma": sd0})
            ldm[t] = b.val["sync"]
    mB = ln.m[:, :].unsqueeze(1).broadcast_to([128, NCHUNK, TT])
    AB = ln.A[:, :].unsqueeze(1).broadcast_to([128, NCHUNK, TT])
    gB = lnw[:, :].unsqueeze(2).broadcast_to([128, NCHUNK, TT])
    bB = lnb[:, :].unsqueeze(2).broadcast_to([128, NCHUNK, TT])
    A2B = None
    load(0)
    for t in range(NTILE):
        load(t + 1)
        x = xt[t % 2]
        ln.stats(None, TT, comp=True, first={"sync": ldm[t]}, x_all=x[:, :, :])
        if A2B is None:
            A2B = ln.A2[:, :].unsqueeze(1).broadcast_to([128, NCHUNK, TT])
        b.op("vector", lambda e, x=x: e.tensor_sub(y3[:, :, :], x[:, :, :], mB),
             extra=({"sdma": stm[t - 1]} if t - 1 in stm else None))
        b.op("vector", lambda e: e.tensor_mul(xn[:, :, 1:TT + 1], y3[:, :, :], A2B))
        b.op("vector", lambda e: e.tensor_mul(xn[:, :, 1:TT + 1], xn[:, :, 1:TT + 1], gB))
        b.op("vector", lambda e: e.tensor_add(xn[:, :, 1:TT + 1], xn[:, :, 1:TT + 1], bB))
        b.op("gpsimd", lambda e: e.tensor_mul(y3[:, :, :], y3[:, :, :], AB))
        b.op("vector", lambda e: e.tensor_copy(xn[:, :, 0:1], xcar[:, :].unsqueeze(2)))
        b.op("vector", lambda e: e.tensor_copy(xcar[:, :], xn[:, :, TT:TT + 1].squeeze(2)))
        if t == 0:
            b.op("vector", lambda e: e.tensor_mul(
                xcar[:, :], xcar[:, :], flag[:, 0:1].broadcast_to([128, 8])))
        for mix, tm, om, meng in ((mixk, tmk, omk, "vector"), (mixv, tmv, omv, "gpsimd"),
                                  (mixr, tmr, omr, "vector")):
            tmB = tm[:, :].unsqueeze(2).broadcast_to([128, NCHUNK, TT])
            omB = om[:, :].unsqueeze(2).broadcast_to([128, NCHUNK, TT])
            sc = rf if meng == "vector" else ssb
            b.op(meng, lambda e, sc=sc, tmB=tmB: e.tensor_mul(sc[:, :, :], xn[:, :, 1:TT + 1], tmB))
            b.op(meng, lambda e, mix=mix, omB=omB: e.tensor_mul(mix[:, :, :], xn[:, :, 0:TT], omB))
            b.op(meng, lambda e, mix=mix, sc=sc: e.tensor_add(mix[:, :, :], mix[:, :, :], sc[:, :, :]))
        b.pipe(8, [
            ("tensor", lambda e, i: b.mmg(e, psK[i % 2][:, :], [(wk[:, k, i, :], mixk[:, k, :]) for k in range(8)])),
            ("scalar", lambda e, i: e.activation(kf[:, i % 2, :], psK[i % 2][:, :], AF.Copy, bias=0.0, scale=1.0)),
            ("tensor", lambda e, i: b.mmg(e, psV[i % 2][:, :], [(wv[:, k, i, :], mixv[:, k, :]) for k in range(8)])),
            ("vector", lambda e, i: e.tensor_mul(kv[:, i % 2, :], kf[:, i % 2, :], psV[i % 2][:, :])),
            ("vector", lambda e, i: e.tensor_tensor_scan(
                ssq[:, i % 2, :], dec[:, i:i + 1].broadcast_to([128, TT]), kv[:, i % 2, :],
                scol[:, i:i + 1], ALU.mult, ALU.add)),
            ("vector", lambda e, i: e.tensor_copy(scol[:, i:i + 1], ssq[:, i % 2, TT - 1:TT])),
            ("scalar", lambda e, i: e.activation(ssb[:, i, :], ssq[:, i % 2, :], AF.Copy, bias=0.0, scale=1.0)),
            ("tensor", lambda e, i: b.mmg(e, psR[i % 2][:, :], [(wr[:, k, i, :], mixr[:, k, :]) for k in range(8)])),
            ("scalar", lambda e, i: e.activation(rf[:, i, :], psR[i % 2][:, :], AF.Sigmoid, bias=0.0, scale=1.0)),
        ])
        if t == 0:
            b.op("vector", lambda e: e.tensor_mul(
                scol[:, :], scol[:, :], flag[:, 0:1].broadcast_to([128, 8])))
        b.pipe(8, [
            ("tensor", lambda e, i: b.mmg(e, psK[i % 2][:, :], [(wo[:, k, i, :], ssb[:, k, :]) for k in range(8)])),
            ("vector", lambda e, i: e.tensor_mul(rf[:, i, :], rf[:, i, :], psK[i % 2][:, :])),
            ("vector", lambda e, i: e.tensor_add(y3[:, i, :], y3[:, i, :], rf[:, i, :])),
        ])
        ln.stats(None, TT, x_all=y3[:, :, :])
        b.op("vector", lambda e: e.tensor_sub(y3[:, :, :], y3[:, :, :], mB))
        b.op("vector", lambda e: e.tensor_mul(y3[:, :, :], y3[:, :, :], AB))
        b.op("vector", lambda e, x=x: e.tensor_add(y3[:, :, :], y3[:, :, :], x[:, :, :]))
        b.op("scalar", lambda e, t=t: e.dma_start(out=d["s3"][:, :, t * TT:(t + 1) * TT], in_=y3[:, :, :]), dma=True)
        stm[t] = b.val["sdma"]
    b.pop()


def emit_l4(b, cst, d):
    b.push()
    u1 = load_w(b, "u1_s", d["u1"], 8, 2)
    v1 = load_w(b, "v1_s", d["v1"], 2, 32)
    u2 = load_w(b, "u2_s", d["u2"], 32, 2)
    v2 = load_w(b, "v2_s", d["v2"], 2, 8)
    b1 = load_cols(b, "b1_s", d["fb1"], 32)
    b2 = load_cols(b, "b2_s", d["fb2"], 8)
    ln = LNScratch(b, TT, cst)
    lnP = LNScratch(b, TT, cst)
    xt = [b.sbuf("xt0", [128, NCHUNK, TT], F32), b.sbuf("xt1", [128, NCHUNK, TT], F32),
          b.sbuf("xt2", [128, NCHUNK, TT], F32)]
    y4b = [b.sbuf("y4b0", [128, NCHUNK, TT], BF16), b.sbuf("y4b1", [128, NCHUNK, TT], BF16)]
    pb = b.sbuf("pb", [128, 2, TT], BF16)
    hb = b.sbuf("hb", [128, 32, TT], BF16)
    qb = b.sbuf("qb", [128, 2, TT], BF16)
    G = [b.sbuf("G40", [128, NCHUNK, TT], F32), b.sbuf("G41", [128, NCHUNK, TT], F32)]
    ps1 = b.psum_pair("ps41", [128, TT], F32)
    ldm = {}
    stm = {}
    sd0 = b.val["sdma"]

    def load(t):
        if 1 <= t < NTILE:
            b.op("sync", lambda e, t=t: e.dma_start(
                out=xt[t % 3][:, :, :], in_=d["s3"][:, :, t * TT:(t + 1) * TT]), dma=True,
                extra={"sdma": sd0})
            ldm[t] = b.val["sync"]

    mB = ln.m[:, :].unsqueeze(1).broadcast_to([128, NCHUNK, TT])
    AB = ln.A[:, :].unsqueeze(1).broadcast_to([128, NCHUNK, TT])
    mPB = lnP.m[:, :].unsqueeze(1).broadcast_to([128, NCHUNK, TT])
    APB = lnP.A[:, :].unsqueeze(1).broadcast_to([128, NCHUNK, TT])

    def preA_(t):
        ln.statsA(TT, xt[t % 3][:, :, :], first={"sync": ldm[t]})
        preAEnd[t] = b.now()

    def preRest(t):
        x = xt[t % 3]
        yb = y4b[t % 2]
        ln.statsB(TT)
        b.op("vector", lambda e, x=x, yb=yb: e.tensor_sub(yb[:, :, :], x[:, :, :], mB))
        b.op("vector", lambda e, yb=yb: e.tensor_mul(yb[:, :, :], yb[:, :, :], AB))

    def pipe(t, start):
        yb = y4b[t % 2]
        g = G[t % 2]
        m1 = b.pipe(2, [
            ("tensor", lambda e, i, yb=yb: b.mmg(e, ps1[i % 2][:, :], [(u1[:, k, i, :], yb[:, k, :]) for k in range(8)])),
            ("scalar", lambda e, i: e.activation(pb[:, i, :], ps1[i % 2][:, :], AF.Copy, bias=0.0, scale=1.0)),
        ], start=start)
        m2 = b.pipe(32, [
            ("tensor", lambda e, i: b.mmg(e, ps1[i % 2][:, :], [(v1[:, k, i, :], pb[:, k, :]) for k in range(2)])),
            ("scalar", lambda e, i: e.activation(
                hb[:, i, :], ps1[i % 2][:, :], AF.Gelu_apprx_tanh, bias=b1[:, i:i + 1], scale=1.0)),
        ], start=mmax(start, m1))
        m3 = b.pipe(2, [
            ("tensor", lambda e, i: b.mmg(e, ps1[i % 2][:, :], [(u2[:, k, i, :], hb[:, k, :]) for k in range(32)])),
            ("scalar", lambda e, i: e.activation(qb[:, i, :], ps1[i % 2][:, :], AF.Copy, bias=0.0, scale=1.0)),
        ], start=mmax(start, m2))
        b.pipe(8, [
            ("tensor", lambda e, i: b.mmg(e, ps1[i % 2][:, :], [(v2[:, k, i, :], qb[:, k, :]) for k in range(2)])),
            ("scalar", lambda e, i, g=g: e.activation(
                g[:, i, :], ps1[i % 2][:, :], AF.Identity, bias=b2[:, i:i + 1], scale=1.0)),
        ], start=mmax(start, m3))

    def postA(t):
        lnP.statsA(TT, G[t % 2][:, :, :])
        postAEnd[t] = b.now()

    def postRest(t):
        x = xt[t % 3]
        g = G[t % 2]
        lnP.statsB(TT)
        b.op("vector", lambda e, g=g: e.tensor_sub(g[:, :, :], g[:, :, :], mPB))
        b.op("vector", lambda e, g=g: e.tensor_mul(g[:, :, :], g[:, :, :], APB))
        b.op("vector", lambda e, g=g, x=x: e.tensor_add(g[:, :, :], g[:, :, :], x[:, :, :]))
        b.op("scalar", lambda e, t=t, g=g: e.dma_start(
            out=d["out"][:, :, (t - 1) * TT:t * TT], in_=g[:, :, :]), dma=True)
        stm[t] = b.val["sdma"]
        load(t + 3)

    load(1)
    load(2)
    load(3)
    preSnap = {}
    preAEnd = {}
    pipeEnd = {}
    postAEnd = {}
    postEnd = {}
    b.scope_push()
    preA_(1)
    b.scope_pop()
    b.scope_push()
    preRest(1)
    b.scope_pop()
    preSnap[1] = b.now()
    for t in range(1, NTILE):
        if t - 1 >= 1:
            b.scope_push(base=mmax(pipeEnd[t - 1], postEnd.get(t - 2)))
            postA(t - 1)
            b.scope_pop()
        if t + 1 < NTILE:
            b.scope_push(base=mmax(preSnap[t], postEnd.get(t - 2)))
            preA_(t + 1)
            b.scope_pop()
        pipe(t, mmax(preSnap[t], pipeEnd.get(t - 1), postEnd.get(t - 2),
                     {"sdma": stm[t - 2]} if t - 2 in stm else None))
        pipeEnd[t] = b.now()
        if t + 1 < NTILE:
            b.scope_push(base=preAEnd[t + 1])
            preRest(t + 1)
            b.scope_pop()
            preSnap[t + 1] = b.now()
        if t - 1 >= 1:
            b.scope_push(base=postAEnd[t - 1])
            postRest(t - 1)
            b.scope_pop()
            postEnd[t - 1] = b.now()
    b.scope_push()
    postA(NTILE - 1)
    b.scope_pop()
    b.scope_push(base=postAEnd[NTILE - 1])
    postRest(NTILE - 1)
    b.scope_pop()
    b.pop()



def build_fused():
    b = B()
    d = {}
    d["xs"] = b.dram_in("xs", [128, NCHUNK, TW])
    d["flag"] = b.dram_in("flag", [128, 1])
    d["w_in_r"] = b.dram_in("w_in_r", [128, 8, 64])
    d["w_in_i"] = b.dram_in("w_in_i", [128, 8, 64])
    d["w_gate"] = b.dram_in("w_gate", [128, 8, 8, 128])
    d["w_out_r"] = b.dram_in("w_out_r", [64, 8, 128])
    d["w_out_i"] = b.dram_in("w_out_i", [64, 8, 128])
    d["coupT"] = b.dram_in("coupT", [4, 16, 16])
    d["b_gate"] = b.dram_in("b_gate", [128, 8])
    d["ln1g"] = b.dram_in("ln1g", [128, 8])
    d["ln1b"] = b.dram_in("ln1b", [128, 8])
    d["ldecay"] = b.dram_in("ldecay", [64, 1])
    d["freq"] = b.dram_in("freq", [64, 1])
    d["w_mlp1"] = b.dram_in("w_mlp1", [128, 8, 32, 128])
    d["w_mlp2"] = b.dram_in("w_mlp2", [128, 32, 8, 128])
    d["b_mlp1"] = b.dram_in("b_mlp1", [128, 32])
    d["b_mlp2"] = b.dram_in("b_mlp2", [128, 8])
    d["ln2g"] = b.dram_in("ln2g", [128, 8])
    d["ln2b"] = b.dram_in("ln2b", [128, 8])
    d["wg"] = b.dram_in("wg", [128, 8, 24, 128], I32)
    d["wu"] = b.dram_in("wu", [128, 8, 24, 128], I32)
    d["wd"] = b.dram_in("wd", [128, 24, 8, 128], I32)
    d["wk"] = b.dram_in("wk", [128, 8, 8, 128])
    d["wv"] = b.dram_in("wv", [128, 8, 8, 128])
    d["wr"] = b.dram_in("wr", [128, 8, 8, 128])
    d["wo"] = b.dram_in("wo", [128, 8, 8, 128])
    d["lnw"] = b.dram_in("lnw", [128, 8])
    d["lnb"] = b.dram_in("lnb", [128, 8])
    d["tmk"] = b.dram_in("tmk", [128, 8])
    d["tmv"] = b.dram_in("tmv", [128, 8])
    d["tmr"] = b.dram_in("tmr", [128, 8])
    d["dw"] = b.dram_in("dw", [128, 8])
    d["u1"] = b.dram_in("u1", [128, 8, 2, 128])
    d["v1"] = b.dram_in("v1", [128, 2, 32, 128])
    d["u2"] = b.dram_in("u2", [128, 32, 2, 128])
    d["v2"] = b.dram_in("v2", [128, 2, 8, 128])
    d["fb1"] = b.dram_in("fb1", [128, 32])
    d["fb2"] = b.dram_in("fb2", [128, 8])
    d["out"] = b.dram_out("out", [128, NCHUNK, TW - TT])
    d["x1"] = b.dram_tmp("x1_t", [128, NCHUNK, TW])
    d["s1"] = b.dram_tmp("s1_t", [128, NCHUNK, TW])
    d["hq"] = b.dram_tmp("hq_t", [128, 24, TW], BF16)
    d["s2"] = b.dram_tmp("s2_t", [128, NCHUNK, TW])
    d["s3"] = b.dram_tmp("s3_t", [128, NCHUNK, TW])
    b.begin()
    cst = emit_consts(b)
    emit_l1a(b, cst, d)
    emit_l1b(b, cst, d)
    emit_l2(b, cst, d)
    emit_l3(b, cst, d)
    emit_l4(b, cst, d)
    return b.end()



_NCS = {}


def _get_nc(name, fn):
    if name not in _NCS:
        _NCS[name] = fn()
    return _NCS[name]


def kernel(**ins):
    x = np.asarray(ins["L0_attn_x"])
    B_, T_, D_ = x.shape
    H = T_ // 2

    xs_cores = []
    flags = []
    for b_i in range(B_):
        for h in range(2):
            main = x[b_i, h * H:(h + 1) * H, :]
            if h == 1:
                wu = x[b_i, H - TT:H, :]
                fl = np.ones((128, 1), np.float32)
            else:
                wu = np.zeros((TT, D_), np.float32)
                fl = np.zeros((128, 1), np.float32)
            slab = np.concatenate([wu, main], axis=0)
            xs_cores.append(stage_stream(slab))
            flags.append(fl)

    stat = dict(
        w_in_r=stage_w64_out(np.asarray(ins["L0_attn_w_in"])[0:64, :]),
        w_in_i=stage_w64_out(np.asarray(ins["L0_attn_w_in"])[64:128, :]),
        w_gate=stage_w(ins["L0_attn_w_gate"]),
        w_out_r=stage_w64_in(np.asarray(ins["L0_attn_w_out"])[:, 0:64]),
        w_out_i=stage_w64_in(np.asarray(ins["L0_attn_w_out"])[:, 64:128]),
        coupT=np.ascontiguousarray(np.asarray(ins["L0_attn_coupling"]).transpose(0, 2, 1)),
        b_gate=stage_cols(ins["L0_attn_b_gate"]),
        ln1g=stage_cols(ins["L0_attn_ln1_g"]),
        ln1b=stage_cols(ins["L0_attn_ln1_b"]),
        ldecay=np.ascontiguousarray(np.asarray(ins["L0_attn_log_decay"]).reshape(64, 1)),
        freq=np.ascontiguousarray(np.asarray(ins["L0_attn_frequency"]).reshape(64, 1)),
        w_mlp1=stage_w(ins["L0_attn_w_mlp1"]),
        w_mlp2=stage_w(ins["L0_attn_w_mlp2"]),
        b_mlp1=stage_cols(ins["L0_attn_b_mlp1"]),
        b_mlp2=stage_cols(ins["L0_attn_b_mlp2"]),
        ln2g=stage_cols(ins["L0_attn_ln2_g"]),
        ln2b=stage_cols(ins["L0_attn_ln2_b"]),
        wg=stage_w(ins["L0_ffn_w_gate"]),
        wu=stage_w(ins["L0_ffn_w_up"]),
        wd=stage_w(ins["L0_ffn_w_down"]),
        wk=stage_w(ins["L1_attn_Wk"]),
        wv=stage_w(ins["L1_attn_Wv"]),
        wr=stage_w(ins["L1_attn_Wr"]),
        wo=stage_w(ins["L1_attn_Wo"]),
        lnw=stage_cols(ins["L1_attn_ln_w"]),
        lnb=stage_cols(ins["L1_attn_ln_b"]),
        tmk=stage_cols(ins["L1_attn_tmk"]),
        tmv=stage_cols(ins["L1_attn_tmv"]),
        tmr=stage_cols(ins["L1_attn_tmr"]),
        dw=stage_cols(ins["L1_attn_decay_w"]),
        u1=stage_w(np.asarray(ins["L1_ffn_U1"]).T),
        v1=stage_w(np.asarray(ins["L1_ffn_V1"]).T),
        u2=stage_w(np.asarray(ins["L1_ffn_U2"]).T),
        v2=stage_w(np.asarray(ins["L1_ffn_V2"]).T),
        fb1=stage_cols(ins["L1_ffn_b1"]),
        fb2=stage_cols(ins["L1_ffn_b2"]),
    )

    nc = _get_nc("fused", build_fused)
    maps = [dict(stat, xs=xs_cores[i], flag=flags[i]) for i in range(8)]
    r = run_bass_kernel_spmd(nc, maps, core_ids=list(range(8)))

    out = np.empty((B_, T_, D_), np.float32)
    for b_i in range(B_):
        for h in range(2):
            arr = np.asarray(r.results[b_i * 2 + h]["out"])
            out[b_i, h * H:(h + 1) * H, :] = unstage_stream(arr)
    return out
